# version 03
# baseline (speedup 1.0000x reference)
import numpy as np
import ml_dtypes
import concourse.bass as bass
import concourse.mybir as mybir
from concourse.bass_utils import run_bass_kernel_spmd

F32 = mybir.dt.float32
BF16 = mybir.dt.bfloat16
AF = mybir.ActivationFunctionType
ALU = mybir.AluOpType
AX = mybir.AxisListType

NCORES = 8
BPC = 2
SEQ = 2048
T = BPC * SEQ
D = 1024
DT = D // 128
NEG = -1.0e30

ENGINES = ("tensor", "vector", "scalar", "gpsimd", "sync")
DMA_QUEUES = ("sync", "scalar", "gpsimd")


class Sched:

    def __init__(self, nc):
        self.nc = nc
        self.plan = {e: [] for e in ENGINES}
        self.esem = {e: nc.alloc_semaphore(f"es_{e}") for e in ENGINES}
        self.dsem = {q: nc.alloc_semaphore(f"ds_{q}") for q in DMA_QUEUES}
        self.ecnt = {e: 0 for e in ENGINES}
        self.dcnt = {q: 0 for q in DMA_QUEUES}

    SELF_SERIAL = ("vector", "scalar", "gpsimd")

    def op(self, eng, fn, waits=(), mark=False):
        waits = [w for w in waits if w is not None]
        if eng in self.SELF_SERIAL:
            if self.ecnt[eng]:
                waits.append(("e", eng, self.ecnt[eng]))
            mark = True
        tok = None
        if mark:
            self.ecnt[eng] += 1
            tok = ("e", eng, self.ecnt[eng])
        self.plan[eng].append((fn, tuple(waits),
                               ("e", eng) if mark else None))
        return tok

    def dma(self, q, out, in_, waits=(), **kw):
        self.dcnt[q] += 16
        tok = ("d", q, self.dcnt[q])
        def fn(e, out=out, in_=in_, kw=kw):
            return e.dma_start(out=out, in_=in_, **kw)
        self.plan[q].append((fn, tuple(w for w in waits if w is not None),
                             ("d", q)))
        return tok

    def barrier_toks(self):
        toks = []
        for e in ENGINES:
            if self.ecnt[e]:
                toks.append(("e", e, self.ecnt[e]))
        for q in DMA_QUEUES:
            if self.dcnt[q]:
                toks.append(("d", q, self.dcnt[q]))
        return toks

    def full_barrier(self):
        toks = self.barrier_toks()
        for e in ENGINES:
            self.plan[e].append((None, tuple(toks), None))

    def finalize(self):
        nc = self.nc
        with nc.Block() as block:
            for eng_name in ENGINES:
                def run(eng, eng_name=eng_name):
                    hwm = {}
                    for fn, waits, markkind in self.plan[eng_name]:
                        for kind, name, val in waits:
                            if (kind == "e" and name == eng_name
                                    and eng_name not in self.SELF_SERIAL):
                                continue
                            if val <= 0:
                                continue
                            sem = self.esem[name] if kind == "e" else self.dsem[name]
                            key = (kind, name)
                            if hwm.get(key, 0) < val:
                                hwm[key] = val
                                eng.wait_ge(sem, val)
                        if fn is None:
                            continue
                        ins = fn(eng)
                        if markkind is not None:
                            kind, name = markkind
                            if kind == "e":
                                ins.then_inc(self.esem[name], 1)
                            else:
                                ins.then_inc(self.dsem[name], 16)
                    if eng_name == "sync":
                        for q in DMA_QUEUES:
                            if self.dcnt[q]:
                                eng.wait_ge(self.dsem[q], self.dcnt[q])
                        for e2 in ENGINES:
                            if e2 != "sync" and self.ecnt[e2]:
                                eng.wait_ge(self.esem[e2], self.ecnt[e2])
                getattr(block, eng_name)(run)


class Bufs:

    def __init__(self, nc):
        ap = lambda h: h.ap()
        self.ab = ap(nc.alloc_sbuf_tensor("ab", [128, 88000], BF16))
        self.af = ap(nc.alloc_sbuf_tensor("af", [128, 8600], F32))
        self.ident_bf = ap(nc.alloc_sbuf_tensor("ident_bf", [128, 128], BF16))
        self.ident_f = ap(nc.alloc_sbuf_tensor("ident_f", [128, 128], F32))
        self.triu64 = ap(nc.alloc_sbuf_tensor("triu64", [64, 64], F32))
        self.causal = ap(nc.alloc_sbuf_tensor("causal", [128, 128], F32))
        self.ones_row = ap(nc.alloc_sbuf_tensor("ones_row", [1, 128], F32))
        self.ones_col = ap(nc.alloc_sbuf_tensor("ones_col", [128, 1], F32))
        self.cc = ap(nc.alloc_sbuf_tensor("cc", [128, 8], F32))
        self.pb = [ap(nc.alloc_psum_tensor(f"pb{i}", [128, 512], F32)) for i in range(6)]
        self.tb = [ap(nc.alloc_psum_tensor(f"tb{i}", [128, 1024], BF16)) for i in range(2)]


def emit_const(nc, S, B):
    S.op("gpsimd", lambda e: e.memset(B.ident_bf, 0.0))
    S.op("gpsimd", lambda e: e.affine_select(
        out=B.ident_bf, in_=B.ident_bf, compare_op=ALU.not_equal, fill=1.0,
        base=0, pattern=[[-1, 128]], channel_multiplier=1))
    S.op("gpsimd", lambda e: e.memset(B.ident_f, 0.0))
    S.op("gpsimd", lambda e: e.affine_select(
        out=B.ident_f, in_=B.ident_f, compare_op=ALU.not_equal, fill=1.0,
        base=0, pattern=[[-1, 128]], channel_multiplier=1))
    S.op("gpsimd", lambda e: e.memset(B.triu64, 0.0))
    S.op("gpsimd", lambda e: e.affine_select(
        out=B.triu64, in_=B.triu64, compare_op=ALU.is_gt, fill=1.0,
        base=0, pattern=[[-1, 64]], channel_multiplier=1))
    S.op("gpsimd", lambda e: e.memset(B.causal, 0.0))
    S.op("gpsimd", lambda e: e.affine_select(
        out=B.causal, in_=B.causal, compare_op=ALU.is_ge, fill=NEG,
        base=0, pattern=[[-1, 128]], channel_multiplier=1))
    S.op("gpsimd", lambda e: e.memset(B.ones_row, 1.0))
    S.op("gpsimd", lambda e: e.memset(B.cc[:, 0:1], 1e-5))
    S.op("gpsimd", lambda e: e.memset(B.cc[:, 1:2], 1e-6))
    S.op("gpsimd", lambda e: e.memset(B.cc[:, 2:3], 1.0))
    S.op("gpsimd", lambda e: e.memset(B.cc[:, 3:4], 128 * 1e-5))
    S.op("gpsimd", lambda e: e.memset(B.ones_col, 1.0), mark=True)


def emit_prep(nc, S, B, aps):
    af, ab = B.af, B.ab

    def vec_to_cols(src_ap, n, dst_sb):
        stg = af[0:n, 0:128]
        ld = S.dma("sync", stg, src_ap.rearrange("(a b) -> a b", b=128))
        tt = S.op("tensor", lambda e: e.transpose(
            B.pb[0][:, 0:n], stg, B.ident_f[0:n, 0:n]), waits=[ld], mark=True)
        cv = S.op("vector", lambda e: e.tensor_copy(dst_sb, B.pb[0][:, 0:n]),
                  waits=[tt], mark=True)
        S.full_barrier()
        return cv

    nwc = af[0:128, 256:264]
    vec_to_cols(aps["L0_attn_norm_w"], 8, nwc)
    if "dbg_nwc" in aps:
        S.dma("sync", aps["dbg_nwc"], nwc)
        S.full_barrier()
    rwc = af[0:128, 264:272]
    vec_to_cols(aps["L0_ffn_rms_w"], 8, rwc)
    S.dma("sync", aps["rmsw_c"], rwc)
    for ei in range(8):
        dst = af[0:128, 280:312]
        vec_to_cols(aps["L0_ffn_b1"][ei], 32, dst)
        S.dma("sync", aps["b1c"][ei], dst)
        S.full_barrier()
        dst2 = af[0:128, 312:320]
        vec_to_cols(aps["L0_ffn_b2"][ei], 8, dst2)
        S.dma("sync", aps["b2c"][ei], dst2)
        S.full_barrier()
    dst = af[0:128, 280:312]
    vec_to_cols(aps["L1_ffn_b1"], 32, dst)
    S.dma("sync", aps["l1b1c"], dst)
    S.full_barrier()
    dst2 = af[0:128, 312:320]
    vec_to_cols(aps["L1_ffn_b2"], 8, dst2)
    S.dma("sync", aps["l1b2c"], dst2)
    S.full_barrier()

    gsb = af[0:8, 4096:5120]
    g0 = S.dma("sync", gsb, aps["L0_ffn_gate_w"])
    for kt in range(DT):
        tt = S.op("tensor", lambda e, kt=kt: e.transpose(
            B.pb[1][:, kt * 8:(kt + 1) * 8], gsb[:, kt * 128:(kt + 1) * 128],
            B.ident_f[0:8, 0:8]), waits=[g0], mark=(kt == DT - 1))
    gft = af[0:128, 5120:5184]
    for kt in range(DT):
        cg = S.op("vector", lambda e, kt=kt: e.tensor_scalar(
            gft[:, kt * 8:(kt + 1) * 8], B.pb[1][:, kt * 8:(kt + 1) * 8],
            rwc[:, kt:kt + 1], None, ALU.mult), waits=[tt], mark=True)
    for kt in range(DT):
        mg = S.op("tensor", lambda e, kt=kt: e.matmul(
            B.pb[2][0:1, 0:8], B.ones_col, gft[:, kt * 8:(kt + 1) * 8],
            start=(kt == 0), stop=(kt == DT - 1)), waits=[cg], mark=(kt == DT - 1))
    gsum_sb = af[0:1, 5184:5192]
    cgs = S.op("vector", lambda e: e.tensor_copy(gsum_sb, B.pb[2][0:1, 0:8]),
               waits=[mg], mark=True)
    S.dma("sync", aps["gate_wTf"].rearrange("(t p) n -> p t n", p=128),
          gft.rearrange("p (t n) -> p t n", t=8), waits=[cg])
    S.dma("sync", aps["gsum"], gsum_sb, waits=[cgs])
    S.full_barrier()

    for wname, oname in (("L1_attn_Wq", "wqT_bf"), ("L1_attn_Wk", "wkT_bf"),
                         ("L1_attn_Wv", "wvT_bf"), ("L1_attn_Wo", "woT_bf")):
        for rb in range(DT):
            wrow = af[0:128, 1024:2048]
            ld = S.dma("sync", wrow, aps[wname][rb * 128:(rb + 1) * 128, :])
            prev_tc = {0: None, 1: None}
            for ct in range(DT):
                bank = B.pb[ct % 2]
                tt = S.op("tensor", lambda e, ct=ct, bank=bank: e.transpose(
                    bank[:, 0:128], wrow[:, ct * 128:(ct + 1) * 128], B.ident_f),
                    waits=[ld, prev_tc[ct % 2]], mark=True)
                wtile = ab[0:128, ct * 128:(ct + 1) * 128]
                cv = S.op("vector", lambda e, ct=ct, bank=bank, wtile=wtile: e.tensor_copy(
                    wtile, bank[:, 0:128]), waits=[tt], mark=True)
                prev_tc[ct % 2] = cv
                S.dma("scalar", aps[oname][ct * 128:(ct + 1) * 128,
                                           rb * 128:(rb + 1) * 128], wtile,
                      waits=[cv])
            S.full_barrier()

    for kt in range(DT):
        wrow = af[0:128, 1024:2048]
        wbf = ab[0:128, 0:1024]
        ld = S.dma("sync", wrow, aps["L0_attn_Wo"][kt * 128:(kt + 1) * 128, :])
        sc = S.op("vector", lambda e, kt=kt: e.tensor_scalar(
            wbf, wrow, nwc[:, kt:kt + 1], None, ALU.mult), waits=[ld], mark=True)
        S.dma("sync", aps["wo_sc_bf"][kt * 128:(kt + 1) * 128, :], wbf, waits=[sc])
        S.full_barrier()


def emit_ingest(nc, S, B, aps):
    af, ab = B.af, B.ab
    x_rows = aps["x_in"].rearrange("b n d -> (b n) d")

    rwc = af[0:128, 0:8]
    S.dma("sync", rwc, aps["rmsw_c"])
    gwTf = af[0:128, 8192:8256]
    S.dma("sync", gwTf.rearrange("p (t n) -> p t n", t=8),
          aps["gate_wTf"].rearrange("(t p) n -> p t n", p=128))
    gsum_sb = af[0:1, 8256:8264]
    g1 = S.dma("sync", gsum_sb, aps["gsum"])
    mgb = S.op("tensor", lambda e: e.matmul(
        B.pb[4][0:128, 0:8], B.ones_row, gsum_sb, start=True, stop=True),
        waits=[g1], mark=True)
    gsum_bc = af[0:128, 8264:8272]
    S.op("vector", lambda e: e.tensor_copy(gsum_bc, B.pb[4][0:128, 0:8]),
         waits=[mgb], mark=True)
    S.full_barrier()

    xt = af[0:128, 16:1040]
    sq = af[0:128, 1040:2064]
    xnf = af[0:128, 7168:8192]
    st = af[0:128, 2064:2080]
    st16 = af[0:128, 2080:2096]
    lgf = af[0:128, 2096:2112]
    xnt = ab[0:128, 64:1088]
    x2t = ab[0:128, 1088:2112]
    xoT = af[0:128, 2560:6656]
    xnoT = ab[0:128, 2112:6208]
    x2oT = ab[0:128, 6208:10304]
    selT = af[0:16, 6656:7168]

    for tt in range(32):
        gi = tt % 4
        tok_off = gi * 128
        ld = S.dma("sync", xt, x_rows[tt * 128:(tt + 1) * 128, :])
        s1 = S.op("vector", lambda e: e.reduce_sum(st[:, 0:1], xt, axis=AX.X),
                  waits=[ld], mark=True)
        qq = S.op("scalar", lambda e: e.activation(sq, xt, AF.Square),
                  waits=[ld], mark=True)
        s2 = S.op("vector", lambda e: e.reduce_sum(st[:, 1:2], sq, axis=AX.X),
                  waits=[qq], mark=True)
        S.op("vector", lambda e: e.tensor_scalar(st[:, 2:3], st[:, 0:1], 1.0 / D, None, ALU.mult))
        S.op("vector", lambda e: e.tensor_mul(st[:, 7:8], st[:, 2:3], st[:, 2:3]))
        S.op("vector", lambda e: e.tensor_scalar(st[:, 6:7], st[:, 1:2], 1.0 / D, None, ALU.mult))
        vv = S.op("vector", lambda e: e.tensor_sub(st[:, 3:4], st[:, 6:7], st[:, 7:8]),
                  mark=True)
        sd = S.op("scalar", lambda e: e.activation(st[:, 4:5], st[:, 3:4], AF.Ln,
                                                   bias=B.cc[:, 0:1]), waits=[vv], mark=True)
        vi = S.op("scalar", lambda e: e.activation(st[:, 10:11], st[:, 4:5], AF.Exp,
                                                   scale=-0.5), mark=True)
        xn = S.op("vector", lambda e: e.tensor_scalar(
            xnf, xt, st[:, 2:3], st[:, 10:11], ALU.subtract, ALU.mult),
            waits=[vi], mark=True)
        xnc = S.op("vector", lambda e: e.tensor_copy(xnt, xnf), mark=True)
        q2 = S.op("scalar", lambda e: e.activation(sq, xnf, AF.Square),
                  waits=[xn], mark=True)
        ms = S.op("vector", lambda e: e.reduce_sum(st[:, 5:6], sq, axis=AX.X),
                  waits=[q2], mark=True)
        s2d = S.op("scalar", lambda e: e.activation(st[:, 6:7], st[:, 5:6], AF.Ln,
                                                    bias=B.cc[:, 1:2], scale=1.0 / D),
                   waits=[ms], mark=True)
        r2 = S.op("scalar", lambda e: e.activation(st[:, 11:12], st[:, 6:7], AF.Exp,
                                                   scale=-0.5), mark=True)
        x2 = S.op("vector", lambda e: e.tensor_scalar(
            x2t, xnf, st[:, 11:12], None, ALU.mult), waits=[r2], mark=True)

        prev_c1 = {0: None, 1: None}
        for kt in range(DT):
            pslot = B.pb[kt % 2][:, 0:128]
            t1 = S.op("tensor", lambda e, kt=kt, pslot=pslot: e.transpose(
                pslot, xt[:, kt * 128:(kt + 1) * 128], B.ident_f),
                waits=[ld, xn, prev_c1[kt % 2]], mark=True)
            c1 = S.op("vector", lambda e, kt=kt, pslot=pslot, tok_off=tok_off: e.tensor_copy(
                xoT[:, kt * 512 + tok_off: kt * 512 + tok_off + 128], pslot),
                waits=[t1], mark=True)
            prev_c1[kt % 2] = c1
            tslot = B.tb[kt % 2][:, 0:128]
            t2 = S.op("tensor", lambda e, kt=kt, tslot=tslot: e.transpose(
                tslot, xnt[:, kt * 128:(kt + 1) * 128], B.ident_bf),
                waits=[c1, xnc], mark=True)
            c2 = S.op("vector", lambda e, kt=kt, tslot=tslot, tok_off=tok_off: e.tensor_copy(
                xnoT[:, kt * 512 + tok_off: kt * 512 + tok_off + 128], tslot),
                waits=[t2], mark=True)
            t3 = S.op("tensor", lambda e, kt=kt, tslot=tslot: e.transpose(
                tslot, x2t[:, kt * 128:(kt + 1) * 128], B.ident_bf),
                waits=[c2, x2], mark=True)
            c3 = S.op("vector", lambda e, kt=kt, tslot=tslot, tok_off=tok_off: e.tensor_scalar(
                x2oT[:, kt * 512 + tok_off: kt * 512 + tok_off + 128], tslot,
                rwc[:, kt:kt + 1], None, ALU.mult), waits=[t3], mark=True)

        ps_lg = B.pb[2]
        for kt in range(DT):
            mm = S.op("tensor", lambda e, kt=kt, tok_off=tok_off: e.matmul(
                ps_lg[0:128, 0:8],
                xoT[:, kt * 512 + tok_off: kt * 512 + tok_off + 128],
                gwTf[:, kt * 8:(kt + 1) * 8],
                start=(kt == 0), stop=(kt == DT - 1)),
                waits=[c1, c3], mark=(kt == DT - 1))
        v1 = S.op("vector", lambda e: e.tensor_scalar(
            lgf[:, 8:16], gsum_bc, st[:, 2:3], None, ALU.mult), waits=[xn], mark=True)
        v2 = S.op("vector", lambda e: e.tensor_sub(
            lgf[:, 0:8], ps_lg[0:128, 0:8], lgf[:, 8:16]), waits=[mm], mark=True)
        v3 = S.op("vector", lambda e: e.tensor_mul(
            st[:, 12:13], st[:, 10:11], st[:, 11:12]), waits=[r2], mark=True)
        v4 = S.op("vector", lambda e: e.tensor_scalar(
            lgf[:, 0:8], lgf[:, 0:8], st[:, 12:13], None, ALU.mult),
            waits=[v3], mark=True)
        mx = S.op("vector", lambda e: e.reduce_max(
            st[:, 8:9], lgf[:, 0:8], axis=AX.X, negate=True),
            waits=[v4], mark=True)
        ex = S.op("scalar", lambda e: e.activation(
            st16[:, 8:16], lgf[:, 0:8], AF.Exp, bias=st[:, 8:9],
            accum_out=st[:, 9:10]), waits=[mx], mark=True)
        sc = S.op("vector", lambda e: e.reciprocal(st16[:, 8:9], st[:, 9:10]),
                  waits=[ex], mark=True)
        sel = S.op("vector", lambda e: e.tensor_scalar(
            lgf[:, 8:16], lgf[:, 0:8], st[:, 8:9], 0.0, ALU.add, ALU.is_ge),
            waits=[mx, ex], mark=True)
        ssc = S.op("vector", lambda e: e.tensor_scalar(
            st16[:, 0:8], lgf[:, 8:16], st16[:, 8:9], None, ALU.mult),
            waits=[sel, sc], mark=True)
        ts = S.op("tensor", lambda e: e.transpose(
            B.pb[3][0:16, 0:128], st16[:, 0:16], B.ident_f),
            waits=[ssc], mark=True)
        cs = S.op("vector", lambda e, tok_off=tok_off: e.tensor_copy(
            selT[:, tok_off:tok_off + 128], B.pb[3][0:16, 0:128]),
            waits=[ts], mark=True)

        if gi == 3:
            g = tt // 4
            col0 = g * 512
            w = S.barrier_toks()
            S.dma("sync", aps["x0T"].rearrange("(t p) c -> p t c", p=128)[:, :, col0:col0 + 512],
                  xoT.rearrange("p (t c) -> p t c", t=8), waits=w)
            S.dma("gpsimd", aps["xn0T"].rearrange("(t p) c -> p t c", p=128)[:, :, col0:col0 + 512],
                  xnoT.rearrange("p (t c) -> p t c", t=8), waits=w)
            S.dma("gpsimd", aps["xn2T"].rearrange("(t p) c -> p t c", p=128)[:, :, col0:col0 + 512],
                  x2oT.rearrange("p (t c) -> p t c", t=8), waits=w)
            S.dma("scalar", aps["selsc"][:, col0:col0 + 512], selT, waits=w)
        S.full_barrier()


def fm_view(ap_):
    return ap_.rearrange("(t p) c -> p t c", p=128)


DBG_GSA = {"nch": None, "heads": None, "skip_post": False, "skip_wo": False, "skip_fg": False, "skip_qkv": False, "plain_evac": False, "skip_memset": False, "skip_ft": False}


def emit_l0attn(nc, S, B, aps):
    af, ab = B.af, B.ab
    C = 64
    NCH = SEQ // C
    H = 8
    xn_b = ab[0:128, 0:16384]
    og = ab[0:128, 0:16384]
    qb = ab[0:128, 16384:32768]
    kb = ab[0:128, 32768:49152]
    vb = ab[0:128, 49152:65536]
    fT = ab[0:128, 65536:73728]
    t_f = ab[0:128, 73728:75776]
    t_g = ab[0:128, 75776:77824]
    wg2 = ab[0:128, 77824:78848]
    wslot = ab[0:128, 78848:82944]
    stil_fm = ab[0:128, 82944:83200]
    stil_tm = ab[0:64, 83200:83712]
    pm_sb = ab[0:64, 83712:83776]
    am_sb = ab[0:64, 83776:83840]
    qt_sb = ab[0:64, 83840:83904]
    qtT_sb = ab[0:128, 83904:83968]
    k_tm = ab[0:64, 83968:84096]
    v_tm = ab[0:64, 84096:84224]
    ogf_sb = ab[0:64, 84224:84352]
    ogn = ab[0:128, 32768:49152]
    th = af[0:128, 0:512]
    sp = af[0:128, 512:1024]
    s_fm = af[0:128, 1024:1280]
    sig = af[0:128, 1280:1536]
    ep_fm = af[0:128, 1536:1792]
    en_fm = af[0:128, 1792:2048]
    ep_tm = af[0:64, 2048:2560]
    eend = af[0:1, 2560:3072]
    hk = ab[0:128, 84352:84864]
    hv = ab[0:64, 84864:85888]
    stilT_sb = ab[0:64, 85888:85952]
    tmp1 = af[0:128, 4096:4160]
    tmp2 = af[0:128, 4160:4288]
    ok_sb = af[0:64, 4288:4352]
    qv_sb = af[0:64, 4352:4416]
    cols = af[0:64, 4416:4432]
    thg = af[0:64, 4432:4560]
    sgb = af[0:64, 4560:4688]
    sqb = af[0:128, 4688:5200]
    rrow = af[0:1, 5200:5712]
    rrow2 = af[0:1, 5712:6224]

    lwg2 = S.dma("gpsimd", wg2.rearrange("p (t n) -> p t n", t=1),
                 aps["L0_attn_Wg2"].rearrange("(t p) n -> p t n", p=128))
    last_wo_mm = None
    last_proj_mm = None
    for b in range(DBG_GSA.get("bpc") or BPC):
        lxn = S.dma("sync", xn_b.rearrange("p (t c) -> p t c", t=8),
                    fm_view(aps["xn0T"])[:, :, b * SEQ:(b + 1) * SEQ],
                    waits=[last_wo_mm])
        for wname, dst, silu in (() if DBG_GSA["skip_qkv"] else (("L0_attn_Wq", qb, True),
                                 ("L0_attn_Wk", kb, True),
                                 ("L0_attn_Wv", vb, False))):
            for halfW in range(2):
                lw = S.dma("gpsimd", wslot.rearrange("p (t n) -> p t n", t=8),
                           aps[wname][:, halfW * 512:(halfW + 1) * 512]
                           .rearrange("(t p) n -> p t n", p=128),
                           waits=[last_proj_mm, last_wo_mm])
                prev_ev = {0: None, 1: None}
                for mt4 in range(4):
                    for grp in range(4):
                        bank = B.pb[grp % 2]
                        for kt in range(DT):
                            mm = S.op("tensor", lambda e_, mt4=mt4, grp=grp, kt=kt, bank=bank: e_.matmul(
                                bank[:, 0:512],
                                wslot[:, kt * 512 + mt4 * 128: kt * 512 + (mt4 + 1) * 128],
                                xn_b[:, kt * 2048 + grp * 512: kt * 2048 + (grp + 1) * 512],
                                start=(kt == 0), stop=(kt == DT - 1)),
                                waits=[lw, lxn, prev_ev[grp % 2]],
                                mark=(kt == DT - 1))
                        last_proj_mm = mm
                        o = (halfW * 4 + mt4) * 2048 + grp * 512
                        if silu and not DBG_GSA["plain_evac"]:
                            t1 = S.op("scalar", lambda e_, bank=bank: e_.activation(
                                th, bank[:, 0:512], AF.Tanh, scale=0.5),
                                waits=[mm], mark=True)
                            t2 = S.op("vector", lambda e_: e_.tensor_scalar(
                                sp, th, 0.5, 0.5, ALU.mult, ALU.add),
                                waits=[t1], mark=True)
                            ev = S.op("vector", lambda e_, o=o, bank=bank, dst=dst: e_.tensor_mul(
                                dst[:, o:o + 512], bank[:, 0:512], sp),
                                waits=[t2], mark=True)
                        else:
                            ev = S.op("vector", lambda e_, o=o, bank=bank, dst=dst: e_.tensor_copy(
                                dst[:, o:o + 512], bank[:, 0:512]), waits=[mm], mark=True)
                        prev_ev[grp % 2] = ev
        fg_list = (("L0_attn_Wf1", 128, t_f), ("L0_attn_Wg1", 128, t_g))
        if DBG_GSA.get("fg_ngrp") is not None:
            pass
        if DBG_GSA.get("fg_first_only"):
            fg_list = fg_list[:1]
        for wname, wcols, dst in (() if DBG_GSA["skip_fg"] else fg_list):
            lw = S.dma("gpsimd", wslot[:, 0:1024].rearrange("p (t n) -> p t n", t=8),
                       aps[wname].rearrange("(t p) n -> p t n", p=128),
                       waits=[last_proj_mm])
            prev_ev = {0: None, 1: None}
            for grp in range(DBG_GSA.get("fg_ngrp", 4)):
                bank = B.pb[grp % 2]
                for kt in range(DT):
                    mm = S.op("tensor", lambda e_, grp=grp, kt=kt, bank=bank: e_.matmul(
                        bank[:, 0:512],
                        wslot[:, kt * 128:(kt + 1) * 128],
                        xn_b[:, kt * 2048 + grp * 512: kt * 2048 + (grp + 1) * 512],
                        start=(kt == 0), stop=(kt == DT - 1)),
                        waits=[lw, prev_ev[grp % 2]], mark=(kt == DT - 1))
                last_proj_mm = mm
                ev = S.op("vector", lambda e_, grp=grp, bank=bank, dst=dst: e_.tensor_copy(
                    dst[:, grp * 512:(grp + 1) * 512], bank[:, 0:512]),
                    waits=[mm], mark=True)
                prev_ev[grp % 2] = ev
        if DBG_GSA["skip_fg"]:
            continue
        lw2 = S.dma("gpsimd", wslot[:, 0:512].rearrange("p (t n) -> p t n", t=1),
                    aps["L0_attn_Wf2"].rearrange("(t p) n -> p t n", p=128),
                    waits=[last_proj_mm])
        prev_ev = {0: None, 1: None}
        for mt in range(4 if not DBG_GSA["skip_ft"] else 0):
            for grp in range(4):
                bank = B.pb[grp % 2]
                mm = S.op("tensor", lambda e_, mt=mt, grp=grp, bank=bank: e_.matmul(
                    bank[:, 0:512], wslot[:, mt * 128:(mt + 1) * 128],
                    t_f[:, grp * 512:(grp + 1) * 512], start=True, stop=True),
                    waits=[lw2, prev_ev[grp % 2]], mark=True)
                last_proj_mm = mm
                ev = S.op("vector", lambda e_, mt=mt, grp=grp, bank=bank: e_.tensor_copy(
                    fT[:, mt * 2048 + grp * 512: mt * 2048 + (grp + 1) * 512],
                    bank[:, 0:512]), waits=[mm], mark=True)
                prev_ev[grp % 2] = ev
        if DBG_GSA["skip_memset"]:
            z1 = z2 = None
        else:
            z1 = S.op("gpsimd", lambda e_: e_.memset(hk, 0.0), waits=[("e", "vector", S.ecnt["vector"])])
            z2 = S.op("gpsimd", lambda e_: e_.memset(hv, 0.0))
        bank_free = {}
        vhk_prev = None
        for h in range(H):
            bank_free[("hk", h)] = z2
            bank_free[("hv", h)] = z2
        for ct in range(NCH if DBG_GSA["nch"] is None else DBG_GSA["nch"]):
            tnh = S.op("scalar", lambda e_, ct=ct: e_.activation(
                th[:, 0:256].rearrange("p (a c) -> p a c", a=4),
                fT.rearrange("p (a c) -> p a c", a=4)[:, :, ct * C:(ct + 1) * C],
                AF.Tanh, scale=0.5), waits=[("e", "vector", S.ecnt["vector"]),
                                            z1, z2], mark=True)
            vs = S.op("vector", lambda e_: e_.tensor_scalar(
                sig, th[:, 0:256], 0.5, 0.5, ALU.mult, ALU.add), waits=[tnh], mark=True)
            vsn = S.op("vector", lambda e_: e_.tensor_scalar(
                s_fm, th[:, 0:256], -0.5, 0.5, ALU.mult, ALU.add), mark=True)
            scans = []
            for mt in range(4):
                sc = S.op("vector", lambda e_, mt=mt: e_.tensor_tensor_scan(
                    ep_fm[:, mt * C:(mt + 1) * C], sig[:, mt * C:(mt + 1) * C],
                    sig[:, mt * C:(mt + 1) * C], 1.0, ALU.mult, ALU.bypass),
                    waits=[vs], mark=True)
                scans.append(sc)
            rec = S.op("vector", lambda e_: e_.reciprocal(en_fm, ep_fm),
                       waits=scans, mark=True)
            stf = S.op("vector", lambda e_: e_.tensor_mul(stil_fm, s_fm, en_fm),
                       waits=[rec, vsn], mark=True)
            for mt in range(4):
                tsl = B.tb[mt % 2][0:64, 0:128]
                tt = S.op("tensor", lambda e_, mt=mt, tsl=tsl: e_.transpose(
                    tsl, stil_fm[:, mt * C:(mt + 1) * C], B.ident_bf),
                    waits=[stf, bank_free.get(("tb", mt % 2))], mark=True)
                cv = S.op("vector", lambda e_, mt=mt, tsl=tsl: e_.tensor_copy(
                    stil_tm[:, mt * 128:(mt + 1) * 128], tsl),
                    waits=[tt], mark=True)
                bank_free[("tb", mt % 2)] = cv
                pbk = B.pb[4 + mt % 2]
                te = S.op("tensor", lambda e_, mt=mt, pbk=pbk: e_.transpose(
                    pbk[0:64, 0:128], ep_fm[:, mt * C:(mt + 1) * C],
                    B.ident_f), waits=scans + [bank_free.get(("pb", 4 + mt % 2))],
                    mark=True)
                ce = S.op("vector", lambda e_, mt=mt, pbk=pbk: e_.tensor_copy(
                    ep_tm[:, mt * 128:(mt + 1) * 128], pbk[0:64, 0:128]),
                    waits=[te], mark=True)
                bank_free[("pb", 4 + mt % 2)] = ce
            for mt in range(4):
                pbk = B.pb[4 + mt % 2]
                tz = S.op("tensor", lambda e_, mt=mt, pbk=pbk: e_.transpose(
                    pbk[0:1, 0:128], ep_fm[:, mt * C + C - 1: mt * C + C],
                    B.ident_f), waits=scans + [bank_free.get(("pb", 4 + mt % 2))],
                    mark=True)
                cz = S.op("vector", lambda e_, mt=mt, pbk=pbk: e_.tensor_copy(
                    eend[0:1, mt * 128:(mt + 1) * 128], pbk[0:1, 0:128]),
                    waits=[tz], mark=True)
                bank_free[("pb", 4 + mt % 2)] = cz
            dec_done = S.barrier_toks()
            for h in range(H if DBG_GSA["heads"] is None else DBG_GSA["heads"]):
                mt, r0 = h // 2, (h % 2) * 64
                col = mt * 128 + r0
                qc = qb[:, h * 2048 + ct * C: h * 2048 + (ct + 1) * C]
                kc = kb[:, h * 2048 + ct * C: h * 2048 + (ct + 1) * C]
                vc = vb[:, h * 2048 + ct * C: h * 2048 + (ct + 1) * C]
                st_tm = stil_tm[0:64, col:col + 64]
                ep_h = ep_tm[0:64, col:col + 64]
                eend_h = eend[0:1, col:col + 64]
                hk_h = hk[:, h * 64:(h + 1) * 64]
                hv_h = hv[0:64, h * 128:(h + 1) * 128]
                mmP = S.op("tensor", lambda e_, kc=kc, qc=qc: e_.matmul(
                    B.pb[0][0:64, 0:64], kc, qc, start=True, stop=True),
                    waits=[bank_free.get(("pb", 0))] + dec_done, mark=True)
                vP = S.op("vector", lambda e_: e_.tensor_mul(
                    pm_sb, B.pb[0][0:64, 0:64], B.triu64), waits=[mmP], mark=True)
                mmOa = S.op("tensor", lambda e_, st_tm=st_tm: e_.matmul(
                    B.pb[1][0:64, 0:64], pm_sb, st_tm, start=True, stop=False),
                    waits=[vP, bank_free.get(("pb", 1))], mark=True)
                mmOb = S.op("tensor", lambda e_, qc=qc, hk_h=hk_h: e_.matmul(
                    B.pb[1][0:64, 0:64], qc, hk_h, start=False, stop=True),
                    waits=[bank_free.get(("hk", h))], mark=True)
                vOk = S.op("vector", lambda e_, ep_h=ep_h: e_.tensor_mul(
                    ok_sb, B.pb[1][0:64, 0:64], ep_h), waits=[mmOb], mark=True)
                bank_free[("pb", 1)] = vOk
                vmx = S.op("vector", lambda e_: e_.reduce_max(
                    cols[:, 0:1], ok_sb, axis=AX.X, negate=True), mark=True)
                sex = S.op("scalar", lambda e_: e_.activation(
                    qv_sb, ok_sb, AF.Exp, bias=cols[:, 0:1],
                    accum_out=cols[:, 1:2]), waits=[vmx], mark=True)
                vrd = S.op("vector", lambda e_: e_.reciprocal(
                    cols[:, 2:3], cols[:, 1:2]), waits=[sex], mark=True)
                vqt = S.op("vector", lambda e_, ep_h=ep_h: e_.scalar_tensor_tensor(
                    qt_sb, qv_sb, cols[:, 2:3], ep_h, ALU.mult, ALU.mult),
                    waits=[vrd], mark=True)
                ttq = S.op("tensor", lambda e_: e_.transpose(
                    B.tb[0][0:64, 0:64], qt_sb, B.ident_bf[0:64, 0:64]),
                    waits=[vqt, bank_free.get(("tb", 0))], mark=True)
                cq = S.op("vector", lambda e_: e_.tensor_copy(
                    qtT_sb[0:64, 0:64], B.tb[0][0:64, 0:64]),
                    waits=[ttq], mark=True)
                bank_free[("tb", 0)] = cq
                ttk = S.op("tensor", lambda e_, kc=kc: e_.transpose(
                    B.tb[1][0:64, 0:128], kc, B.ident_bf),
                    waits=[bank_free.get(("tb", 1))], mark=True)
                ck = S.op("vector", lambda e_: e_.tensor_copy(
                    k_tm, B.tb[1][0:64, 0:128]), waits=[ttk], mark=True)
                ttv = S.op("tensor", lambda e_, vc=vc: e_.transpose(
                    B.tb[0][0:64, 128:256], vc, B.ident_bf),
                    waits=[cq], mark=True)
                cvv = S.op("vector", lambda e_: e_.tensor_copy(
                    v_tm, B.tb[0][0:64, 128:256]), waits=[ttv], mark=True)
                bank_free[("tb", 0)] = cvv
                bank_free[("tb", 1)] = ck
                tts = S.op("tensor", lambda e_, st_tm=st_tm: e_.transpose(
                    B.tb[0][0:64, 320:384], st_tm, B.ident_bf[0:64, 0:64]),
                    waits=[cq, bank_free.get(("tbs", 0))] + dec_done, mark=True)
                cts = S.op("vector", lambda e_: e_.tensor_copy(
                    stilT_sb, B.tb[0][0:64, 320:384]), waits=[tts], mark=True)
                bank_free[("tbs", 0)] = cts
                mmA = S.op("tensor", lambda e_: e_.matmul(
                    B.pb[0][0:64, 0:64], stilT_sb, qtT_sb[0:64, 0:64],
                    start=True, stop=True), waits=[vP, cq, cts], mark=True)
                bank_free[("pb", 0)] = mmA
                vA = S.op("vector", lambda e_: e_.tensor_mul(
                    am_sb, B.pb[0][0:64, 0:64], B.triu64), waits=[mmA], mark=True)
                bank_free[("pb", 0)] = vA
                mmVa = S.op("tensor", lambda e_: e_.matmul(
                    B.pb[4][0:64, 0:128], am_sb, v_tm, start=True, stop=False),
                    waits=[vA, cvv, bank_free.get(("pb", 4))], mark=True)
                mmVb = S.op("tensor", lambda e_, hv_h=hv_h: e_.matmul(
                    B.pb[4][0:64, 0:128], qtT_sb[0:64, 0:64], hv_h,
                    start=False, stop=True),
                    waits=[bank_free.get(("hv", h))], mark=True)
                mmHk = S.op("tensor", lambda e_, st_tm=st_tm: e_.matmul(
                    B.pb[2][0:128, 0:64], k_tm, st_tm, start=True, stop=True),
                    waits=[ck, bank_free.get(("pb", 2)), mmOb], mark=True)
                mmBc = S.op("tensor", lambda e_, eend_h=eend_h: e_.matmul(
                    B.pb[3][0:128, 0:64], B.ones_row, eend_h, start=True, stop=True),
                    waits=[bank_free.get(("pb", 3)), bank_free.get(("pbec", 3))] + dec_done,
                    mark=True)
                vt1 = S.op("vector", lambda e_, hk_h=hk_h: e_.tensor_add(
                    tmp1, B.pb[2][0:128, 0:64], hk_h), waits=[mmHk], mark=True)
                bank_free[("pb", 2)] = vt1
                vhk = S.op("vector", lambda e_, hk_h=hk_h: e_.tensor_mul(
                    hk_h, tmp1, B.pb[3][0:128, 0:64]), waits=[vt1, mmBc], mark=True)
                bank_free[("pb", 3)] = vhk
                bank_free[("hk", h)] = vhk
                vhk_prev = vhk
                mmHv = S.op("tensor", lambda e_, st_tm=st_tm: e_.matmul(
                    B.pb[2][0:64, 64:192], st_tm, v_tm, start=True, stop=True),
                    waits=[cvv, vt1, mmVb], mark=True)
                mmEc = S.op("tensor", lambda e_, eend_h=eend_h: e_.matmul(
                    B.pb[3][0:64, 64:65], eend_h, B.ones_row[0:1, 0:1],
                    start=True, stop=True),
                    waits=[vhk, bank_free.get(("pbec", 3))] + dec_done, mark=True)
                cec = S.op("vector", lambda e_: e_.tensor_copy(
                    cols[:, 3:4], B.pb[3][0:64, 64:65]), waits=[mmEc], mark=True)
                bank_free[("pbec", 3)] = cec
                vt2 = S.op("vector", lambda e_, hv_h=hv_h: e_.tensor_add(
                    tmp2[0:64, 0:128], B.pb[2][0:64, 64:192], hv_h),
                    waits=[mmHv], mark=True)
                bank_free[("pb", 2)] = vt2
                vhv = S.op("vector", lambda e_, hv_h=hv_h: e_.tensor_scalar(
                    hv_h, tmp2[0:64, 0:128], cols[:, 3:4], None, ALU.mult),
                    waits=[vt2, cec], mark=True)
                bank_free[("hv", h)] = vhv
                mmG = S.op("tensor", lambda e_, ct=ct, h=h: e_.matmul(
                    B.pb[5][0:64, 0:128], t_g[:, ct * C:(ct + 1) * C],
                    wg2[:, h * 128:(h + 1) * 128], start=True, stop=True),
                    waits=[lwg2, bank_free.get(("pb", 5))], mark=True)
                sth = S.op("scalar", lambda e_: e_.activation(
                    thg, B.pb[5][0:64, 0:128], AF.Tanh, scale=0.5),
                    waits=[mmG], mark=True)
                bank_free[("pb", 5)] = sth
                vsg = S.op("vector", lambda e_: e_.tensor_scalar(
                    sgb, thg, 0.5, 0.5, ALU.mult, ALU.add), waits=[sth], mark=True)
                vog = S.op("vector", lambda e_: e_.tensor_mul(
                    ogf_sb, B.pb[4][0:64, 0:128], sgb), waits=[vsg, mmVb], mark=True)
                bank_free[("pb", 4)] = vog
                tto = S.op("tensor", lambda e_: e_.transpose(
                    B.tb[1][0:128, 256:320], ogf_sb, B.ident_bf[0:64, 0:64]),
                    waits=[vog, bank_free.get(("tb", 1))], mark=True)
                cog = S.op("vector", lambda e_, h=h, ct=ct: e_.tensor_copy(
                    og[:, h * 2048 + ct * C:(h * 2048 + (ct + 1) * C)],
                    B.tb[1][0:128, 256:320]), waits=[tto], mark=True)
                bank_free[("tb", 1)] = cog
        for h in range(H if not DBG_GSA["skip_post"] else 0):
            for grp in range(4):
                o = h * 2048 + grp * 512
                sq1 = S.op("scalar", lambda e_, o=o: e_.activation(
                    sqb, og[:, o:o + 512], AF.Square),
                    waits=[("e", "vector", S.ecnt["vector"])], mark=True)
                mmS = S.op("tensor", lambda e_: e_.matmul(
                    B.pb[0][0:1, 0:512], B.ones_col, sqb, start=True, stop=True),
                    waits=[sq1, bank_free.get(("pb", 0))], mark=True)
                ssd = S.op("scalar", lambda e_: e_.activation(
                    rrow, B.pb[0][0:1, 0:512], AF.Ln, bias=B.cc[0:1, 3:4]),
                    waits=[mmS], mark=True)
                bank_free[("pb", 0)] = ssd
                srr = S.op("scalar", lambda e_: e_.activation(
                    rrow2, rrow, AF.Exp, scale=-0.5), mark=True)
                mmB = S.op("tensor", lambda e_: e_.matmul(
                    B.pb[1][0:128, 0:512], B.ones_row, rrow2, start=True, stop=True),
                    waits=[srr, bank_free.get(("pb", 1))], mark=True)
                von = S.op("vector", lambda e_, o=o: e_.scalar_tensor_tensor(
                    ogn[:, o:o + 512], og[:, o:o + 512], float(np.sqrt(128.0)),
                    B.pb[1][0:128, 0:512], ALU.mult, ALU.mult),
                    waits=[mmB], mark=True)
                bank_free[("pb", 1)] = von
        for halfW in range(2 if not DBG_GSA["skip_wo"] else 0):
            lwo = S.dma("gpsimd", wslot.rearrange("p (t n) -> p t n", t=4),
                        aps["wo_sc_bf"][halfW * 512:(halfW + 1) * 512, :]
                        .rearrange("(t p) n -> p t n", p=128),
                        waits=[last_proj_mm, last_wo_mm])
            prev_ao = {0: None, 1: None}
            prev_aodma = {0: None, 1: None}
            for dout in range(DT):
                for grp in range(4):
                    par = (dout * 4 + grp) % 2
                    bank = B.pb[2 + par]
                    for kt in range(4):
                        mm = S.op("tensor", lambda e_, dout=dout, grp=grp, kt=kt, halfW=halfW, bank=bank: e_.matmul(
                            bank[:, 0:512],
                            wslot[:, kt * 1024 + dout * 128: kt * 1024 + (dout + 1) * 128],
                            ogn[:, (halfW * 4 + kt) * 2048 + grp * 512:
                                (halfW * 4 + kt) * 2048 + (grp + 1) * 512],
                            start=(kt == 0), stop=(kt == 3)),
                            waits=[lwo, prev_ao[par],
                                   ("e", "vector", S.ecnt["vector"])],
                            mark=(kt == 3))
                    last_wo_mm = mm
                    abuf = th if par == 0 else sp
                    aev = S.op("vector", lambda e_, par=par, bank=bank, abuf=abuf: e_.tensor_copy(
                        abuf, bank[:, 0:512]), waits=[mm, prev_aodma[par]], mark=True)
                    prev_ao[par] = aev
                    kw = {} if halfW == 0 else {"accum_op": ALU.add}
                    ad = S.dma("gpsimd", fm_view(aps["a0T"])[:, dout,
                               b * SEQ + grp * 512: b * SEQ + (grp + 1) * 512],
                               abuf, waits=[aev], **kw)
                    prev_aodma[par] = ad
    S.full_barrier()


def emit_l1attn(nc, S, B, aps):
    af, ab = B.af, B.ab
    xn_b = ab[0:128, 0:16384]
    qb = ab[0:128, 16384:24576]
    kb = ab[0:128, 24576:32768]
    vb = ab[0:128, 32768:40960]
    og = ab[0:128, 40960:49152]
    wslot = ab[0:128, 49152:53248]
    probs_e = ab[0:128, 53248:55296]
    probs_n = ab[0:128, 55296:57344]
    pT = [ab[0:128, 57344 + i * 128: 57344 + (i + 1) * 128] for i in range(2)]
    v_tm = ab[0:128, 57600:58624]
    scores = af[0:128, 0:2048]
    st = af[0:128, 2048:2064]
    ao = [af[0:128, 2064 + i * 512: 2064 + (i + 1) * 512] for i in range(2)]

    last_proj_mm = None
    last_wo_mm = None
    for b in range(BPC):
        lxn = S.dma("sync", xn_b.rearrange("p (t c) -> p t c", t=8),
                    fm_view(aps["xn1T"])[:, :, b * SEQ:(b + 1) * SEQ],
                    waits=[last_proj_mm])
        for half in range(2):
            for wname, dst, scale in (("wqT_bf", qb, 0.125),
                                      ("wkT_bf", kb, None),
                                      ("wvT_bf", vb, None)):
                lw = S.dma("gpsimd", wslot[:, 0:4096].rearrange("p (t n) -> p t n", t=8),
                           aps[wname][:, half * 512:(half + 1) * 512]
                           .rearrange("(t p) n -> p t n", p=128),
                           waits=[last_proj_mm, last_wo_mm])
                prev_ev = {0: None, 1: None}
                for mt4 in range(4):
                    for grp in range(4):
                        bank = B.pb[grp % 2]
                        for kt in range(DT):
                            mm = S.op("tensor", lambda e_, mt4=mt4, grp=grp, kt=kt, bank=bank: e_.matmul(
                                bank[:, 0:512],
                                wslot[:, kt * 512 + mt4 * 128: kt * 512 + (mt4 + 1) * 128],
                                xn_b[:, kt * 2048 + grp * 512: kt * 2048 + (grp + 1) * 512],
                                start=(kt == 0), stop=(kt == DT - 1)),
                                waits=[lw, lxn, prev_ev[grp % 2]],
                                mark=(kt == DT - 1))
                        last_proj_mm = mm
                        o = mt4 * 2048 + grp * 512
                        if scale is None:
                            ev = S.op("vector", lambda e_, o=o, bank=bank, dst=dst: e_.tensor_copy(
                                dst[:, o:o + 512], bank[:, 0:512]), waits=[mm], mark=True)
                        else:
                            ev = S.op("vector", lambda e_, o=o, bank=bank, dst=dst, scale=scale: e_.tensor_scalar(
                                dst[:, o:o + 512], bank[:, 0:512], scale, None, ALU.mult),
                                waits=[mm], mark=True)
                        prev_ev[grp % 2] = ev
            if b == 0 and half == 0 and "dbgq" in aps:
                S.full_barrier()
                S.dma("sync", aps["dbgq"], qb)
                S.dma("sync", aps["dbgk"], kb)
                S.dma("sync", aps["dbgv"], vb)
            for h8 in range(8):
                hp, row0 = h8 // 2, (h8 % 2) * 64
                qh = qb[row0:row0 + 64, hp * 2048:(hp + 1) * 2048]
                kh = kb[row0:row0 + 64, hp * 2048:(hp + 1) * 2048]
                vh = vb[row0:row0 + 64, hp * 2048:(hp + 1) * 2048]
                prev_vc = {0: None, 1: None}
                vts = []
                idv = B.ident_bf[row0:row0 + 64, row0:row0 + 64]
                for kt2 in range(16):
                    tsl = B.tb[kt2 % 2][:, 0:64]
                    tv = S.op("tensor", lambda e_, kt2=kt2, tsl=tsl, vh=vh, idv=idv: e_.transpose(
                        tsl, vh[:, kt2 * 128:(kt2 + 1) * 128], idv),
                        waits=[prev_vc[kt2 % 2], ("e", "vector", S.ecnt["vector"])],
                        mark=True)
                    cv = S.op("vector", lambda e_, kt2=kt2, tsl=tsl: e_.tensor_copy(
                        v_tm[:, kt2 * 64:(kt2 + 1) * 64], tsl), waits=[tv], mark=True)
                    prev_vc[kt2 % 2] = cv
                    vts.append(cv)
                prev_avev = None
                for qt in range(16):
                    dt_tile = qt if qt < 8 else qt - 8
                    klen = (dt_tile + 1) * 128
                    ngrp = (klen + 511) // 512
                    evs = []
                    for g2 in range(ngrp):
                        n = min(512, klen - g2 * 512)
                        bank = B.pb[g2 % 2]
                        mm = S.op("tensor", lambda e_, qt=qt, g2=g2, n=n, bank=bank, qh=qh, kh=kh: e_.matmul(
                            bank[:, 0:n], qh[:, qt * 128:(qt + 1) * 128],
                            kh[:, g2 * 512: g2 * 512 + n], start=True, stop=True),
                            waits=[evs[-2] if len(evs) >= 2 else None],
                            mark=True)
                        if n > 128 or g2 < ngrp - 1:
                            ncopy = n if g2 < ngrp - 1 else n - 128
                            ev = S.op("vector", lambda e_, g2=g2, ncopy=ncopy, bank=bank: e_.tensor_copy(
                                scores[:, g2 * 512: g2 * 512 + ncopy], bank[:, 0:ncopy]),
                                waits=[mm], mark=True)
                            evs.append(ev)
                        if g2 == ngrp - 1:
                            ev = S.op("vector", lambda e_, dt_tile=dt_tile, g2=g2, n=n, bank=bank: e_.tensor_add(
                                scores[:, dt_tile * 128: dt_tile * 128 + 128],
                                bank[:, n - 128:n], B.causal),
                                waits=[mm], mark=True)
                            evs.append(ev)
                    mx = S.op("vector", lambda e_, klen=klen: e_.reduce_max(
                        st[:, 0:1], scores[:, 0:klen], axis=AX.X, negate=True),
                        waits=evs, mark=True)
                    ex = S.op("scalar", lambda e_, klen=klen: e_.activation(
                        probs_e[:, 0:klen], scores[:, 0:klen], AF.Exp,
                        bias=st[:, 0:1], accum_out=st[:, 1:2]), waits=[mx], mark=True)
                    rd = S.op("vector", lambda e_: e_.reciprocal(st[:, 2:3], st[:, 1:2]),
                              waits=[ex], mark=True)
                    nm = S.op("vector", lambda e_, klen=klen: e_.tensor_scalar(
                        probs_n[:, 0:klen], probs_e[:, 0:klen], st[:, 2:3], None,
                        ALU.mult), waits=[rd], mark=True)
                    prev_pc = {0: None, 1: None}
                    for kt2 in range(dt_tile + 1):
                        tsl = B.tb[kt2 % 2][:, 64:192]
                        tp = S.op("tensor", lambda e_, kt2=kt2, tsl=tsl: e_.transpose(
                            tsl, probs_n[:, kt2 * 128:(kt2 + 1) * 128], B.ident_bf),
                            waits=[nm, prev_pc[kt2 % 2]], mark=True)
                        cp = S.op("vector", lambda e_, kt2=kt2, tsl=tsl: e_.tensor_copy(
                            pT[kt2 % 2], tsl), waits=[tp], mark=True)
                        prev_pc[kt2 % 2] = cp
                        mm = S.op("tensor", lambda e_, kt2=kt2, dt_tile=dt_tile: e_.matmul(
                            B.pb[4][0:64, 0:128], v_tm[:, kt2 * 64:(kt2 + 1) * 64],
                            pT[kt2 % 2], start=(kt2 == 0), stop=(kt2 == dt_tile)),
                            waits=[cp, vts[kt2], prev_avev], mark=(kt2 == dt_tile))
                    oev = S.op("vector", lambda e_, qt=qt, hp=hp, row0=row0: e_.tensor_copy(
                        og[row0:row0 + 64, hp * 2048 + qt * 128: hp * 2048 + (qt + 1) * 128],
                        B.pb[4][0:64, 0:128]), waits=[mm], mark=True)
                    prev_avev = oev
            if b == 0 and "dbgog" in aps:
                S.full_barrier()
                S.dma("sync", aps["dbgog" if half == 0 else "dbgog2"], og)
            lwo = S.dma("gpsimd", wslot[:, 0:4096].rearrange("p (t n) -> p t n", t=4),
                        aps["woT_bf"][half * 512:(half + 1) * 512, :]
                        .rearrange("(t p) n -> p t n", p=128),
                        waits=[last_proj_mm, last_wo_mm])
            prev_ao = {0: None, 1: None}
            prev_aodma = {0: None, 1: None}
            for dout in range(DT):
                for grp in range(4):
                    bank = B.pb[2 + (dout * 4 + grp) % 2]
                    for kt in range(4):
                        mm = S.op("tensor", lambda e_, dout=dout, grp=grp, kt=kt, bank=bank: e_.matmul(
                            bank[:, 0:512],
                            wslot[:, kt * 1024 + dout * 128: kt * 1024 + (dout + 1) * 128],
                            og[:, kt * 2048 + grp * 512: kt * 2048 + (grp + 1) * 512],
                            start=(kt == 0), stop=(kt == 3)),
                            waits=[lwo, prev_ao[(dout * 4 + grp) % 2],
                                   ("e", "vector", S.ecnt["vector"])],
                            mark=(kt == 3))
                    last_wo_mm = mm
                    par = (dout * 4 + grp) % 2
                    aev = S.op("vector", lambda e_, par=par, bank=bank: e_.tensor_copy(
                        ao[par], bank[:, 0:512]), waits=[mm, prev_aodma[par]], mark=True)
                    prev_ao[par] = aev
                    kw = {} if half == 0 else {"accum_op": ALU.add}
                    ad = S.dma("gpsimd", fm_view(aps["a1T"])[:, dout,
                               b * SEQ + grp * 512: b * SEQ + (grp + 1) * 512],
                               ao[par], waits=[aev], **kw)
                    prev_aodma[par] = ad
    S.full_barrier()


def emit_l0moe(nc, S, B, aps):
    af, ab = B.af, B.ab
    w1sb = ab[0:128, 0:32768]
    w2sb = ab[0:128, 32768:65536]
    hg = ab[0:128, 65536:81920]
    xng = ab[0:128, 81920:86016]
    fo = [af[0:128, 0:512], af[0:128, 512:1024]]
    ssc = af[0:128, 1024:1536]
    srow = af[0:1, 1536:5632]
    b1c = af[0:128, 5632:5664]
    b2c = af[0:128, 5664:5672]

    last_mm1 = None
    last_mm2 = None
    last_silu = None
    last_fev = None
    last_bcast = None
    for e in range(8):
        lw1 = S.dma("gpsimd", w1sb.rearrange("p (t n) -> p t n", t=8),
                    aps["L0_ffn_W1"][e].rearrange("(t p) n -> p t n", p=128),
                    waits=[last_mm1])
        lw2 = S.dma("gpsimd", w2sb.rearrange("p (t n) -> p t n", t=32),
                    aps["L0_ffn_W2"][e].rearrange("(t p) n -> p t n", p=128),
                    waits=[last_mm2])
        lb1 = S.dma("sync", b1c, aps["b1c"][e], waits=[last_silu])
        lb2 = S.dma("sync", b2c, aps["b2c"][e], waits=[last_fev])
        lsr = S.dma("sync", srow, aps["selsc"][e:e + 1, :], waits=[last_bcast])
        prev_silu = {0: None, 1: None}
        prev_fev = {0: None, 1: None}
        prev_fdma = {0: None, 1: None}
        prev_cssc = None
        for blk in range(8):
            c0 = blk * 512
            lx = S.dma("sync" if blk % 2 == 0 else "scalar",
                       xng.rearrange("p (t c) -> p t c", t=8),
                       fm_view(aps["xn2T"])[:, :, c0:c0 + 512],
                       waits=[last_mm1])
            mbc = S.op("tensor", lambda e_, c0=c0: e_.matmul(
                B.pb[4][0:128, 0:512], B.ones_row, srow[0:1, c0:c0 + 512],
                start=True, stop=True), waits=[lsr, prev_cssc], mark=True)
            last_bcast = mbc
            cssc = S.op("vector", lambda e_: e_.tensor_copy(ssc, B.pb[4][0:128, 0:512]),
                        waits=[mbc], mark=True)
            prev_cssc = cssc
            silu_toks = []
            for ft in range(32):
                bank = B.pb[ft % 2]
                for kt in range(DT):
                    mm = S.op("tensor", lambda e_, ft=ft, kt=kt, bank=bank: e_.matmul(
                        bank[:, 0:512],
                        w1sb[:, kt * 4096 + ft * 128: kt * 4096 + (ft + 1) * 128],
                        xng[:, kt * 512:(kt + 1) * 512],
                        start=(kt == 0), stop=(kt == DT - 1)),
                        waits=[lw1, lx, prev_silu[ft % 2]],
                        mark=(kt == DT - 1))
                last_mm1 = mm
                sil = S.op("scalar", lambda e_, ft=ft, bank=bank: e_.activation(
                    hg[:, ft * 512:(ft + 1) * 512], bank[:, 0:512], AF.Silu,
                    bias=b1c[:, ft:ft + 1]), waits=[mm, lb1], mark=True)
                prev_silu[ft % 2] = sil
                silu_toks.append(sil)
                last_silu = sil
            for ot in range(DT):
                bank = B.pb[2 + ot % 2]
                for kt in range(32):
                    mm = S.op("tensor", lambda e_, ot=ot, kt=kt, bank=bank: e_.matmul(
                        bank[:, 0:512],
                        w2sb[:, kt * 1024 + ot * 128: kt * 1024 + (ot + 1) * 128],
                        hg[:, kt * 512:(kt + 1) * 512],
                        start=(kt == 0), stop=(kt == 31)),
                        waits=[lw2, silu_toks[31], prev_fev[ot % 2]],
                        mark=(kt == 31))
                last_mm2 = mm
                fev = S.op("vector", lambda e_, ot=ot, bank=bank: e_.scalar_tensor_tensor(
                    fo[ot % 2], bank[:, 0:512], b2c[:, ot:ot + 1], ssc,
                    ALU.add, ALU.mult),
                    waits=[mm, lb2, cssc, prev_fdma[ot % 2]], mark=True)
                last_fev = fev
                prev_fev[ot % 2] = fev
                kw = {"accum_op": ALU.add} if e > 0 else {}
                fd = S.dma("gpsimd", fm_view(aps["f0T"])[:, ot, c0:c0 + 512],
                           fo[ot % 2], waits=[fev], **kw)
                prev_fdma[ot % 2] = fd
    S.full_barrier()


def emit_resln(nc, S, B, aps, xin, ain, fin, xout, xnout):
    af, ab = B.af, B.ab
    GT = 256
    xa = af[0:128, 0:2048]
    xb = af[0:128, 2048:4096]
    xd = af[0:128, 4096:6144]
    xn1 = ab[0:128, 0:2048]
    r = lambda i: af[0:1, 6144 + i * 256: 6144 + (i + 1) * 256]
    for g in range(T // GT):
        c0 = g * GT
        la = S.dma("sync", xa.rearrange("p (t c) -> p t c", t=8),
                   fm_view(aps[xin])[:, :, c0:c0 + GT])
        lb = S.dma("scalar", xb.rearrange("p (t c) -> p t c", t=8),
                   fm_view(aps[ain])[:, :, c0:c0 + GT])
        adds1 = []
        for kt in range(DT):
            o = kt * GT
            a1 = S.op("vector", lambda e, o=o: e.tensor_add(
                xd[:, o:o + GT], xa[:, o:o + GT], xb[:, o:o + GT]),
                waits=[la, lb], mark=True)
            adds1.append(a1)
        lc = S.dma("gpsimd", xb.rearrange("p (t c) -> p t c", t=8),
                   fm_view(aps[fin])[:, :, c0:c0 + GT], waits=adds1)
        adds = []
        for kt in range(DT):
            o = kt * GT
            a2 = S.op("vector", lambda e, o=o: e.tensor_add(
                xa[:, o:o + GT], xd[:, o:o + GT], xb[:, o:o + GT]),
                waits=[lc], mark=True)
            adds.append(a2)
        for kt in range(DT):
            o = kt * GT
            sqk = S.op("scalar", lambda e, o=o: e.activation(
                xb[:, o:o + GT], xa[:, o:o + GT], AF.Square),
                waits=[adds[kt]], mark=True)
            S.op("tensor", lambda e, o=o, kt=kt: e.matmul(
                B.pb[2][0:1, 0:GT], B.ones_col, xa[:, o:o + GT],
                start=(kt == 0), stop=(kt == DT - 1)), waits=[adds[kt]])
            mS2 = S.op("tensor", lambda e, o=o, kt=kt: e.matmul(
                B.pb[3][0:1, 0:GT], B.ones_col, xb[:, o:o + GT],
                start=(kt == 0), stop=(kt == DT - 1)), waits=[sqk],
                mark=(kt == DT - 1))
        S.op("vector", lambda e: e.tensor_scalar(r(0), B.pb[2][0:1, 0:GT],
                                                 1.0 / D, None, ALU.mult), waits=[mS2])
        S.op("vector", lambda e: e.tensor_mul(r(1), r(0), r(0)))
        S.op("vector", lambda e: e.tensor_scalar(r(2), B.pb[3][0:1, 0:GT],
                                                 1.0 / D, None, ALU.mult))
        vv = S.op("vector", lambda e: e.tensor_sub(r(3), r(2), r(1)), mark=True)
        ln = S.op("scalar", lambda e: e.activation(r(2), r(3), AF.Ln,
                                                   bias=B.cc[0:1, 0:1]),
                  waits=[vv], mark=True)
        ist = S.op("scalar", lambda e: e.activation(r(4), r(2), AF.Exp,
                                                    scale=-0.5), mark=True)
        mb = S.op("tensor", lambda e: e.matmul(
            B.pb[4][0:128, 0:GT], B.ones_row, r(0), start=True, stop=True),
            waits=[vv], mark=True)
        ib = S.op("tensor", lambda e: e.matmul(
            B.pb[5][0:128, 0:GT], B.ones_row, r(4), start=True, stop=True),
            waits=[ist], mark=True)
        outs = []
        for kt in range(DT):
            o = kt * GT
            S.op("vector", lambda e, o=o: e.tensor_sub(
                xd[:, o:o + GT], xa[:, o:o + GT], B.pb[4][0:128, 0:GT]),
                waits=[mb])
            nx = S.op("vector", lambda e, o=o: e.tensor_mul(
                xn1[:, o:o + GT], xd[:, o:o + GT], B.pb[5][0:128, 0:GT]),
                waits=[ib], mark=True)
            outs.append(nx)
        S.dma("sync", fm_view(aps[xout])[:, :, c0:c0 + GT],
              xa.rearrange("p (t c) -> p t c", t=8), waits=adds)
        S.dma("gpsimd", fm_view(aps[xnout])[:, :, c0:c0 + GT],
              xn1.rearrange("p (t c) -> p t c", t=8), waits=outs)
        S.full_barrier()


def emit_egress(nc, S, B, aps):
    af = B.af
    GT = 256
    o_rows = aps["out"].rearrange("b n d -> (b n) d")
    xa = af[0:128, 0:2048]
    xb = af[0:128, 2048:4096]
    xd = af[0:128, 4096:6144]
    otile = af[0:128, 6144:7168]
    for g in range(T // GT):
        c0 = g * GT
        la = S.dma("sync", xa.rearrange("p (t c) -> p t c", t=8),
                   fm_view(aps["x1T"])[:, :, c0:c0 + GT])
        lb = S.dma("scalar", xb.rearrange("p (t c) -> p t c", t=8),
                   fm_view(aps["a1T"])[:, :, c0:c0 + GT])
        adds1 = []
        for kt in range(DT):
            o = kt * GT
            a1 = S.op("vector", lambda e, o=o: e.tensor_add(
                xd[:, o:o + GT], xa[:, o:o + GT], xb[:, o:o + GT]),
                waits=[la, lb], mark=True)
            adds1.append(a1)
        lc = S.dma("gpsimd", xb.rearrange("p (t c) -> p t c", t=8),
                   fm_view(aps["f1T"])[:, :, c0:c0 + GT], waits=adds1)
        adds = []
        for kt in range(DT):
            o = kt * GT
            a2 = S.op("vector", lambda e, o=o: e.tensor_add(
                xa[:, o:o + GT], xd[:, o:o + GT], xb[:, o:o + GT]),
                waits=[lc], mark=True)
            adds.append(a2)
        prev_cp = {0: None, 1: None}
        for s in range(GT // 128):
            cps = []
            for kt in range(DT):
                pslot = B.pb[(kt % 2) + 2][:, 0:128]
                t1 = S.op("tensor", lambda e, kt=kt, s=s, pslot=pslot: e.transpose(
                    pslot, xa[:, kt * GT + s * 128: kt * GT + s * 128 + 128],
                    B.ident_f), waits=[adds[kt], prev_cp[kt % 2]], mark=True)
                c1 = S.op("vector", lambda e, kt=kt, pslot=pslot: e.tensor_copy(
                    otile[:, kt * 128:(kt + 1) * 128], pslot),
                    waits=[t1], mark=True)
                prev_cp[kt % 2] = c1
                cps.append(c1)
            dto = S.dma("sync", o_rows[(g * 2 + s) * 128:(g * 2 + s) * 128 + 128, :],
                        otile, waits=cps)
            prev_cp = {0: dto, 1: dto}
        S.full_barrier()


def emit_l1ffn(nc, S, B, aps):
    af, ab = B.af, B.ab
    GT = 512
    u1 = ab[0:128, 0:2048]
    v1 = ab[0:128, 2048:10240]
    u2 = ab[0:128, 10240:18432]
    v2 = ab[0:128, 18432:20480]
    w0 = S.dma("gpsimd", u1.rearrange("p (t n) -> p t n", t=8),
               aps["L1_ffn_U1"].rearrange("(t p) n -> p t n", p=128))
    w1 = S.dma("gpsimd", v1.rearrange("p (t n) -> p t n", t=2),
               aps["L1_ffn_V1"].rearrange("(t p) n -> p t n", p=128))
    w2 = S.dma("gpsimd", u2.rearrange("p (t n) -> p t n", t=32),
               aps["L1_ffn_U2"].rearrange("(t p) n -> p t n", p=128))
    w3 = S.dma("gpsimd", v2.rearrange("p (t n) -> p t n", t=2),
               aps["L1_ffn_V2"].rearrange("(t p) n -> p t n", p=128))
    b1c = af[0:128, 0:32]
    b2c = af[0:128, 32:40]
    wb1 = S.dma("sync", b1c, aps["l1b1c"])
    wb2 = S.dma("sync", b2c, aps["l1b2c"])
    xng = ab[0:128, 20480:24576]
    y1 = ab[0:128, 24576:25600]
    hg = ab[0:128, 25600:41984]
    qg = ab[0:128, 41984:43008]
    t0 = af[0:128, 64:576]
    sqc = af[0:128, 576:1088]
    cub = af[0:128, 1088:1600]
    inn = af[0:128, 1600:2112]
    th = af[0:128, 2112:2624]
    tt_ = af[0:128, 2624:3136]
    fo = af[0:128, 3136:3648]
    S.full_barrier()
    C1 = 0.044715
    C2 = 0.7978845608028654
    for g in range(T // GT):
        c0 = g * GT
        lx = S.dma("sync", xng.rearrange("p (t c) -> p t c", t=8),
                   fm_view(aps["xn1T"])[:, :, c0:c0 + GT])
        for mt in range(2):
            for kt in range(DT):
                mm = S.op("tensor", lambda e, mt=mt, kt=kt: e.matmul(
                    B.pb[0][:, 0:GT], u1[:, kt * 256 + mt * 128: kt * 256 + mt * 128 + 128],
                    xng[:, kt * GT:(kt + 1) * GT],
                    start=(kt == 0), stop=(kt == DT - 1)),
                    waits=[lx, w0, ("e", "vector", S.ecnt["vector"])],
                    mark=(kt == DT - 1))
            S.op("vector", lambda e, mt=mt: e.tensor_copy(
                y1[:, mt * GT:(mt + 1) * GT], B.pb[0][:, 0:GT]),
                waits=[mm], mark=True)
        hts = []
        for ft in range(32):
            for kt in range(2):
                mm = S.op("tensor", lambda e, ft=ft, kt=kt: e.matmul(
                    B.pb[1][:, 0:GT], v1[:, kt * 4096 + ft * 128: kt * 4096 + ft * 128 + 128],
                    y1[:, kt * GT:(kt + 1) * GT],
                    start=(kt == 0), stop=(kt == 1)),
                    waits=[w1, ("e", "vector", S.ecnt["vector"])],
                    mark=(kt == 1))
            bc = S.op("vector", lambda e, ft=ft: e.tensor_scalar(
                t0, B.pb[1][:, 0:GT], b1c[:, ft:ft + 1], None, ALU.add),
                waits=[mm, wb1], mark=True)
            S.op("scalar", lambda e: e.activation(sqc, t0, AF.Square))
            S.op("vector", lambda e: e.tensor_mul(cub, sqc, t0),
                 waits=[("e", "scalar", S.ecnt["scalar"])])
            S.op("vector", lambda e: e.scalar_tensor_tensor(
                inn, cub, C1, t0, ALU.mult, ALU.add))
            tha = S.op("scalar", lambda e: e.activation(th, inn, AF.Tanh, scale=C2),
                       waits=[("e", "vector", S.ecnt["vector"])], mark=True)
            S.op("vector", lambda e: e.tensor_scalar(tt_, th, 0.5, 0.5,
                                                     ALU.mult, ALU.add), waits=[tha])
            hv = S.op("vector", lambda e, ft=ft: e.tensor_mul(
                hg[:, ft * GT:(ft + 1) * GT], t0, tt_), mark=True)
            hts.append(hv)
        for mt in range(2):
            for kt in range(32):
                mm = S.op("tensor", lambda e, mt=mt, kt=kt: e.matmul(
                    B.pb[2][:, 0:GT], u2[:, kt * 256 + mt * 128: kt * 256 + mt * 128 + 128],
                    hg[:, kt * GT:(kt + 1) * GT],
                    start=(kt == 0), stop=(kt == 31)),
                    waits=[w2, hts[kt], ("e", "vector", S.ecnt["vector"])],
                    mark=(kt == 31))
            S.op("vector", lambda e, mt=mt: e.tensor_copy(
                qg[:, mt * GT:(mt + 1) * GT], B.pb[2][:, 0:GT]),
                waits=[mm], mark=True)
        prev_fc = None
        prev_fd = None
        for ot in range(DT):
            for kt in range(2):
                mm = S.op("tensor", lambda e, ot=ot, kt=kt: e.matmul(
                    B.pb[3][:, 0:GT], v2[:, kt * 1024 + ot * 128: kt * 1024 + ot * 128 + 128],
                    qg[:, kt * GT:(kt + 1) * GT],
                    start=(kt == 0), stop=(kt == 1)),
                    waits=[w3, ("e", "vector", S.ecnt["vector"]), prev_fc],
                    mark=(kt == 1))
            fc = S.op("vector", lambda e, ot=ot: e.tensor_scalar(
                fo, B.pb[3][:, 0:GT], b2c[:, ot:ot + 1], None, ALU.add),
                waits=[mm, wb2, prev_fd], mark=True)
            prev_fc = fc
            prev_fd = S.dma("sync", fm_view(aps["f1T"])[:, ot, c0:c0 + GT], fo, waits=[fc])
        S.full_barrier()


def catalog():
    return {
        "x_in": ([BPC, SEQ, D], F32),
        "out": ([BPC, SEQ, D], F32),
        "wo_sc_bf": ([D, D], BF16),
        "wqT_bf": ([D, D], BF16),
        "wkT_bf": ([D, D], BF16),
        "wvT_bf": ([D, D], BF16),
        "woT_bf": ([D, D], BF16),
        "gate_wTf": ([D, 8], F32),
        "gsum": ([1, 8], F32),
        "b1c": ([8, 128, 32], F32),
        "b2c": ([8, 128, 8], F32),
        "l1b1c": ([128, 32], F32),
        "l1b2c": ([128, 8], F32),
        "rmsw_c": ([128, 8], F32),
        "dbg_nwc": ([128, 8], F32),
        "dbgq": ([128, 8192], BF16),
        "dbgk": ([128, 8192], BF16),
        "dbgv": ([128, 8192], BF16),
        "dbgog": ([128, 8192], BF16),
        "dbgog2": ([128, 8192], BF16),
        "x0T": ([D, T], F32),
        "xn0T": ([D, T], BF16),
        "xn2T": ([D, T], BF16),
        "selsc": ([16, T], F32),
        "a0T": ([D, T], F32),
        "f0T": ([D, T], F32),
        "x1T": ([D, T], F32),
        "xn1T": ([D, T], BF16),
        "a1T": ([D, T], F32),
        "f1T": ([D, T], F32),
    }


WEIGHT_SHAPES = {
    "L0_attn_Wq": [D, D], "L0_attn_Wk": [D, D], "L0_attn_Wv": [D, D],
    "L0_attn_Wf1": [D, 128], "L0_attn_Wf2": [128, 512],
    "L0_attn_Wg1": [D, 128], "L0_attn_Wg2": [128, D],
    "L0_attn_Wo": [D, D], "L0_attn_norm_w": [D],
    "L0_ffn_rms_w": [D], "L0_ffn_gate_w": [8, D],
    "L0_ffn_W1": [8, D, 4096], "L0_ffn_b1": [8, 4096],
    "L0_ffn_W2": [8, 4096, D], "L0_ffn_b2": [8, D],
    "L1_attn_Wq": [D, D], "L1_attn_Wk": [D, D], "L1_attn_Wv": [D, D],
    "L1_attn_Wo": [D, D],
    "L1_ffn_U1": [D, 256], "L1_ffn_V1": [256, 4096], "L1_ffn_b1": [4096],
    "L1_ffn_U2": [4096, 256], "L1_ffn_V2": [256, D], "L1_ffn_b2": [D],
}


def build_launch(phase_fns, ext_in, ext_out, weights_needed):
    nc = bass.Bass("TRN2", target_bir_lowering=False)
    cat = catalog()
    aps = {}
    for name in weights_needed:
        aps[name] = nc.dram_tensor(name, WEIGHT_SHAPES[name], F32,
                                   kind="ExternalInput").ap()
    for name in ext_in:
        sh, dt = cat[name]
        aps[name] = nc.dram_tensor(name, sh, dt, kind="ExternalInput").ap()
    for name in ext_out:
        sh, dt = cat[name]
        aps[name] = nc.dram_tensor(name, sh, dt, kind="ExternalOutput").ap()
    S = Sched(nc)
    B = Bufs(nc)
    emit_const(nc, S, B)
    S.full_barrier()
    for fn in phase_fns:
        fn(nc, S, B, aps)
        S.full_barrier()
    S.finalize()
    return nc


LAUNCHES = [
    ("prep", [emit_prep], [],
     ["wo_sc_bf", "gate_wTf", "gsum", "b1c", "b2c", "l1b1c", "l1b2c", "rmsw_c",
      "wqT_bf", "wkT_bf", "wvT_bf", "woT_bf"],
     ["L0_attn_norm_w", "L0_ffn_rms_w", "L0_ffn_b1", "L0_ffn_b2",
      "L1_ffn_b1", "L1_ffn_b2", "L0_ffn_gate_w", "L0_attn_Wo",
      "L1_attn_Wq", "L1_attn_Wk", "L1_attn_Wv", "L1_attn_Wo"]),
    ("ingest", [emit_ingest], ["x_in", "rmsw_c", "gate_wTf", "gsum"],
     ["x0T", "xn0T", "xn2T", "selsc"], []),
    ("l0attn", [emit_l0attn], ["xn0T", "wo_sc_bf"], ["a0T"],
     ["L0_attn_Wq", "L0_attn_Wk", "L0_attn_Wv", "L0_attn_Wf1", "L0_attn_Wf2",
      "L0_attn_Wg1", "L0_attn_Wg2"]),
    ("l1attn", [emit_l1attn], ["xn1T", "wqT_bf", "wkT_bf", "wvT_bf", "woT_bf"],
     ["a1T"], []),
    ("l0moe", [emit_l0moe], ["xn2T", "selsc", "b1c", "b2c"], ["f0T"],
     ["L0_ffn_W1", "L0_ffn_W2"]),
    ("resln", [lambda nc, S, B, aps: emit_resln(nc, S, B, aps, "x0T", "a0T",
                                                "f0T", "x1T", "xn1T")],
     ["x0T", "a0T", "f0T"], ["x1T", "xn1T"], []),
    ("l1ffn", [emit_l1ffn], ["xn1T", "l1b1c", "l1b2c"], ["f1T"],
     ["L1_ffn_U1", "L1_ffn_V1", "L1_ffn_U2", "L1_ffn_V2"]),
    ("egress", [emit_egress], ["x1T", "a1T", "f1T"], ["out"], []),
]


_NCS = {}


def _get_nc(name):
    if name not in _NCS:
        for n, phases, ext_in, ext_out, weights in LAUNCHES:
            if n == name:
                _NCS[name] = (build_launch(phases, ext_in, ext_out, weights),
                              ext_in, ext_out, weights)
                break
    return _NCS[name]


def kernel(**inputs):
    x = np.ascontiguousarray(np.asarray(inputs["L0_attn_x"]))
    stores = [{"x_in": np.ascontiguousarray(x[c * BPC:(c + 1) * BPC])}
              for c in range(NCORES)]
    weights_np = {k: np.ascontiguousarray(np.asarray(v))
                  for k, v in inputs.items() if k != "L0_attn_x"}
    order = ["prep", "ingest", "l0attn", "l0moe", "resln", "l1attn", "l1ffn",
             "egress"]
    for name in order:
        nc, ext_in, ext_out, weights = _get_nc(name)
        in_maps = []
        for c in range(NCORES):
            im = {}
            for w in weights:
                im[w] = weights_np[w]
            for t in ext_in:
                im[t] = stores[c][t]
            in_maps.append(im)
        res = run_bass_kernel_spmd(nc, in_maps, core_ids=list(range(NCORES)))
        for c in range(NCORES):
            for t in ext_out:
                stores[c][t] = np.asarray(res.results[c][t])
    return np.concatenate([stores[c]["out"] for c in range(NCORES)], axis=0)


# version 04
# speedup vs baseline: 1.0855x; 1.0855x over previous
import numpy as np
import ml_dtypes
import concourse.bass as bass
import concourse.mybir as mybir
from concourse.bass_utils import run_bass_kernel_spmd

F32 = mybir.dt.float32
BF16 = mybir.dt.bfloat16
AF = mybir.ActivationFunctionType
ALU = mybir.AluOpType
AX = mybir.AxisListType

NCORES = 8
BPC = 2
SEQ = 2048
T = BPC * SEQ
D = 1024
DT = D // 128
NEG = -1.0e30

ENGINES = ("tensor", "vector", "scalar", "gpsimd", "sync")
DMA_QUEUES = ("sync", "scalar", "gpsimd")


class Sched:

    def __init__(self, nc):
        self.nc = nc
        self.plan = {e: [] for e in ENGINES}
        self.esem = {e: nc.alloc_semaphore(f"es_{e}") for e in ENGINES}
        self.dsem = {q: nc.alloc_semaphore(f"ds_{q}") for q in DMA_QUEUES}
        self.ecnt = {e: 0 for e in ENGINES}
        self.dcnt = {q: 0 for q in DMA_QUEUES}

    SELF_SERIAL = ("vector", "scalar", "gpsimd")

    def op(self, eng, fn, waits=(), mark=False):
        waits = [w for w in waits if w is not None]
        if eng in self.SELF_SERIAL:
            if self.ecnt[eng]:
                waits.append(("e", eng, self.ecnt[eng]))
            mark = True
        tok = None
        if mark:
            self.ecnt[eng] += 1
            tok = ("e", eng, self.ecnt[eng])
        self.plan[eng].append((fn, tuple(waits),
                               ("e", eng) if mark else None))
        return tok

    def dma(self, q, out, in_, waits=(), **kw):
        self.dcnt[q] += 16
        tok = ("d", q, self.dcnt[q])
        def fn(e, out=out, in_=in_, kw=kw):
            return e.dma_start(out=out, in_=in_, **kw)
        self.plan[q].append((fn, tuple(w for w in waits if w is not None),
                             ("d", q)))
        return tok

    def barrier_toks(self):
        toks = []
        for e in ENGINES:
            if self.ecnt[e]:
                toks.append(("e", e, self.ecnt[e]))
        for q in DMA_QUEUES:
            if self.dcnt[q]:
                toks.append(("d", q, self.dcnt[q]))
        return toks

    def full_barrier(self):
        toks = self.barrier_toks()
        for e in ENGINES:
            self.plan[e].append((None, tuple(toks), None))

    def finalize(self):
        nc = self.nc
        with nc.Block() as block:
            for eng_name in ENGINES:
                def run(eng, eng_name=eng_name):
                    hwm = {}
                    for fn, waits, markkind in self.plan[eng_name]:
                        for kind, name, val in waits:
                            if (kind == "e" and name == eng_name
                                    and eng_name not in self.SELF_SERIAL):
                                continue
                            if val <= 0:
                                continue
                            sem = self.esem[name] if kind == "e" else self.dsem[name]
                            key = (kind, name)
                            if hwm.get(key, 0) < val:
                                hwm[key] = val
                                eng.wait_ge(sem, val)
                        if fn is None:
                            continue
                        ins = fn(eng)
                        if markkind is not None:
                            kind, name = markkind
                            if kind == "e":
                                ins.then_inc(self.esem[name], 1)
                            else:
                                ins.then_inc(self.dsem[name], 16)
                    if eng_name == "sync":
                        for q in DMA_QUEUES:
                            if self.dcnt[q]:
                                eng.wait_ge(self.dsem[q], self.dcnt[q])
                        for e2 in ENGINES:
                            if e2 != "sync" and self.ecnt[e2]:
                                eng.wait_ge(self.esem[e2], self.ecnt[e2])
                getattr(block, eng_name)(run)


class Bufs:

    def __init__(self, nc):
        ap = lambda h: h.ap()
        self.ab = ap(nc.alloc_sbuf_tensor("ab", [128, 88000], BF16))
        self.af = ap(nc.alloc_sbuf_tensor("af", [128, 8600], F32))
        self.ident_bf = ap(nc.alloc_sbuf_tensor("ident_bf", [128, 128], BF16))
        self.ident_f = ap(nc.alloc_sbuf_tensor("ident_f", [128, 128], F32))
        self.triu64 = ap(nc.alloc_sbuf_tensor("triu64", [64, 64], F32))
        self.causal = ap(nc.alloc_sbuf_tensor("causal", [128, 128], F32))
        self.ones_row = ap(nc.alloc_sbuf_tensor("ones_row", [1, 128], F32))
        self.ones_col = ap(nc.alloc_sbuf_tensor("ones_col", [128, 1], F32))
        self.cc = ap(nc.alloc_sbuf_tensor("cc", [128, 8], F32))
        self.pb = [ap(nc.alloc_psum_tensor(f"pb{i}", [128, 512], F32)) for i in range(6)]
        self.tb = [ap(nc.alloc_psum_tensor(f"tb{i}", [128, 1024], BF16)) for i in range(2)]


def emit_const(nc, S, B):
    S.op("gpsimd", lambda e: e.memset(B.ident_bf, 0.0))
    S.op("gpsimd", lambda e: e.affine_select(
        out=B.ident_bf, in_=B.ident_bf, compare_op=ALU.not_equal, fill=1.0,
        base=0, pattern=[[-1, 128]], channel_multiplier=1))
    S.op("gpsimd", lambda e: e.memset(B.ident_f, 0.0))
    S.op("gpsimd", lambda e: e.affine_select(
        out=B.ident_f, in_=B.ident_f, compare_op=ALU.not_equal, fill=1.0,
        base=0, pattern=[[-1, 128]], channel_multiplier=1))
    S.op("gpsimd", lambda e: e.memset(B.triu64, 0.0))
    S.op("gpsimd", lambda e: e.affine_select(
        out=B.triu64, in_=B.triu64, compare_op=ALU.is_gt, fill=1.0,
        base=0, pattern=[[-1, 64]], channel_multiplier=1))
    S.op("gpsimd", lambda e: e.memset(B.causal, 0.0))
    S.op("gpsimd", lambda e: e.affine_select(
        out=B.causal, in_=B.causal, compare_op=ALU.is_ge, fill=NEG,
        base=0, pattern=[[-1, 128]], channel_multiplier=1))
    S.op("gpsimd", lambda e: e.memset(B.ones_row, 1.0))
    S.op("gpsimd", lambda e: e.memset(B.cc[:, 0:1], 1e-5))
    S.op("gpsimd", lambda e: e.memset(B.cc[:, 1:2], 1e-6))
    S.op("gpsimd", lambda e: e.memset(B.cc[:, 2:3], 1.0))
    S.op("gpsimd", lambda e: e.memset(B.cc[:, 3:4], 128 * 1e-5))
    S.op("gpsimd", lambda e: e.memset(B.ones_col, 1.0), mark=True)


def emit_prep(nc, S, B, aps):
    af, ab = B.af, B.ab

    def vec_to_cols(src_ap, n, dst_sb):
        stg = af[0:n, 0:128]
        ld = S.dma("sync", stg, src_ap.rearrange("(a b) -> a b", b=128))
        tt = S.op("tensor", lambda e: e.transpose(
            B.pb[0][:, 0:n], stg, B.ident_f[0:n, 0:n]), waits=[ld], mark=True)
        cv = S.op("vector", lambda e: e.tensor_copy(dst_sb, B.pb[0][:, 0:n]),
                  waits=[tt], mark=True)
        S.full_barrier()
        return cv

    nwc = af[0:128, 256:264]
    vec_to_cols(aps["L0_attn_norm_w"], 8, nwc)
    if "dbg_nwc" in aps:
        S.dma("sync", aps["dbg_nwc"], nwc)
        S.full_barrier()
    rwc = af[0:128, 264:272]
    vec_to_cols(aps["L0_ffn_rms_w"], 8, rwc)
    S.dma("sync", aps["rmsw_c"], rwc)
    for ei in range(8):
        dst = af[0:128, 280:312]
        vec_to_cols(aps["L0_ffn_b1"][ei], 32, dst)
        S.dma("sync", aps["b1c"][ei], dst)
        S.full_barrier()
        dst2 = af[0:128, 312:320]
        vec_to_cols(aps["L0_ffn_b2"][ei], 8, dst2)
        S.dma("sync", aps["b2c"][ei], dst2)
        S.full_barrier()
    dst = af[0:128, 280:312]
    vec_to_cols(aps["L1_ffn_b1"], 32, dst)
    S.dma("sync", aps["l1b1c"], dst)
    S.full_barrier()
    dst2 = af[0:128, 312:320]
    vec_to_cols(aps["L1_ffn_b2"], 8, dst2)
    S.dma("sync", aps["l1b2c"], dst2)
    S.full_barrier()

    gsb = af[0:8, 4096:5120]
    g0 = S.dma("sync", gsb, aps["L0_ffn_gate_w"])
    for kt in range(DT):
        tt = S.op("tensor", lambda e, kt=kt: e.transpose(
            B.pb[1][:, kt * 8:(kt + 1) * 8], gsb[:, kt * 128:(kt + 1) * 128],
            B.ident_f[0:8, 0:8]), waits=[g0], mark=(kt == DT - 1))
    gft = af[0:128, 5120:5184]
    for kt in range(DT):
        cg = S.op("vector", lambda e, kt=kt: e.tensor_scalar(
            gft[:, kt * 8:(kt + 1) * 8], B.pb[1][:, kt * 8:(kt + 1) * 8],
            rwc[:, kt:kt + 1], None, ALU.mult), waits=[tt], mark=True)
    for kt in range(DT):
        mg = S.op("tensor", lambda e, kt=kt: e.matmul(
            B.pb[2][0:1, 0:8], B.ones_col, gft[:, kt * 8:(kt + 1) * 8],
            start=(kt == 0), stop=(kt == DT - 1)), waits=[cg], mark=(kt == DT - 1))
    gsum_sb = af[0:1, 5184:5192]
    cgs = S.op("vector", lambda e: e.tensor_copy(gsum_sb, B.pb[2][0:1, 0:8]),
               waits=[mg], mark=True)
    S.dma("sync", aps["gate_wTf"].rearrange("(t p) n -> p t n", p=128),
          gft.rearrange("p (t n) -> p t n", t=8), waits=[cg])
    S.dma("sync", aps["gsum"], gsum_sb, waits=[cgs])
    S.full_barrier()

    for wname, oname in (("L1_attn_Wq", "wqT_bf"), ("L1_attn_Wk", "wkT_bf"),
                         ("L1_attn_Wv", "wvT_bf"), ("L1_attn_Wo", "woT_bf")):
        for rb in range(DT):
            wrow = af[0:128, 1024:2048]
            ld = S.dma("sync", wrow, aps[wname][rb * 128:(rb + 1) * 128, :])
            prev_tc = {0: None, 1: None}
            for ct in range(DT):
                bank = B.pb[ct % 2]
                tt = S.op("tensor", lambda e, ct=ct, bank=bank: e.transpose(
                    bank[:, 0:128], wrow[:, ct * 128:(ct + 1) * 128], B.ident_f),
                    waits=[ld, prev_tc[ct % 2]], mark=True)
                wtile = ab[0:128, ct * 128:(ct + 1) * 128]
                cv = S.op("vector", lambda e, ct=ct, bank=bank, wtile=wtile: e.tensor_copy(
                    wtile, bank[:, 0:128]), waits=[tt], mark=True)
                prev_tc[ct % 2] = cv
                S.dma("scalar", aps[oname][ct * 128:(ct + 1) * 128,
                                           rb * 128:(rb + 1) * 128], wtile,
                      waits=[cv])
            S.full_barrier()

    for kt in range(DT):
        wrow = af[0:128, 1024:2048]
        wbf = ab[0:128, 0:1024]
        ld = S.dma("sync", wrow, aps["L0_attn_Wo"][kt * 128:(kt + 1) * 128, :])
        sc = S.op("vector", lambda e, kt=kt: e.tensor_scalar(
            wbf, wrow, nwc[:, kt:kt + 1], None, ALU.mult), waits=[ld], mark=True)
        S.dma("sync", aps["wo_sc_bf"][kt * 128:(kt + 1) * 128, :], wbf, waits=[sc])
        S.full_barrier()


def emit_ingest(nc, S, B, aps):
    af, ab = B.af, B.ab
    x_rows = aps["x_in"].rearrange("b n d -> (b n) d")

    rwc = af[0:128, 0:8]
    S.dma("sync", rwc, aps["rmsw_c"])
    gwTf = af[0:128, 8192:8256]
    S.dma("sync", gwTf.rearrange("p (t n) -> p t n", t=8),
          aps["gate_wTf"].rearrange("(t p) n -> p t n", p=128))
    gsum_sb = af[0:1, 8256:8264]
    g1 = S.dma("sync", gsum_sb, aps["gsum"])
    mgb = S.op("tensor", lambda e: e.matmul(
        B.pb[4][0:128, 0:8], B.ones_row, gsum_sb, start=True, stop=True),
        waits=[g1], mark=True)
    gsum_bc = af[0:128, 8264:8272]
    S.op("vector", lambda e: e.tensor_copy(gsum_bc, B.pb[4][0:128, 0:8]),
         waits=[mgb], mark=True)
    S.full_barrier()

    xt = af[0:128, 16:1040]
    sq = af[0:128, 1040:2064]
    xnf = af[0:128, 7168:8192]
    st = af[0:128, 2064:2080]
    st16 = af[0:128, 2080:2096]
    lgf = af[0:128, 2096:2112]
    xnt = ab[0:128, 64:1088]
    x2t = ab[0:128, 1088:2112]
    xoT = af[0:128, 2560:6656]
    xnoT = ab[0:128, 2112:6208]
    x2oT = ab[0:128, 6208:10304]
    selT = af[0:16, 6656:7168]

    for tt in range(32):
        gi = tt % 4
        tok_off = gi * 128
        ld = S.dma("sync", xt, x_rows[tt * 128:(tt + 1) * 128, :])
        s1 = S.op("vector", lambda e: e.reduce_sum(st[:, 0:1], xt, axis=AX.X),
                  waits=[ld], mark=True)
        qq = S.op("scalar", lambda e: e.activation(sq, xt, AF.Square),
                  waits=[ld], mark=True)
        s2 = S.op("vector", lambda e: e.reduce_sum(st[:, 1:2], sq, axis=AX.X),
                  waits=[qq], mark=True)
        S.op("vector", lambda e: e.tensor_scalar(st[:, 2:3], st[:, 0:1], 1.0 / D, None, ALU.mult))
        S.op("vector", lambda e: e.tensor_mul(st[:, 7:8], st[:, 2:3], st[:, 2:3]))
        S.op("vector", lambda e: e.tensor_scalar(st[:, 6:7], st[:, 1:2], 1.0 / D, None, ALU.mult))
        vv = S.op("vector", lambda e: e.tensor_sub(st[:, 3:4], st[:, 6:7], st[:, 7:8]),
                  mark=True)
        sd = S.op("scalar", lambda e: e.activation(st[:, 4:5], st[:, 3:4], AF.Ln,
                                                   bias=B.cc[:, 0:1]), waits=[vv], mark=True)
        vi = S.op("scalar", lambda e: e.activation(st[:, 10:11], st[:, 4:5], AF.Exp,
                                                   scale=-0.5), mark=True)
        xn = S.op("vector", lambda e: e.tensor_scalar(
            xnf, xt, st[:, 2:3], st[:, 10:11], ALU.subtract, ALU.mult),
            waits=[vi], mark=True)
        xnc = S.op("vector", lambda e: e.tensor_copy(xnt, xnf), mark=True)
        q2 = S.op("scalar", lambda e: e.activation(sq, xnf, AF.Square),
                  waits=[xn], mark=True)
        ms = S.op("vector", lambda e: e.reduce_sum(st[:, 5:6], sq, axis=AX.X),
                  waits=[q2], mark=True)
        s2d = S.op("scalar", lambda e: e.activation(st[:, 6:7], st[:, 5:6], AF.Ln,
                                                    bias=B.cc[:, 1:2], scale=1.0 / D),
                   waits=[ms], mark=True)
        r2 = S.op("scalar", lambda e: e.activation(st[:, 11:12], st[:, 6:7], AF.Exp,
                                                   scale=-0.5), mark=True)
        x2 = S.op("vector", lambda e: e.tensor_scalar(
            x2t, xnf, st[:, 11:12], None, ALU.mult), waits=[r2], mark=True)

        prev_c1 = {0: None, 1: None}
        for kt in range(DT):
            pslot = B.pb[kt % 2][:, 0:128]
            t1 = S.op("tensor", lambda e, kt=kt, pslot=pslot: e.transpose(
                pslot, xt[:, kt * 128:(kt + 1) * 128], B.ident_f),
                waits=[ld, xn, prev_c1[kt % 2]], mark=True)
            c1 = S.op("vector", lambda e, kt=kt, pslot=pslot, tok_off=tok_off: e.tensor_copy(
                xoT[:, kt * 512 + tok_off: kt * 512 + tok_off + 128], pslot),
                waits=[t1], mark=True)
            prev_c1[kt % 2] = c1
            tslot = B.tb[kt % 2][:, 0:128]
            t2 = S.op("tensor", lambda e, kt=kt, tslot=tslot: e.transpose(
                tslot, xnt[:, kt * 128:(kt + 1) * 128], B.ident_bf),
                waits=[c1, xnc], mark=True)
            c2 = S.op("vector", lambda e, kt=kt, tslot=tslot, tok_off=tok_off: e.tensor_copy(
                xnoT[:, kt * 512 + tok_off: kt * 512 + tok_off + 128], tslot),
                waits=[t2], mark=True)
            t3 = S.op("tensor", lambda e, kt=kt, tslot=tslot: e.transpose(
                tslot, x2t[:, kt * 128:(kt + 1) * 128], B.ident_bf),
                waits=[c2, x2], mark=True)
            c3 = S.op("vector", lambda e, kt=kt, tslot=tslot, tok_off=tok_off: e.tensor_scalar(
                x2oT[:, kt * 512 + tok_off: kt * 512 + tok_off + 128], tslot,
                rwc[:, kt:kt + 1], None, ALU.mult), waits=[t3], mark=True)

        ps_lg = B.pb[2]
        for kt in range(DT):
            mm = S.op("tensor", lambda e, kt=kt, tok_off=tok_off: e.matmul(
                ps_lg[0:128, 0:8],
                xoT[:, kt * 512 + tok_off: kt * 512 + tok_off + 128],
                gwTf[:, kt * 8:(kt + 1) * 8],
                start=(kt == 0), stop=(kt == DT - 1)),
                waits=[c1, c3], mark=(kt == DT - 1))
        v1 = S.op("vector", lambda e: e.tensor_scalar(
            lgf[:, 8:16], gsum_bc, st[:, 2:3], None, ALU.mult), waits=[xn], mark=True)
        v2 = S.op("vector", lambda e: e.tensor_sub(
            lgf[:, 0:8], ps_lg[0:128, 0:8], lgf[:, 8:16]), waits=[mm], mark=True)
        v3 = S.op("vector", lambda e: e.tensor_mul(
            st[:, 12:13], st[:, 10:11], st[:, 11:12]), waits=[r2], mark=True)
        v4 = S.op("vector", lambda e: e.tensor_scalar(
            lgf[:, 0:8], lgf[:, 0:8], st[:, 12:13], None, ALU.mult),
            waits=[v3], mark=True)
        mx = S.op("vector", lambda e: e.reduce_max(
            st[:, 8:9], lgf[:, 0:8], axis=AX.X, negate=True),
            waits=[v4], mark=True)
        ex = S.op("scalar", lambda e: e.activation(
            st16[:, 8:16], lgf[:, 0:8], AF.Exp, bias=st[:, 8:9],
            accum_out=st[:, 9:10]), waits=[mx], mark=True)
        sc = S.op("vector", lambda e: e.reciprocal(st16[:, 8:9], st[:, 9:10]),
                  waits=[ex], mark=True)
        sel = S.op("vector", lambda e: e.tensor_scalar(
            lgf[:, 8:16], lgf[:, 0:8], st[:, 8:9], 0.0, ALU.add, ALU.is_ge),
            waits=[mx, ex], mark=True)
        ssc = S.op("vector", lambda e: e.tensor_scalar(
            st16[:, 0:8], lgf[:, 8:16], st16[:, 8:9], None, ALU.mult),
            waits=[sel, sc], mark=True)
        ts = S.op("tensor", lambda e: e.transpose(
            B.pb[3][0:16, 0:128], st16[:, 0:16], B.ident_f),
            waits=[ssc], mark=True)
        cs = S.op("vector", lambda e, tok_off=tok_off: e.tensor_copy(
            selT[:, tok_off:tok_off + 128], B.pb[3][0:16, 0:128]),
            waits=[ts], mark=True)

        if gi == 3:
            g = tt // 4
            col0 = g * 512
            w = S.barrier_toks()
            S.dma("sync", aps["x0T"].rearrange("(t p) c -> p t c", p=128)[:, :, col0:col0 + 512],
                  xoT.rearrange("p (t c) -> p t c", t=8), waits=w)
            S.dma("gpsimd", aps["xn0T"].rearrange("(t p) c -> p t c", p=128)[:, :, col0:col0 + 512],
                  xnoT.rearrange("p (t c) -> p t c", t=8), waits=w)
            S.dma("gpsimd", aps["xn2T"].rearrange("(t p) c -> p t c", p=128)[:, :, col0:col0 + 512],
                  x2oT.rearrange("p (t c) -> p t c", t=8), waits=w)
            S.dma("scalar", aps["selsc"][:, col0:col0 + 512], selT, waits=w)
        S.full_barrier()


def fm_view(ap_):
    return ap_.rearrange("(t p) c -> p t c", p=128)


DBG_GSA = {"nch": None, "heads": None, "skip_post": False, "skip_wo": False, "skip_fg": False, "skip_qkv": False, "plain_evac": False, "skip_memset": False, "skip_ft": False}


def emit_l0attn(nc, S, B, aps):
    af, ab = B.af, B.ab
    C = 64
    NCH = SEQ // C
    H = 8
    xn_b = ab[0:128, 0:16384]
    og = ab[0:128, 0:16384]
    qb = ab[0:128, 16384:32768]
    kb = ab[0:128, 32768:49152]
    vb = ab[0:128, 49152:65536]
    fT = ab[0:128, 65536:73728]
    t_f = ab[0:128, 73728:75776]
    t_g = ab[0:128, 75776:77824]
    wg2 = ab[0:128, 77824:78848]
    wslot = ab[0:128, 78848:82944]
    stil_fm = ab[0:128, 82944:83200]
    stil_tm = ab[0:64, 83200:83712]
    pm_sb = ab[0:64, 83712:83776]
    am_sb = ab[0:64, 83776:83840]
    qt_sb = ab[0:64, 83840:83904]
    qtT_sb = ab[0:128, 83904:83968]
    k_tm = ab[0:64, 83968:84096]
    v_tm = ab[0:64, 84096:84224]
    ogf_sb = ab[0:64, 84224:84352]
    ogn = ab[0:128, 32768:49152]
    th = af[0:128, 0:512]
    sp = af[0:128, 512:1024]
    s_fm = af[0:128, 1024:1280]
    sig = af[0:128, 1280:1536]
    ep_fm = af[0:128, 1536:1792]
    en_fm = af[0:128, 1792:2048]
    ep_tm = af[0:64, 2048:2560]
    eend = af[0:1, 2560:3072]
    hk = ab[0:128, 84352:84864]
    hv = ab[0:64, 84864:85888]
    stilT_sb = ab[0:64, 85888:85952]
    tmp1 = af[0:128, 4096:4160]
    tmp2 = af[0:128, 4160:4288]
    ok_sb = af[0:64, 4288:4352]
    qv_sb = af[0:64, 4352:4416]
    cols = af[0:64, 4416:4432]
    thg = af[0:64, 4432:4560]
    sgb = af[0:64, 4560:4688]
    sqb = af[0:128, 4688:5200]
    rrow = af[0:1, 5200:5712]
    rrow2 = af[0:1, 5712:6224]

    lwg2 = S.dma("gpsimd", wg2.rearrange("p (t n) -> p t n", t=1),
                 aps["L0_attn_Wg2"].rearrange("(t p) n -> p t n", p=128))
    last_wo_mm = None
    last_proj_mm = None
    for b in range(DBG_GSA.get("bpc") or BPC):
        lxn = S.dma("sync", xn_b.rearrange("p (t c) -> p t c", t=8),
                    fm_view(aps["xn0T"])[:, :, b * SEQ:(b + 1) * SEQ],
                    waits=[last_wo_mm])
        for wname, dst, silu in (() if DBG_GSA["skip_qkv"] else (("L0_attn_Wq", qb, True),
                                 ("L0_attn_Wk", kb, True),
                                 ("L0_attn_Wv", vb, False))):
            for halfW in range(2):
                lw = S.dma("gpsimd", wslot.rearrange("p (t n) -> p t n", t=8),
                           aps[wname][:, halfW * 512:(halfW + 1) * 512]
                           .rearrange("(t p) n -> p t n", p=128),
                           waits=[last_proj_mm, last_wo_mm])
                prev_ev = {0: None, 1: None}
                for mt4 in range(4):
                    for grp in range(4):
                        bank = B.pb[grp % 2]
                        for kt in range(DT):
                            mm = S.op("tensor", lambda e_, mt4=mt4, grp=grp, kt=kt, bank=bank: e_.matmul(
                                bank[:, 0:512],
                                wslot[:, kt * 512 + mt4 * 128: kt * 512 + (mt4 + 1) * 128],
                                xn_b[:, kt * 2048 + grp * 512: kt * 2048 + (grp + 1) * 512],
                                start=(kt == 0), stop=(kt == DT - 1)),
                                waits=[lw, lxn, prev_ev[grp % 2]],
                                mark=(kt == DT - 1))
                        last_proj_mm = mm
                        o = (halfW * 4 + mt4) * 2048 + grp * 512
                        if silu and not DBG_GSA["plain_evac"]:
                            t1 = S.op("scalar", lambda e_, bank=bank: e_.activation(
                                th, bank[:, 0:512], AF.Tanh, scale=0.5),
                                waits=[mm], mark=True)
                            t2 = S.op("vector", lambda e_: e_.tensor_scalar(
                                sp, th, 0.5, 0.5, ALU.mult, ALU.add),
                                waits=[t1], mark=True)
                            ev = S.op("vector", lambda e_, o=o, bank=bank, dst=dst: e_.tensor_mul(
                                dst[:, o:o + 512], bank[:, 0:512], sp),
                                waits=[t2], mark=True)
                        else:
                            ev = S.op("vector", lambda e_, o=o, bank=bank, dst=dst: e_.tensor_copy(
                                dst[:, o:o + 512], bank[:, 0:512]), waits=[mm], mark=True)
                        prev_ev[grp % 2] = ev
        fg_list = (("L0_attn_Wf1", 128, t_f), ("L0_attn_Wg1", 128, t_g))
        if DBG_GSA.get("fg_ngrp") is not None:
            pass
        if DBG_GSA.get("fg_first_only"):
            fg_list = fg_list[:1]
        for wname, wcols, dst in (() if DBG_GSA["skip_fg"] else fg_list):
            lw = S.dma("gpsimd", wslot[:, 0:1024].rearrange("p (t n) -> p t n", t=8),
                       aps[wname].rearrange("(t p) n -> p t n", p=128),
                       waits=[last_proj_mm])
            prev_ev = {0: None, 1: None}
            for grp in range(DBG_GSA.get("fg_ngrp", 4)):
                bank = B.pb[grp % 2]
                for kt in range(DT):
                    mm = S.op("tensor", lambda e_, grp=grp, kt=kt, bank=bank: e_.matmul(
                        bank[:, 0:512],
                        wslot[:, kt * 128:(kt + 1) * 128],
                        xn_b[:, kt * 2048 + grp * 512: kt * 2048 + (grp + 1) * 512],
                        start=(kt == 0), stop=(kt == DT - 1)),
                        waits=[lw, prev_ev[grp % 2]], mark=(kt == DT - 1))
                last_proj_mm = mm
                ev = S.op("vector", lambda e_, grp=grp, bank=bank, dst=dst: e_.tensor_copy(
                    dst[:, grp * 512:(grp + 1) * 512], bank[:, 0:512]),
                    waits=[mm], mark=True)
                prev_ev[grp % 2] = ev
        if DBG_GSA["skip_fg"]:
            continue
        lw2 = S.dma("gpsimd", wslot[:, 0:512].rearrange("p (t n) -> p t n", t=1),
                    aps["L0_attn_Wf2"].rearrange("(t p) n -> p t n", p=128),
                    waits=[last_proj_mm])
        prev_ev = {0: None, 1: None}
        for mt in range(4 if not DBG_GSA["skip_ft"] else 0):
            for grp in range(4):
                bank = B.pb[grp % 2]
                mm = S.op("tensor", lambda e_, mt=mt, grp=grp, bank=bank: e_.matmul(
                    bank[:, 0:512], wslot[:, mt * 128:(mt + 1) * 128],
                    t_f[:, grp * 512:(grp + 1) * 512], start=True, stop=True),
                    waits=[lw2, prev_ev[grp % 2]], mark=True)
                last_proj_mm = mm
                ev = S.op("vector", lambda e_, mt=mt, grp=grp, bank=bank: e_.tensor_copy(
                    fT[:, mt * 2048 + grp * 512: mt * 2048 + (grp + 1) * 512],
                    bank[:, 0:512]), waits=[mm], mark=True)
                prev_ev[grp % 2] = ev
        if DBG_GSA["skip_memset"]:
            z1 = z2 = None
        else:
            z1 = S.op("gpsimd", lambda e_: e_.memset(hk, 0.0), waits=[("e", "vector", S.ecnt["vector"])])
            z2 = S.op("gpsimd", lambda e_: e_.memset(hv, 0.0))
        bank_free = {}
        vhk_prev = None
        for h in range(H):
            bank_free[("hk", h)] = z2
            bank_free[("hv", h)] = z2
        for ct in range(NCH if DBG_GSA["nch"] is None else DBG_GSA["nch"]):
            tnh = S.op("scalar", lambda e_, ct=ct: e_.activation(
                th[:, 0:256].rearrange("p (a c) -> p a c", a=4),
                fT.rearrange("p (a c) -> p a c", a=4)[:, :, ct * C:(ct + 1) * C],
                AF.Tanh, scale=0.5), waits=[("e", "vector", S.ecnt["vector"]),
                                            z1, z2], mark=True)
            vs = S.op("vector", lambda e_: e_.tensor_scalar(
                sig, th[:, 0:256], 0.5, 0.5, ALU.mult, ALU.add), waits=[tnh], mark=True)
            vsn = S.op("vector", lambda e_: e_.tensor_scalar(
                s_fm, th[:, 0:256], -0.5, 0.5, ALU.mult, ALU.add), mark=True)
            scans = []
            for mt in range(4):
                sc = S.op("vector", lambda e_, mt=mt: e_.tensor_tensor_scan(
                    ep_fm[:, mt * C:(mt + 1) * C], sig[:, mt * C:(mt + 1) * C],
                    sig[:, mt * C:(mt + 1) * C], 1.0, ALU.mult, ALU.bypass),
                    waits=[vs], mark=True)
                scans.append(sc)
            rec = S.op("vector", lambda e_: e_.reciprocal(en_fm, ep_fm),
                       waits=scans, mark=True)
            stf = S.op("vector", lambda e_: e_.tensor_mul(stil_fm, s_fm, en_fm),
                       waits=[rec, vsn], mark=True)
            for mt in range(4):
                tsl = B.tb[mt % 2][0:64, 0:128]
                tt = S.op("tensor", lambda e_, mt=mt, tsl=tsl: e_.transpose(
                    tsl, stil_fm[:, mt * C:(mt + 1) * C], B.ident_bf),
                    waits=[stf, bank_free.get(("tb", mt % 2))], mark=True)
                cv = S.op("vector", lambda e_, mt=mt, tsl=tsl: e_.tensor_copy(
                    stil_tm[:, mt * 128:(mt + 1) * 128], tsl),
                    waits=[tt], mark=True)
                bank_free[("tb", mt % 2)] = cv
                pbk = B.pb[4 + mt % 2]
                te = S.op("tensor", lambda e_, mt=mt, pbk=pbk: e_.transpose(
                    pbk[0:64, 0:128], ep_fm[:, mt * C:(mt + 1) * C],
                    B.ident_f), waits=scans + [bank_free.get(("pb", 4 + mt % 2))],
                    mark=True)
                ce = S.op("vector", lambda e_, mt=mt, pbk=pbk: e_.tensor_copy(
                    ep_tm[:, mt * 128:(mt + 1) * 128], pbk[0:64, 0:128]),
                    waits=[te], mark=True)
                bank_free[("pb", 4 + mt % 2)] = ce
            for mt in range(4):
                pbk = B.pb[4 + mt % 2]
                tz = S.op("tensor", lambda e_, mt=mt, pbk=pbk: e_.transpose(
                    pbk[0:1, 0:128], ep_fm[:, mt * C + C - 1: mt * C + C],
                    B.ident_f), waits=scans + [bank_free.get(("pb", 4 + mt % 2))],
                    mark=True)
                cz = S.op("vector", lambda e_, mt=mt, pbk=pbk: e_.tensor_copy(
                    eend[0:1, mt * 128:(mt + 1) * 128], pbk[0:1, 0:128]),
                    waits=[tz], mark=True)
                bank_free[("pb", 4 + mt % 2)] = cz
            dec_done = S.barrier_toks()
            for h in range(H if DBG_GSA["heads"] is None else DBG_GSA["heads"]):
                mt, r0 = h // 2, (h % 2) * 64
                col = mt * 128 + r0
                qc = qb[:, h * 2048 + ct * C: h * 2048 + (ct + 1) * C]
                kc = kb[:, h * 2048 + ct * C: h * 2048 + (ct + 1) * C]
                vc = vb[:, h * 2048 + ct * C: h * 2048 + (ct + 1) * C]
                st_tm = stil_tm[0:64, col:col + 64]
                ep_h = ep_tm[0:64, col:col + 64]
                eend_h = eend[0:1, col:col + 64]
                hk_h = hk[:, h * 64:(h + 1) * 64]
                hv_h = hv[0:64, h * 128:(h + 1) * 128]
                mmP = S.op("tensor", lambda e_, kc=kc, qc=qc: e_.matmul(
                    B.pb[0][0:64, 0:64], kc, qc, start=True, stop=True),
                    waits=[bank_free.get(("pb", 0))] + dec_done, mark=True)
                vP = S.op("vector", lambda e_: e_.tensor_mul(
                    pm_sb, B.pb[0][0:64, 0:64], B.triu64), waits=[mmP], mark=True)
                mmOa = S.op("tensor", lambda e_, st_tm=st_tm: e_.matmul(
                    B.pb[1][0:64, 0:64], pm_sb, st_tm, start=True, stop=False),
                    waits=[vP, bank_free.get(("pb", 1))], mark=True)
                mmOb = S.op("tensor", lambda e_, qc=qc, hk_h=hk_h: e_.matmul(
                    B.pb[1][0:64, 0:64], qc, hk_h, start=False, stop=True),
                    waits=[bank_free.get(("hk", h))], mark=True)
                vOk = S.op("vector", lambda e_, ep_h=ep_h: e_.tensor_mul(
                    ok_sb, B.pb[1][0:64, 0:64], ep_h), waits=[mmOb], mark=True)
                bank_free[("pb", 1)] = vOk
                vmx = S.op("vector", lambda e_: e_.reduce_max(
                    cols[:, 0:1], ok_sb, axis=AX.X, negate=True), mark=True)
                sex = S.op("scalar", lambda e_: e_.activation(
                    qv_sb, ok_sb, AF.Exp, bias=cols[:, 0:1],
                    accum_out=cols[:, 1:2]), waits=[vmx], mark=True)
                vrd = S.op("vector", lambda e_: e_.reciprocal(
                    cols[:, 2:3], cols[:, 1:2]), waits=[sex], mark=True)
                vqt = S.op("vector", lambda e_, ep_h=ep_h: e_.scalar_tensor_tensor(
                    qt_sb, qv_sb, cols[:, 2:3], ep_h, ALU.mult, ALU.mult),
                    waits=[vrd], mark=True)
                ttq = S.op("tensor", lambda e_: e_.transpose(
                    B.tb[0][0:64, 0:64], qt_sb, B.ident_bf[0:64, 0:64]),
                    waits=[vqt, bank_free.get(("tb", 0))], mark=True)
                cq = S.op("vector", lambda e_: e_.tensor_copy(
                    qtT_sb[0:64, 0:64], B.tb[0][0:64, 0:64]),
                    waits=[ttq], mark=True)
                bank_free[("tb", 0)] = cq
                ttk = S.op("tensor", lambda e_, kc=kc: e_.transpose(
                    B.tb[1][0:64, 0:128], kc, B.ident_bf),
                    waits=[bank_free.get(("tb", 1))], mark=True)
                ck = S.op("vector", lambda e_: e_.tensor_copy(
                    k_tm, B.tb[1][0:64, 0:128]), waits=[ttk], mark=True)
                ttv = S.op("tensor", lambda e_, vc=vc: e_.transpose(
                    B.tb[0][0:64, 128:256], vc, B.ident_bf),
                    waits=[cq], mark=True)
                cvv = S.op("vector", lambda e_: e_.tensor_copy(
                    v_tm, B.tb[0][0:64, 128:256]), waits=[ttv], mark=True)
                bank_free[("tb", 0)] = cvv
                bank_free[("tb", 1)] = ck
                tts = S.op("tensor", lambda e_, st_tm=st_tm: e_.transpose(
                    B.tb[0][0:64, 320:384], st_tm, B.ident_bf[0:64, 0:64]),
                    waits=[cq, bank_free.get(("tbs", 0))] + dec_done, mark=True)
                cts = S.op("vector", lambda e_: e_.tensor_copy(
                    stilT_sb, B.tb[0][0:64, 320:384]), waits=[tts], mark=True)
                bank_free[("tbs", 0)] = cts
                mmA = S.op("tensor", lambda e_: e_.matmul(
                    B.pb[0][0:64, 0:64], stilT_sb, qtT_sb[0:64, 0:64],
                    start=True, stop=True), waits=[vP, cq, cts], mark=True)
                bank_free[("pb", 0)] = mmA
                vA = S.op("vector", lambda e_: e_.tensor_mul(
                    am_sb, B.pb[0][0:64, 0:64], B.triu64), waits=[mmA], mark=True)
                bank_free[("pb", 0)] = vA
                mmVa = S.op("tensor", lambda e_: e_.matmul(
                    B.pb[4][0:64, 0:128], am_sb, v_tm, start=True, stop=False),
                    waits=[vA, cvv, bank_free.get(("pb", 4))], mark=True)
                mmVb = S.op("tensor", lambda e_, hv_h=hv_h: e_.matmul(
                    B.pb[4][0:64, 0:128], qtT_sb[0:64, 0:64], hv_h,
                    start=False, stop=True),
                    waits=[bank_free.get(("hv", h))], mark=True)
                mmHk = S.op("tensor", lambda e_, st_tm=st_tm: e_.matmul(
                    B.pb[2][0:128, 0:64], k_tm, st_tm, start=True, stop=True),
                    waits=[ck, bank_free.get(("pb", 2)), mmOb], mark=True)
                mmBc = S.op("tensor", lambda e_, eend_h=eend_h: e_.matmul(
                    B.pb[3][0:128, 0:64], B.ones_row, eend_h, start=True, stop=True),
                    waits=[bank_free.get(("pb", 3)), bank_free.get(("pbec", 3))] + dec_done,
                    mark=True)
                vt1 = S.op("vector", lambda e_, hk_h=hk_h: e_.tensor_add(
                    tmp1, B.pb[2][0:128, 0:64], hk_h), waits=[mmHk], mark=True)
                bank_free[("pb", 2)] = vt1
                vhk = S.op("vector", lambda e_, hk_h=hk_h: e_.tensor_mul(
                    hk_h, tmp1, B.pb[3][0:128, 0:64]), waits=[vt1, mmBc], mark=True)
                bank_free[("pb", 3)] = vhk
                bank_free[("hk", h)] = vhk
                vhk_prev = vhk
                mmHv = S.op("tensor", lambda e_, st_tm=st_tm: e_.matmul(
                    B.pb[2][0:64, 64:192], st_tm, v_tm, start=True, stop=True),
                    waits=[cvv, vt1, mmVb], mark=True)
                mmEc = S.op("tensor", lambda e_, eend_h=eend_h: e_.matmul(
                    B.pb[3][0:64, 64:65], eend_h, B.ones_row[0:1, 0:1],
                    start=True, stop=True),
                    waits=[vhk, bank_free.get(("pbec", 3))] + dec_done, mark=True)
                cec = S.op("vector", lambda e_: e_.tensor_copy(
                    cols[:, 3:4], B.pb[3][0:64, 64:65]), waits=[mmEc], mark=True)
                bank_free[("pbec", 3)] = cec
                vt2 = S.op("vector", lambda e_, hv_h=hv_h: e_.tensor_add(
                    tmp2[0:64, 0:128], B.pb[2][0:64, 64:192], hv_h),
                    waits=[mmHv], mark=True)
                bank_free[("pb", 2)] = vt2
                vhv = S.op("vector", lambda e_, hv_h=hv_h: e_.tensor_scalar(
                    hv_h, tmp2[0:64, 0:128], cols[:, 3:4], None, ALU.mult),
                    waits=[vt2, cec], mark=True)
                bank_free[("hv", h)] = vhv
                mmG = S.op("tensor", lambda e_, ct=ct, h=h: e_.matmul(
                    B.pb[5][0:64, 0:128], t_g[:, ct * C:(ct + 1) * C],
                    wg2[:, h * 128:(h + 1) * 128], start=True, stop=True),
                    waits=[lwg2, bank_free.get(("pb", 5))], mark=True)
                sth = S.op("scalar", lambda e_: e_.activation(
                    thg, B.pb[5][0:64, 0:128], AF.Tanh, scale=0.5),
                    waits=[mmG], mark=True)
                bank_free[("pb", 5)] = sth
                vsg = S.op("vector", lambda e_: e_.tensor_scalar(
                    sgb, thg, 0.5, 0.5, ALU.mult, ALU.add), waits=[sth], mark=True)
                vog = S.op("vector", lambda e_: e_.tensor_mul(
                    ogf_sb, B.pb[4][0:64, 0:128], sgb), waits=[vsg, mmVb], mark=True)
                bank_free[("pb", 4)] = vog
                tto = S.op("tensor", lambda e_: e_.transpose(
                    B.tb[1][0:128, 256:320], ogf_sb, B.ident_bf[0:64, 0:64]),
                    waits=[vog, bank_free.get(("tb", 1))], mark=True)
                cog = S.op("vector", lambda e_, h=h, ct=ct: e_.tensor_copy(
                    og[:, h * 2048 + ct * C:(h * 2048 + (ct + 1) * C)],
                    B.tb[1][0:128, 256:320]), waits=[tto], mark=True)
                bank_free[("tb", 1)] = cog
        for h in range(H if not DBG_GSA["skip_post"] else 0):
            for grp in range(4):
                o = h * 2048 + grp * 512
                sq1 = S.op("scalar", lambda e_, o=o: e_.activation(
                    sqb, og[:, o:o + 512], AF.Square),
                    waits=[("e", "vector", S.ecnt["vector"])], mark=True)
                mmS = S.op("tensor", lambda e_: e_.matmul(
                    B.pb[0][0:1, 0:512], B.ones_col, sqb, start=True, stop=True),
                    waits=[sq1, bank_free.get(("pb", 0))], mark=True)
                ssd = S.op("scalar", lambda e_: e_.activation(
                    rrow, B.pb[0][0:1, 0:512], AF.Ln, bias=B.cc[0:1, 3:4]),
                    waits=[mmS], mark=True)
                bank_free[("pb", 0)] = ssd
                srr = S.op("scalar", lambda e_: e_.activation(
                    rrow2, rrow, AF.Exp, scale=-0.5), mark=True)
                mmB = S.op("tensor", lambda e_: e_.matmul(
                    B.pb[1][0:128, 0:512], B.ones_row, rrow2, start=True, stop=True),
                    waits=[srr, bank_free.get(("pb", 1))], mark=True)
                von = S.op("vector", lambda e_, o=o: e_.scalar_tensor_tensor(
                    ogn[:, o:o + 512], og[:, o:o + 512], float(np.sqrt(128.0)),
                    B.pb[1][0:128, 0:512], ALU.mult, ALU.mult),
                    waits=[mmB], mark=True)
                bank_free[("pb", 1)] = von
        for halfW in range(2 if not DBG_GSA["skip_wo"] else 0):
            lwo = S.dma("gpsimd", wslot.rearrange("p (t n) -> p t n", t=4),
                        aps["wo_sc_bf"][halfW * 512:(halfW + 1) * 512, :]
                        .rearrange("(t p) n -> p t n", p=128),
                        waits=[last_proj_mm, last_wo_mm])
            prev_ao = {0: None, 1: None}
            prev_aodma = {0: None, 1: None}
            for dout in range(DT):
                for grp in range(4):
                    par = (dout * 4 + grp) % 2
                    bank = B.pb[2 + par]
                    for kt in range(4):
                        mm = S.op("tensor", lambda e_, dout=dout, grp=grp, kt=kt, halfW=halfW, bank=bank: e_.matmul(
                            bank[:, 0:512],
                            wslot[:, kt * 1024 + dout * 128: kt * 1024 + (dout + 1) * 128],
                            ogn[:, (halfW * 4 + kt) * 2048 + grp * 512:
                                (halfW * 4 + kt) * 2048 + (grp + 1) * 512],
                            start=(kt == 0), stop=(kt == 3)),
                            waits=[lwo, prev_ao[par],
                                   ("e", "vector", S.ecnt["vector"])],
                            mark=(kt == 3))
                    last_wo_mm = mm
                    abuf = th if par == 0 else sp
                    aev = S.op("vector", lambda e_, par=par, bank=bank, abuf=abuf: e_.tensor_copy(
                        abuf, bank[:, 0:512]), waits=[mm, prev_aodma[par]], mark=True)
                    prev_ao[par] = aev
                    kw = {} if halfW == 0 else {"accum_op": ALU.add}
                    ad = S.dma("gpsimd", fm_view(aps["a0T"])[:, dout,
                               b * SEQ + grp * 512: b * SEQ + (grp + 1) * 512],
                               abuf, waits=[aev], **kw)
                    prev_aodma[par] = ad
    S.full_barrier()


def emit_l1attn(nc, S, B, aps):
    af, ab = B.af, B.ab
    xn_b = ab[0:128, 0:16384]
    qb = ab[0:128, 16384:24576]
    kb = ab[0:128, 24576:32768]
    vb = ab[0:128, 32768:40960]
    og = ab[0:128, 40960:49152]
    wslot = ab[0:128, 49152:53248]
    probs_e = ab[0:128, 53248:55296]
    probs_n = ab[0:128, 55296:57344]
    pT = [ab[0:128, 57344 + i * 128: 57344 + (i + 1) * 128] for i in range(2)]
    v_tm = ab[0:128, 57600:58624]
    scores = af[0:128, 0:2048]
    st = af[0:128, 2048:2064]
    ao = [af[0:128, 2064 + i * 512: 2064 + (i + 1) * 512] for i in range(2)]

    last_proj_mm = None
    last_wo_mm = None
    for b in range(BPC):
        lxn = S.dma("sync", xn_b.rearrange("p (t c) -> p t c", t=8),
                    fm_view(aps["xn1T"])[:, :, b * SEQ:(b + 1) * SEQ],
                    waits=[last_proj_mm])
        for half in range(2):
            for wname, dst, scale in (("wqT_bf", qb, 0.125),
                                      ("wkT_bf", kb, None),
                                      ("wvT_bf", vb, None)):
                lw = S.dma("gpsimd", wslot[:, 0:4096].rearrange("p (t n) -> p t n", t=8),
                           aps[wname][:, half * 512:(half + 1) * 512]
                           .rearrange("(t p) n -> p t n", p=128),
                           waits=[last_proj_mm, last_wo_mm])
                prev_ev = {0: None, 1: None}
                for mt4 in range(4):
                    for grp in range(4):
                        bank = B.pb[grp % 2]
                        for kt in range(DT):
                            mm = S.op("tensor", lambda e_, mt4=mt4, grp=grp, kt=kt, bank=bank: e_.matmul(
                                bank[:, 0:512],
                                wslot[:, kt * 512 + mt4 * 128: kt * 512 + (mt4 + 1) * 128],
                                xn_b[:, kt * 2048 + grp * 512: kt * 2048 + (grp + 1) * 512],
                                start=(kt == 0), stop=(kt == DT - 1)),
                                waits=[lw, lxn, prev_ev[grp % 2]],
                                mark=(kt == DT - 1))
                        last_proj_mm = mm
                        o = mt4 * 2048 + grp * 512
                        if scale is None:
                            ev = S.op("vector", lambda e_, o=o, bank=bank, dst=dst: e_.tensor_copy(
                                dst[:, o:o + 512], bank[:, 0:512]), waits=[mm], mark=True)
                        else:
                            ev = S.op("vector", lambda e_, o=o, bank=bank, dst=dst, scale=scale: e_.tensor_scalar(
                                dst[:, o:o + 512], bank[:, 0:512], scale, None, ALU.mult),
                                waits=[mm], mark=True)
                        prev_ev[grp % 2] = ev
            if b == 0 and half == 0 and "dbgq" in aps:
                S.full_barrier()
                S.dma("sync", aps["dbgq"], qb)
                S.dma("sync", aps["dbgk"], kb)
                S.dma("sync", aps["dbgv"], vb)
            for h8 in range(8):
                hp, row0 = h8 // 2, (h8 % 2) * 64
                qh = qb[row0:row0 + 64, hp * 2048:(hp + 1) * 2048]
                kh = kb[row0:row0 + 64, hp * 2048:(hp + 1) * 2048]
                vh = vb[row0:row0 + 64, hp * 2048:(hp + 1) * 2048]
                prev_vc = {0: None, 1: None}
                vts = []
                idv = B.ident_bf[row0:row0 + 64, row0:row0 + 64]
                for kt2 in range(16):
                    tsl = B.tb[kt2 % 2][:, 0:64]
                    tv = S.op("tensor", lambda e_, kt2=kt2, tsl=tsl, vh=vh, idv=idv: e_.transpose(
                        tsl, vh[:, kt2 * 128:(kt2 + 1) * 128], idv),
                        waits=[prev_vc[kt2 % 2], ("e", "vector", S.ecnt["vector"])],
                        mark=True)
                    cv = S.op("vector", lambda e_, kt2=kt2, tsl=tsl: e_.tensor_copy(
                        v_tm[:, kt2 * 64:(kt2 + 1) * 64], tsl), waits=[tv], mark=True)
                    prev_vc[kt2 % 2] = cv
                    vts.append(cv)
                prev_avev = None
                for qt in range(16):
                    dt_tile = qt if qt < 8 else qt - 8
                    klen = (dt_tile + 1) * 128
                    ngrp = (klen + 511) // 512
                    evs = []
                    for g2 in range(ngrp):
                        n = min(512, klen - g2 * 512)
                        bank = B.pb[g2 % 2]
                        mm = S.op("tensor", lambda e_, qt=qt, g2=g2, n=n, bank=bank, qh=qh, kh=kh: e_.matmul(
                            bank[:, 0:n], qh[:, qt * 128:(qt + 1) * 128],
                            kh[:, g2 * 512: g2 * 512 + n], start=True, stop=True),
                            waits=[evs[-2] if len(evs) >= 2 else None],
                            mark=True)
                        if n > 128 or g2 < ngrp - 1:
                            ncopy = n if g2 < ngrp - 1 else n - 128
                            ev = S.op("vector", lambda e_, g2=g2, ncopy=ncopy, bank=bank: e_.tensor_copy(
                                scores[:, g2 * 512: g2 * 512 + ncopy], bank[:, 0:ncopy]),
                                waits=[mm], mark=True)
                            evs.append(ev)
                        if g2 == ngrp - 1:
                            ev = S.op("vector", lambda e_, dt_tile=dt_tile, g2=g2, n=n, bank=bank: e_.tensor_add(
                                scores[:, dt_tile * 128: dt_tile * 128 + 128],
                                bank[:, n - 128:n], B.causal),
                                waits=[mm], mark=True)
                            evs.append(ev)
                    mx = S.op("vector", lambda e_, klen=klen: e_.reduce_max(
                        st[:, 0:1], scores[:, 0:klen], axis=AX.X, negate=True),
                        waits=evs, mark=True)
                    ex = S.op("scalar", lambda e_, klen=klen: e_.activation(
                        probs_e[:, 0:klen], scores[:, 0:klen], AF.Exp,
                        bias=st[:, 0:1], accum_out=st[:, 1:2]), waits=[mx], mark=True)
                    rd = S.op("vector", lambda e_: e_.reciprocal(st[:, 2:3], st[:, 1:2]),
                              waits=[ex], mark=True)
                    nm = S.op("vector", lambda e_, klen=klen: e_.tensor_scalar(
                        probs_n[:, 0:klen], probs_e[:, 0:klen], st[:, 2:3], None,
                        ALU.mult), waits=[rd], mark=True)
                    prev_pc = {0: None, 1: None}
                    for kt2 in range(dt_tile + 1):
                        tsl = B.tb[kt2 % 2][:, 64:192]
                        tp = S.op("tensor", lambda e_, kt2=kt2, tsl=tsl: e_.transpose(
                            tsl, probs_n[:, kt2 * 128:(kt2 + 1) * 128], B.ident_bf),
                            waits=[nm, prev_pc[kt2 % 2]], mark=True)
                        cp = S.op("vector", lambda e_, kt2=kt2, tsl=tsl: e_.tensor_copy(
                            pT[kt2 % 2], tsl), waits=[tp], mark=True)
                        prev_pc[kt2 % 2] = cp
                        mm = S.op("tensor", lambda e_, kt2=kt2, dt_tile=dt_tile: e_.matmul(
                            B.pb[4][0:64, 0:128], v_tm[:, kt2 * 64:(kt2 + 1) * 64],
                            pT[kt2 % 2], start=(kt2 == 0), stop=(kt2 == dt_tile)),
                            waits=[cp, vts[kt2], prev_avev], mark=(kt2 == dt_tile))
                    oev = S.op("vector", lambda e_, qt=qt, hp=hp, row0=row0: e_.tensor_copy(
                        og[row0:row0 + 64, hp * 2048 + qt * 128: hp * 2048 + (qt + 1) * 128],
                        B.pb[4][0:64, 0:128]), waits=[mm], mark=True)
                    prev_avev = oev
            if b == 0 and "dbgog" in aps:
                S.full_barrier()
                S.dma("sync", aps["dbgog" if half == 0 else "dbgog2"], og)
            lwo = S.dma("gpsimd", wslot[:, 0:4096].rearrange("p (t n) -> p t n", t=4),
                        aps["woT_bf"][half * 512:(half + 1) * 512, :]
                        .rearrange("(t p) n -> p t n", p=128),
                        waits=[last_proj_mm, last_wo_mm])
            prev_ao = {0: None, 1: None}
            prev_aodma = {0: None, 1: None}
            for dout in range(DT):
                for grp in range(4):
                    bank = B.pb[2 + (dout * 4 + grp) % 2]
                    for kt in range(4):
                        mm = S.op("tensor", lambda e_, dout=dout, grp=grp, kt=kt, bank=bank: e_.matmul(
                            bank[:, 0:512],
                            wslot[:, kt * 1024 + dout * 128: kt * 1024 + (dout + 1) * 128],
                            og[:, kt * 2048 + grp * 512: kt * 2048 + (grp + 1) * 512],
                            start=(kt == 0), stop=(kt == 3)),
                            waits=[lwo, prev_ao[(dout * 4 + grp) % 2],
                                   ("e", "vector", S.ecnt["vector"])],
                            mark=(kt == 3))
                    last_wo_mm = mm
                    par = (dout * 4 + grp) % 2
                    aev = S.op("vector", lambda e_, par=par, bank=bank: e_.tensor_copy(
                        ao[par], bank[:, 0:512]), waits=[mm, prev_aodma[par]], mark=True)
                    prev_ao[par] = aev
                    kw = {} if half == 0 else {"accum_op": ALU.add}
                    ad = S.dma("gpsimd", fm_view(aps["a1T"])[:, dout,
                               b * SEQ + grp * 512: b * SEQ + (grp + 1) * 512],
                               ao[par], waits=[aev], **kw)
                    prev_aodma[par] = ad
    S.full_barrier()


def emit_l0moe(nc, S, B, aps):
    af, ab = B.af, B.ab
    w1sb = ab[0:128, 0:32768]
    w2sb = ab[0:128, 32768:65536]
    hg = ab[0:128, 65536:81920]
    xng = ab[0:128, 81920:86016]
    fo = [af[0:128, 0:512], af[0:128, 512:1024]]
    ssc = af[0:128, 1024:1536]
    srow = af[0:1, 1536:5632]
    b1c = af[0:128, 5632:5664]
    b2c = af[0:128, 5664:5672]

    last_mm1 = None
    last_mm2 = None
    last_silu = None
    last_fev = None
    last_bcast = None
    for e in range(8):
        lw1 = S.dma("gpsimd", w1sb.rearrange("p (t n) -> p t n", t=8),
                    aps["L0_ffn_W1"][e].rearrange("(t p) n -> p t n", p=128),
                    waits=[last_mm1])
        lw2 = S.dma("gpsimd", w2sb.rearrange("p (t n) -> p t n", t=32),
                    aps["L0_ffn_W2"][e].rearrange("(t p) n -> p t n", p=128),
                    waits=[last_mm2])
        lb1 = S.dma("sync", b1c, aps["b1c"][e], waits=[last_silu])
        lb2 = S.dma("sync", b2c, aps["b2c"][e], waits=[last_fev])
        lsr = S.dma("sync", srow, aps["selsc"][e:e + 1, :], waits=[last_bcast])
        prev_silu = {0: None, 1: None}
        prev_fev = {0: None, 1: None}
        prev_fdma = {0: None, 1: None}
        prev_cssc = None
        for blk in range(8):
            c0 = blk * 512
            lx = S.dma("sync" if blk % 2 == 0 else "scalar",
                       xng.rearrange("p (t c) -> p t c", t=8),
                       fm_view(aps["xn2T"])[:, :, c0:c0 + 512],
                       waits=[last_mm1])
            mbc = S.op("tensor", lambda e_, c0=c0: e_.matmul(
                B.pb[4][0:128, 0:512], B.ones_row, srow[0:1, c0:c0 + 512],
                start=True, stop=True), waits=[lsr, prev_cssc], mark=True)
            last_bcast = mbc
            cssc = S.op("vector", lambda e_: e_.tensor_copy(ssc, B.pb[4][0:128, 0:512]),
                        waits=[mbc], mark=True)
            prev_cssc = cssc
            silu_toks = []
            for ft in range(32):
                bank = B.pb[ft % 2]
                for kt in range(DT):
                    mm = S.op("tensor", lambda e_, ft=ft, kt=kt, bank=bank: e_.matmul(
                        bank[:, 0:512],
                        w1sb[:, kt * 4096 + ft * 128: kt * 4096 + (ft + 1) * 128],
                        xng[:, kt * 512:(kt + 1) * 512],
                        start=(kt == 0), stop=(kt == DT - 1)),
                        waits=[lw1, lx, prev_silu[ft % 2]],
                        mark=(kt == DT - 1))
                last_mm1 = mm
                sil = S.op("scalar", lambda e_, ft=ft, bank=bank: e_.activation(
                    hg[:, ft * 512:(ft + 1) * 512], bank[:, 0:512], AF.Silu,
                    bias=b1c[:, ft:ft + 1]), waits=[mm, lb1], mark=True)
                prev_silu[ft % 2] = sil
                silu_toks.append(sil)
                last_silu = sil
            for ot in range(DT):
                bank = B.pb[2 + ot % 2]
                for kt in range(32):
                    mm = S.op("tensor", lambda e_, ot=ot, kt=kt, bank=bank: e_.matmul(
                        bank[:, 0:512],
                        w2sb[:, kt * 1024 + ot * 128: kt * 1024 + (ot + 1) * 128],
                        hg[:, kt * 512:(kt + 1) * 512],
                        start=(kt == 0), stop=(kt == 31)),
                        waits=[lw2, silu_toks[31], prev_fev[ot % 2]],
                        mark=(kt == 31))
                last_mm2 = mm
                fev = S.op("vector", lambda e_, ot=ot, bank=bank: e_.scalar_tensor_tensor(
                    fo[ot % 2], bank[:, 0:512], b2c[:, ot:ot + 1], ssc,
                    ALU.add, ALU.mult),
                    waits=[mm, lb2, cssc, prev_fdma[ot % 2]], mark=True)
                last_fev = fev
                prev_fev[ot % 2] = fev
                kw = {"accum_op": ALU.add} if e > 0 else {}
                fd = S.dma("gpsimd", fm_view(aps["f0T"])[:, ot, c0:c0 + 512],
                           fo[ot % 2], waits=[fev], **kw)
                prev_fdma[ot % 2] = fd
    S.full_barrier()


def emit_resln(nc, S, B, aps, xin, ain, fin, xout, xnout):
    af, ab = B.af, B.ab
    GT = 256
    xa = af[0:128, 0:2048]
    xb = af[0:128, 2048:4096]
    xd = af[0:128, 4096:6144]
    xn1 = ab[0:128, 0:2048]
    r = lambda i: af[0:1, 6144 + i * 256: 6144 + (i + 1) * 256]
    for g in range(T // GT):
        c0 = g * GT
        la = S.dma("sync", xa.rearrange("p (t c) -> p t c", t=8),
                   fm_view(aps[xin])[:, :, c0:c0 + GT])
        lb = S.dma("scalar", xb.rearrange("p (t c) -> p t c", t=8),
                   fm_view(aps[ain])[:, :, c0:c0 + GT])
        adds1 = []
        for kt in range(DT):
            o = kt * GT
            a1 = S.op("vector", lambda e, o=o: e.tensor_add(
                xd[:, o:o + GT], xa[:, o:o + GT], xb[:, o:o + GT]),
                waits=[la, lb], mark=True)
            adds1.append(a1)
        lc = S.dma("gpsimd", xb.rearrange("p (t c) -> p t c", t=8),
                   fm_view(aps[fin])[:, :, c0:c0 + GT], waits=adds1)
        adds = []
        for kt in range(DT):
            o = kt * GT
            a2 = S.op("vector", lambda e, o=o: e.tensor_add(
                xa[:, o:o + GT], xd[:, o:o + GT], xb[:, o:o + GT]),
                waits=[lc], mark=True)
            adds.append(a2)
        for kt in range(DT):
            o = kt * GT
            sqk = S.op("scalar", lambda e, o=o: e.activation(
                xb[:, o:o + GT], xa[:, o:o + GT], AF.Square),
                waits=[adds[kt]], mark=True)
            S.op("tensor", lambda e, o=o, kt=kt: e.matmul(
                B.pb[2][0:1, 0:GT], B.ones_col, xa[:, o:o + GT],
                start=(kt == 0), stop=(kt == DT - 1)), waits=[adds[kt]])
            mS2 = S.op("tensor", lambda e, o=o, kt=kt: e.matmul(
                B.pb[3][0:1, 0:GT], B.ones_col, xb[:, o:o + GT],
                start=(kt == 0), stop=(kt == DT - 1)), waits=[sqk],
                mark=(kt == DT - 1))
        S.op("vector", lambda e: e.tensor_scalar(r(0), B.pb[2][0:1, 0:GT],
                                                 1.0 / D, None, ALU.mult), waits=[mS2])
        S.op("vector", lambda e: e.tensor_mul(r(1), r(0), r(0)))
        S.op("vector", lambda e: e.tensor_scalar(r(2), B.pb[3][0:1, 0:GT],
                                                 1.0 / D, None, ALU.mult))
        vv = S.op("vector", lambda e: e.tensor_sub(r(3), r(2), r(1)), mark=True)
        ln = S.op("scalar", lambda e: e.activation(r(2), r(3), AF.Ln,
                                                   bias=B.cc[0:1, 0:1]),
                  waits=[vv], mark=True)
        ist = S.op("scalar", lambda e: e.activation(r(4), r(2), AF.Exp,
                                                    scale=-0.5), mark=True)
        mb = S.op("tensor", lambda e: e.matmul(
            B.pb[4][0:128, 0:GT], B.ones_row, r(0), start=True, stop=True),
            waits=[vv], mark=True)
        ib = S.op("tensor", lambda e: e.matmul(
            B.pb[5][0:128, 0:GT], B.ones_row, r(4), start=True, stop=True),
            waits=[ist], mark=True)
        outs = []
        for kt in range(DT):
            o = kt * GT
            S.op("vector", lambda e, o=o: e.tensor_sub(
                xd[:, o:o + GT], xa[:, o:o + GT], B.pb[4][0:128, 0:GT]),
                waits=[mb])
            nx = S.op("vector", lambda e, o=o: e.tensor_mul(
                xn1[:, o:o + GT], xd[:, o:o + GT], B.pb[5][0:128, 0:GT]),
                waits=[ib], mark=True)
            outs.append(nx)
        S.dma("sync", fm_view(aps[xout])[:, :, c0:c0 + GT],
              xa.rearrange("p (t c) -> p t c", t=8), waits=adds)
        S.dma("gpsimd", fm_view(aps[xnout])[:, :, c0:c0 + GT],
              xn1.rearrange("p (t c) -> p t c", t=8), waits=outs)
        S.full_barrier()


def emit_egress(nc, S, B, aps):
    af = B.af
    GT = 256
    o_rows = aps["out"].rearrange("b n d -> (b n) d")
    xa = af[0:128, 0:2048]
    xb = af[0:128, 2048:4096]
    xd = af[0:128, 4096:6144]
    otile = af[0:128, 6144:7168]
    for g in range(T // GT):
        c0 = g * GT
        la = S.dma("sync", xa.rearrange("p (t c) -> p t c", t=8),
                   fm_view(aps["x1T"])[:, :, c0:c0 + GT])
        lb = S.dma("scalar", xb.rearrange("p (t c) -> p t c", t=8),
                   fm_view(aps["a1T"])[:, :, c0:c0 + GT])
        adds1 = []
        for kt in range(DT):
            o = kt * GT
            a1 = S.op("vector", lambda e, o=o: e.tensor_add(
                xd[:, o:o + GT], xa[:, o:o + GT], xb[:, o:o + GT]),
                waits=[la, lb], mark=True)
            adds1.append(a1)
        lc = S.dma("gpsimd", xb.rearrange("p (t c) -> p t c", t=8),
                   fm_view(aps["f1T"])[:, :, c0:c0 + GT], waits=adds1)
        adds = []
        for kt in range(DT):
            o = kt * GT
            a2 = S.op("vector", lambda e, o=o: e.tensor_add(
                xa[:, o:o + GT], xd[:, o:o + GT], xb[:, o:o + GT]),
                waits=[lc], mark=True)
            adds.append(a2)
        prev_cp = {0: None, 1: None}
        for s in range(GT // 128):
            cps = []
            for kt in range(DT):
                pslot = B.pb[(kt % 2) + 2][:, 0:128]
                t1 = S.op("tensor", lambda e, kt=kt, s=s, pslot=pslot: e.transpose(
                    pslot, xa[:, kt * GT + s * 128: kt * GT + s * 128 + 128],
                    B.ident_f), waits=[adds[kt], prev_cp[kt % 2]], mark=True)
                c1 = S.op("vector", lambda e, kt=kt, pslot=pslot: e.tensor_copy(
                    otile[:, kt * 128:(kt + 1) * 128], pslot),
                    waits=[t1], mark=True)
                prev_cp[kt % 2] = c1
                cps.append(c1)
            dto = S.dma("sync", o_rows[(g * 2 + s) * 128:(g * 2 + s) * 128 + 128, :],
                        otile, waits=cps)
            prev_cp = {0: dto, 1: dto}
        S.full_barrier()


def emit_l1ffn(nc, S, B, aps):
    af, ab = B.af, B.ab
    GT = 512
    u1 = ab[0:128, 0:2048]
    v1 = ab[0:128, 2048:10240]
    u2 = ab[0:128, 10240:18432]
    v2 = ab[0:128, 18432:20480]
    w0 = S.dma("gpsimd", u1.rearrange("p (t n) -> p t n", t=8),
               aps["L1_ffn_U1"].rearrange("(t p) n -> p t n", p=128))
    w1 = S.dma("gpsimd", v1.rearrange("p (t n) -> p t n", t=2),
               aps["L1_ffn_V1"].rearrange("(t p) n -> p t n", p=128))
    w2 = S.dma("gpsimd", u2.rearrange("p (t n) -> p t n", t=32),
               aps["L1_ffn_U2"].rearrange("(t p) n -> p t n", p=128))
    w3 = S.dma("gpsimd", v2.rearrange("p (t n) -> p t n", t=2),
               aps["L1_ffn_V2"].rearrange("(t p) n -> p t n", p=128))
    b1c = af[0:128, 0:32]
    b2c = af[0:128, 32:40]
    wb1 = S.dma("sync", b1c, aps["l1b1c"])
    wb2 = S.dma("sync", b2c, aps["l1b2c"])
    xng = ab[0:128, 20480:24576]
    y1 = ab[0:128, 24576:25600]
    hg = ab[0:128, 25600:41984]
    qg = ab[0:128, 41984:43008]
    t0 = af[0:128, 64:576]
    sqc = af[0:128, 576:1088]
    cub = af[0:128, 1088:1600]
    inn = af[0:128, 1600:2112]
    th = af[0:128, 2112:2624]
    tt_ = af[0:128, 2624:3136]
    fo = af[0:128, 3136:3648]
    S.full_barrier()
    C1 = 0.044715
    C2 = 0.7978845608028654
    for g in range(T // GT):
        c0 = g * GT
        lx = S.dma("sync", xng.rearrange("p (t c) -> p t c", t=8),
                   fm_view(aps["xn1T"])[:, :, c0:c0 + GT])
        for mt in range(2):
            for kt in range(DT):
                mm = S.op("tensor", lambda e, mt=mt, kt=kt: e.matmul(
                    B.pb[0][:, 0:GT], u1[:, kt * 256 + mt * 128: kt * 256 + mt * 128 + 128],
                    xng[:, kt * GT:(kt + 1) * GT],
                    start=(kt == 0), stop=(kt == DT - 1)),
                    waits=[lx, w0, ("e", "vector", S.ecnt["vector"])],
                    mark=(kt == DT - 1))
            S.op("vector", lambda e, mt=mt: e.tensor_copy(
                y1[:, mt * GT:(mt + 1) * GT], B.pb[0][:, 0:GT]),
                waits=[mm], mark=True)
        hts = []
        for ft in range(32):
            for kt in range(2):
                mm = S.op("tensor", lambda e, ft=ft, kt=kt: e.matmul(
                    B.pb[1][:, 0:GT], v1[:, kt * 4096 + ft * 128: kt * 4096 + ft * 128 + 128],
                    y1[:, kt * GT:(kt + 1) * GT],
                    start=(kt == 0), stop=(kt == 1)),
                    waits=[w1, ("e", "vector", S.ecnt["vector"])],
                    mark=(kt == 1))
            bc = S.op("vector", lambda e, ft=ft: e.tensor_scalar(
                t0, B.pb[1][:, 0:GT], b1c[:, ft:ft + 1], None, ALU.add),
                waits=[mm, wb1], mark=True)
            S.op("scalar", lambda e: e.activation(sqc, t0, AF.Square))
            S.op("vector", lambda e: e.tensor_mul(cub, sqc, t0),
                 waits=[("e", "scalar", S.ecnt["scalar"])])
            S.op("vector", lambda e: e.scalar_tensor_tensor(
                inn, cub, C1, t0, ALU.mult, ALU.add))
            tha = S.op("scalar", lambda e: e.activation(th, inn, AF.Tanh, scale=C2),
                       waits=[("e", "vector", S.ecnt["vector"])], mark=True)
            S.op("vector", lambda e: e.tensor_scalar(tt_, th, 0.5, 0.5,
                                                     ALU.mult, ALU.add), waits=[tha])
            hv = S.op("vector", lambda e, ft=ft: e.tensor_mul(
                hg[:, ft * GT:(ft + 1) * GT], t0, tt_), mark=True)
            hts.append(hv)
        for mt in range(2):
            for kt in range(32):
                mm = S.op("tensor", lambda e, mt=mt, kt=kt: e.matmul(
                    B.pb[2][:, 0:GT], u2[:, kt * 256 + mt * 128: kt * 256 + mt * 128 + 128],
                    hg[:, kt * GT:(kt + 1) * GT],
                    start=(kt == 0), stop=(kt == 31)),
                    waits=[w2, hts[kt], ("e", "vector", S.ecnt["vector"])],
                    mark=(kt == 31))
            S.op("vector", lambda e, mt=mt: e.tensor_copy(
                qg[:, mt * GT:(mt + 1) * GT], B.pb[2][:, 0:GT]),
                waits=[mm], mark=True)
        prev_fc = None
        prev_fd = None
        for ot in range(DT):
            for kt in range(2):
                mm = S.op("tensor", lambda e, ot=ot, kt=kt: e.matmul(
                    B.pb[3][:, 0:GT], v2[:, kt * 1024 + ot * 128: kt * 1024 + ot * 128 + 128],
                    qg[:, kt * GT:(kt + 1) * GT],
                    start=(kt == 0), stop=(kt == 1)),
                    waits=[w3, ("e", "vector", S.ecnt["vector"]), prev_fc],
                    mark=(kt == 1))
            fc = S.op("vector", lambda e, ot=ot: e.tensor_scalar(
                fo, B.pb[3][:, 0:GT], b2c[:, ot:ot + 1], None, ALU.add),
                waits=[mm, wb2, prev_fd], mark=True)
            prev_fc = fc
            prev_fd = S.dma("sync", fm_view(aps["f1T"])[:, ot, c0:c0 + GT], fo, waits=[fc])
        S.full_barrier()


def catalog():
    return {
        "x_in": ([BPC, SEQ, D], F32),
        "out": ([BPC, SEQ, D], F32),
        "wo_sc_bf": ([D, D], BF16),
        "wqT_bf": ([D, D], BF16),
        "wkT_bf": ([D, D], BF16),
        "wvT_bf": ([D, D], BF16),
        "woT_bf": ([D, D], BF16),
        "gate_wTf": ([D, 8], F32),
        "gsum": ([1, 8], F32),
        "b1c": ([8, 128, 32], F32),
        "b2c": ([8, 128, 8], F32),
        "l1b1c": ([128, 32], F32),
        "l1b2c": ([128, 8], F32),
        "rmsw_c": ([128, 8], F32),
        "dbg_nwc": ([128, 8], F32),
        "dbgq": ([128, 8192], BF16),
        "dbgk": ([128, 8192], BF16),
        "dbgv": ([128, 8192], BF16),
        "dbgog": ([128, 8192], BF16),
        "dbgog2": ([128, 8192], BF16),
        "x0T": ([D, T], F32),
        "xn0T": ([D, T], BF16),
        "xn2T": ([D, T], BF16),
        "selsc": ([16, T], F32),
        "a0T": ([D, T], F32),
        "f0T": ([D, T], F32),
        "x1T": ([D, T], F32),
        "xn1T": ([D, T], BF16),
        "a1T": ([D, T], F32),
        "f1T": ([D, T], F32),
    }


WEIGHT_SHAPES = {
    "L0_attn_Wq": [D, D], "L0_attn_Wk": [D, D], "L0_attn_Wv": [D, D],
    "L0_attn_Wf1": [D, 128], "L0_attn_Wf2": [128, 512],
    "L0_attn_Wg1": [D, 128], "L0_attn_Wg2": [128, D],
    "L0_attn_Wo": [D, D], "L0_attn_norm_w": [D],
    "L0_ffn_rms_w": [D], "L0_ffn_gate_w": [8, D],
    "L0_ffn_W1": [8, D, 4096], "L0_ffn_b1": [8, 4096],
    "L0_ffn_W2": [8, 4096, D], "L0_ffn_b2": [8, D],
    "L1_attn_Wq": [D, D], "L1_attn_Wk": [D, D], "L1_attn_Wv": [D, D],
    "L1_attn_Wo": [D, D],
    "L1_ffn_U1": [D, 256], "L1_ffn_V1": [256, 4096], "L1_ffn_b1": [4096],
    "L1_ffn_U2": [4096, 256], "L1_ffn_V2": [256, D], "L1_ffn_b2": [D],
}


def build_launch(phase_fns, ext_in, ext_out, weights_needed):
    nc = bass.Bass("TRN2", target_bir_lowering=False)
    cat = catalog()
    aps = {}
    for name in weights_needed:
        aps[name] = nc.dram_tensor(name, WEIGHT_SHAPES[name], F32,
                                   kind="ExternalInput").ap()
    for name in ext_in:
        sh, dt = cat[name]
        aps[name] = nc.dram_tensor(name, sh, dt, kind="ExternalInput").ap()
    for name in ext_out:
        sh, dt = cat[name]
        aps[name] = nc.dram_tensor(name, sh, dt, kind="ExternalOutput").ap()
    S = Sched(nc)
    B = Bufs(nc)
    emit_const(nc, S, B)
    S.full_barrier()
    for fn in phase_fns:
        fn(nc, S, B, aps)
        S.full_barrier()
    S.finalize()
    return nc


LAUNCHES = [
    ("prep", [emit_prep], [],
     ["wo_sc_bf", "gate_wTf", "gsum", "b1c", "b2c", "l1b1c", "l1b2c", "rmsw_c",
      "wqT_bf", "wkT_bf", "wvT_bf", "woT_bf"],
     ["L0_attn_norm_w", "L0_ffn_rms_w", "L0_ffn_b1", "L0_ffn_b2",
      "L1_ffn_b1", "L1_ffn_b2", "L0_ffn_gate_w", "L0_attn_Wo",
      "L1_attn_Wq", "L1_attn_Wk", "L1_attn_Wv", "L1_attn_Wo"]),
    ("ingest", [emit_ingest], ["x_in", "rmsw_c", "gate_wTf", "gsum"],
     ["x0T", "xn0T", "xn2T", "selsc"], []),
    ("l0attn", [emit_l0attn], ["xn0T", "wo_sc_bf"], ["a0T"],
     ["L0_attn_Wq", "L0_attn_Wk", "L0_attn_Wv", "L0_attn_Wf1", "L0_attn_Wf2",
      "L0_attn_Wg1", "L0_attn_Wg2"]),
    ("l1attn", [emit_l1attn], ["xn1T", "wqT_bf", "wkT_bf", "wvT_bf", "woT_bf"],
     ["a1T"], []),
    ("l0moe", [emit_l0moe], ["xn2T", "selsc", "b1c", "b2c"], ["f0T"],
     ["L0_ffn_W1", "L0_ffn_W2"]),
    ("resln", [lambda nc, S, B, aps: emit_resln(nc, S, B, aps, "x0T", "a0T",
                                                "f0T", "x1T", "xn1T")],
     ["x0T", "a0T", "f0T"], ["x1T", "xn1T"], []),
    ("l1ffn", [emit_l1ffn], ["xn1T", "l1b1c", "l1b2c"], ["f1T"],
     ["L1_ffn_U1", "L1_ffn_V1", "L1_ffn_U2", "L1_ffn_V2"]),
    ("egress", [emit_egress], ["x1T", "a1T", "f1T"], ["out"], []),
]


FUSED_ORDER = ["prep", "ingest", "l0attn", "l0moe", "resln", "l1attn",
               "l1ffn", "egress"]


def build_fused():
    nc = bass.Bass("TRN2", target_bir_lowering=False)
    cat = catalog()
    aps = {}
    for name, sh in WEIGHT_SHAPES.items():
        aps[name] = nc.dram_tensor(name, sh, F32, kind="ExternalInput").ap()
    sh, dt = cat["x_in"]
    aps["x_in"] = nc.dram_tensor("x_in", sh, dt, kind="ExternalInput").ap()
    sh, dt = cat["out"]
    aps["out"] = nc.dram_tensor("out", sh, dt, kind="ExternalOutput").ap()
    for name, (sh, dt) in cat.items():
        if name in ("x_in", "out") or name.startswith("dbg"):
            continue
        aps[name] = nc.dram_tensor(name, sh, dt, kind="Internal").ap()
    S = Sched(nc)
    B = Bufs(nc)
    emit_const(nc, S, B)
    S.full_barrier()
    by_name = {l[0]: l for l in LAUNCHES}
    for name in FUSED_ORDER:
        for fn in by_name[name][1]:
            fn(nc, S, B, aps)
            S.full_barrier()
    S.finalize()
    return nc


_FUSED_NC = None
_NCS = {}


def _get_nc(name):
    if name not in _NCS:
        for n, phases, ext_in, ext_out, weights in LAUNCHES:
            if n == name:
                _NCS[name] = (build_launch(phases, ext_in, ext_out, weights),
                              ext_in, ext_out, weights)
                break
    return _NCS[name]


def kernel(**inputs):
    global _FUSED_NC
    x = np.ascontiguousarray(np.asarray(inputs["L0_attn_x"]))
    weights_np = {k: np.ascontiguousarray(np.asarray(v))
                  for k, v in inputs.items() if k != "L0_attn_x"}
    if _FUSED_NC is None:
        _FUSED_NC = build_fused()
    in_maps = []
    for c in range(NCORES):
        im = dict(weights_np)
        im["x_in"] = np.ascontiguousarray(x[c * BPC:(c + 1) * BPC])
        in_maps.append(im)
    res = run_bass_kernel_spmd(_FUSED_NC, in_maps, core_ids=list(range(NCORES)))
    return np.concatenate([np.asarray(res.results[c]["out"])
                           for c in range(NCORES)], axis=0)


# version 05
# speedup vs baseline: 1.1772x; 1.0845x over previous
import numpy as np
import ml_dtypes
import concourse.bass as bass
import concourse.mybir as mybir
from concourse.bass_utils import run_bass_kernel_spmd

F32 = mybir.dt.float32
BF16 = mybir.dt.bfloat16
AF = mybir.ActivationFunctionType
ALU = mybir.AluOpType
AX = mybir.AxisListType

NCORES = 8
BPC = 2
SEQ = 2048
T = BPC * SEQ
D = 1024
DT = D // 128
NEG = -1.0e30

ENGINES = ("tensor", "vector", "scalar", "gpsimd", "sync")
DMA_QUEUES = ("sync", "scalar", "gpsimd")


class Sched:

    def __init__(self, nc):
        self.nc = nc
        self.plan = {e: [] for e in ENGINES}
        self.esem = {e: nc.alloc_semaphore(f"es_{e}") for e in ENGINES}
        self.dsem = {q: nc.alloc_semaphore(f"ds_{q}") for q in DMA_QUEUES}
        self.ecnt = {e: 0 for e in ENGINES}
        self.dcnt = {q: 0 for q in DMA_QUEUES}

    SELF_SERIAL = ("vector", "scalar", "gpsimd")

    def op(self, eng, fn, waits=(), mark=False):
        waits = [w for w in waits if w is not None]
        if eng in self.SELF_SERIAL:
            if self.ecnt[eng]:
                waits.append(("e", eng, self.ecnt[eng]))
            mark = True
        tok = None
        if mark:
            self.ecnt[eng] += 1
            tok = ("e", eng, self.ecnt[eng])
        self.plan[eng].append((fn, tuple(waits),
                               ("e", eng) if mark else None))
        return tok

    def dma(self, q, out, in_, waits=(), **kw):
        self.dcnt[q] += 16
        tok = ("d", q, self.dcnt[q])
        def fn(e, out=out, in_=in_, kw=kw):
            return e.dma_start(out=out, in_=in_, **kw)
        self.plan[q].append((fn, tuple(w for w in waits if w is not None),
                             ("d", q)))
        return tok

    def barrier_toks(self):
        toks = []
        for e in ENGINES:
            if self.ecnt[e]:
                toks.append(("e", e, self.ecnt[e]))
        for q in DMA_QUEUES:
            if self.dcnt[q]:
                toks.append(("d", q, self.dcnt[q]))
        return toks

    def full_barrier(self):
        toks = self.barrier_toks()
        for e in ENGINES:
            self.plan[e].append((None, tuple(toks), None))

    def finalize(self):
        nc = self.nc
        with nc.Block() as block:
            for eng_name in ENGINES:
                def run(eng, eng_name=eng_name):
                    hwm = {}
                    for fn, waits, markkind in self.plan[eng_name]:
                        for kind, name, val in waits:
                            if (kind == "e" and name == eng_name
                                    and eng_name not in self.SELF_SERIAL):
                                continue
                            if val <= 0:
                                continue
                            sem = self.esem[name] if kind == "e" else self.dsem[name]
                            key = (kind, name)
                            if hwm.get(key, 0) < val:
                                hwm[key] = val
                                eng.wait_ge(sem, val)
                        if fn is None:
                            continue
                        ins = fn(eng)
                        if markkind is not None:
                            kind, name = markkind
                            if kind == "e":
                                ins.then_inc(self.esem[name], 1)
                            else:
                                ins.then_inc(self.dsem[name], 16)
                    if eng_name == "sync":
                        for q in DMA_QUEUES:
                            if self.dcnt[q]:
                                eng.wait_ge(self.dsem[q], self.dcnt[q])
                        for e2 in ENGINES:
                            if e2 != "sync" and self.ecnt[e2]:
                                eng.wait_ge(self.esem[e2], self.ecnt[e2])
                getattr(block, eng_name)(run)


class Bufs:

    def __init__(self, nc):
        ap = lambda h: h.ap()
        self.ab = ap(nc.alloc_sbuf_tensor("ab", [128, 88000], BF16))
        self.af = ap(nc.alloc_sbuf_tensor("af", [128, 8600], F32))
        self.ident_bf = ap(nc.alloc_sbuf_tensor("ident_bf", [128, 128], BF16))
        self.ident_f = ap(nc.alloc_sbuf_tensor("ident_f", [128, 128], F32))
        self.triu64 = ap(nc.alloc_sbuf_tensor("triu64", [64, 64], F32))
        self.causal = ap(nc.alloc_sbuf_tensor("causal", [128, 128], F32))
        self.ones_row = ap(nc.alloc_sbuf_tensor("ones_row", [1, 128], F32))
        self.ones_col = ap(nc.alloc_sbuf_tensor("ones_col", [128, 1], F32))
        self.cc = ap(nc.alloc_sbuf_tensor("cc", [128, 8], F32))
        self.pb = [ap(nc.alloc_psum_tensor(f"pb{i}", [128, 512], F32)) for i in range(6)]
        self.tb = [ap(nc.alloc_psum_tensor(f"tb{i}", [128, 1024], BF16)) for i in range(2)]


def emit_const(nc, S, B):
    S.op("gpsimd", lambda e: e.memset(B.ident_bf, 0.0))
    S.op("gpsimd", lambda e: e.affine_select(
        out=B.ident_bf, in_=B.ident_bf, compare_op=ALU.not_equal, fill=1.0,
        base=0, pattern=[[-1, 128]], channel_multiplier=1))
    S.op("gpsimd", lambda e: e.memset(B.ident_f, 0.0))
    S.op("gpsimd", lambda e: e.affine_select(
        out=B.ident_f, in_=B.ident_f, compare_op=ALU.not_equal, fill=1.0,
        base=0, pattern=[[-1, 128]], channel_multiplier=1))
    S.op("gpsimd", lambda e: e.memset(B.triu64, 0.0))
    S.op("gpsimd", lambda e: e.affine_select(
        out=B.triu64, in_=B.triu64, compare_op=ALU.is_gt, fill=1.0,
        base=0, pattern=[[-1, 64]], channel_multiplier=1))
    S.op("gpsimd", lambda e: e.memset(B.causal, 0.0))
    S.op("gpsimd", lambda e: e.affine_select(
        out=B.causal, in_=B.causal, compare_op=ALU.is_ge, fill=NEG,
        base=0, pattern=[[-1, 128]], channel_multiplier=1))
    S.op("gpsimd", lambda e: e.memset(B.ones_row, 1.0))
    S.op("gpsimd", lambda e: e.memset(B.cc[:, 0:1], 1e-5))
    S.op("gpsimd", lambda e: e.memset(B.cc[:, 1:2], 1e-6))
    S.op("gpsimd", lambda e: e.memset(B.cc[:, 2:3], 1.0))
    S.op("gpsimd", lambda e: e.memset(B.cc[:, 3:4], 128 * 1e-5))
    S.op("gpsimd", lambda e: e.memset(B.ones_col, 1.0), mark=True)


def emit_prep(nc, S, B, aps):
    af, ab = B.af, B.ab

    def vec_to_cols(src_ap, n, dst_sb):
        stg = af[0:n, 0:128]
        ld = S.dma("sync", stg, src_ap.rearrange("(a b) -> a b", b=128))
        tt = S.op("tensor", lambda e: e.transpose(
            B.pb[0][:, 0:n], stg, B.ident_f[0:n, 0:n]), waits=[ld], mark=True)
        cv = S.op("vector", lambda e: e.tensor_copy(dst_sb, B.pb[0][:, 0:n]),
                  waits=[tt], mark=True)
        S.full_barrier()
        return cv

    nwc = af[0:128, 256:264]
    vec_to_cols(aps["L0_attn_norm_w"], 8, nwc)
    if "dbg_nwc" in aps:
        S.dma("sync", aps["dbg_nwc"], nwc)
        S.full_barrier()
    rwc = af[0:128, 264:272]
    vec_to_cols(aps["L0_ffn_rms_w"], 8, rwc)
    S.dma("sync", aps["rmsw_c"], rwc)
    for ei in range(8):
        dst = af[0:128, 280:312]
        vec_to_cols(aps["L0_ffn_b1"][ei], 32, dst)
        S.dma("sync", aps["b1c"][ei], dst)
        S.full_barrier()
        dst2 = af[0:128, 312:320]
        vec_to_cols(aps["L0_ffn_b2"][ei], 8, dst2)
        S.dma("sync", aps["b2c"][ei], dst2)
        S.full_barrier()
    dst = af[0:128, 280:312]
    vec_to_cols(aps["L1_ffn_b1"], 32, dst)
    S.dma("sync", aps["l1b1c"], dst)
    S.full_barrier()
    dst2 = af[0:128, 312:320]
    vec_to_cols(aps["L1_ffn_b2"], 8, dst2)
    S.dma("sync", aps["l1b2c"], dst2)
    S.full_barrier()

    gsb = af[0:8, 4096:5120]
    g0 = S.dma("sync", gsb, aps["L0_ffn_gate_w"])
    for kt in range(DT):
        tt = S.op("tensor", lambda e, kt=kt: e.transpose(
            B.pb[1][:, kt * 8:(kt + 1) * 8], gsb[:, kt * 128:(kt + 1) * 128],
            B.ident_f[0:8, 0:8]), waits=[g0], mark=(kt == DT - 1))
    gft = af[0:128, 5120:5184]
    for kt in range(DT):
        cg = S.op("vector", lambda e, kt=kt: e.tensor_scalar(
            gft[:, kt * 8:(kt + 1) * 8], B.pb[1][:, kt * 8:(kt + 1) * 8],
            rwc[:, kt:kt + 1], None, ALU.mult), waits=[tt], mark=True)
    for kt in range(DT):
        mg = S.op("tensor", lambda e, kt=kt: e.matmul(
            B.pb[2][0:1, 0:8], B.ones_col, gft[:, kt * 8:(kt + 1) * 8],
            start=(kt == 0), stop=(kt == DT - 1)), waits=[cg], mark=(kt == DT - 1))
    gsum_sb = af[0:1, 5184:5192]
    cgs = S.op("vector", lambda e: e.tensor_copy(gsum_sb, B.pb[2][0:1, 0:8]),
               waits=[mg], mark=True)
    S.dma("sync", aps["gate_wTf"].rearrange("(t p) n -> p t n", p=128),
          gft.rearrange("p (t n) -> p t n", t=8), waits=[cg])
    S.dma("sync", aps["gsum"], gsum_sb, waits=[cgs])
    S.full_barrier()

    for wname, oname in (("L1_attn_Wq", "wqT_bf"), ("L1_attn_Wk", "wkT_bf"),
                         ("L1_attn_Wv", "wvT_bf"), ("L1_attn_Wo", "woT_bf")):
        for rb in range(DT):
            wrow = af[0:128, 1024:2048]
            ld = S.dma("sync", wrow, aps[wname][rb * 128:(rb + 1) * 128, :])
            prev_tc = {0: None, 1: None}
            for ct in range(DT):
                bank = B.pb[ct % 2]
                tt = S.op("tensor", lambda e, ct=ct, bank=bank: e.transpose(
                    bank[:, 0:128], wrow[:, ct * 128:(ct + 1) * 128], B.ident_f),
                    waits=[ld, prev_tc[ct % 2]], mark=True)
                wtile = ab[0:128, ct * 128:(ct + 1) * 128]
                cv = S.op("vector", lambda e, ct=ct, bank=bank, wtile=wtile: e.tensor_copy(
                    wtile, bank[:, 0:128]), waits=[tt], mark=True)
                prev_tc[ct % 2] = cv
                S.dma("scalar", aps[oname][ct * 128:(ct + 1) * 128,
                                           rb * 128:(rb + 1) * 128], wtile,
                      waits=[cv])
            S.full_barrier()

    for kt in range(DT):
        wrow = af[0:128, 1024:2048]
        wbf = ab[0:128, 0:1024]
        ld = S.dma("sync", wrow, aps["L0_attn_Wo"][kt * 128:(kt + 1) * 128, :])
        sc = S.op("vector", lambda e, kt=kt: e.tensor_scalar(
            wbf, wrow, nwc[:, kt:kt + 1], None, ALU.mult), waits=[ld], mark=True)
        S.dma("sync", aps["wo_sc_bf"][kt * 128:(kt + 1) * 128, :], wbf, waits=[sc])
        S.full_barrier()


def emit_ingest(nc, S, B, aps):
    af, ab = B.af, B.ab
    x_rows = aps["x_in"].rearrange("b n d -> (b n) d")

    rwc = af[0:128, 0:8]
    S.dma("sync", rwc, aps["rmsw_c"])
    gwTf = af[0:128, 8192:8256]
    S.dma("sync", gwTf.rearrange("p (t n) -> p t n", t=8),
          aps["gate_wTf"].rearrange("(t p) n -> p t n", p=128))
    gsum_sb = af[0:1, 8256:8264]
    g1 = S.dma("sync", gsum_sb, aps["gsum"])
    mgb = S.op("tensor", lambda e: e.matmul(
        B.pb[4][0:128, 0:8], B.ones_row, gsum_sb, start=True, stop=True),
        waits=[g1], mark=True)
    gsum_bc = af[0:128, 8264:8272]
    S.op("vector", lambda e: e.tensor_copy(gsum_bc, B.pb[4][0:128, 0:8]),
         waits=[mgb], mark=True)
    S.full_barrier()

    xt = af[0:128, 16:1040]
    sq = af[0:128, 1040:2064]
    xnf = af[0:128, 7168:8192]
    st = af[0:128, 2064:2080]
    st16 = af[0:128, 2080:2096]
    lgf = af[0:128, 2096:2112]
    xnt = ab[0:128, 64:1088]
    x2t = ab[0:128, 1088:2112]
    xoT = af[0:128, 2560:6656]
    xnoT = ab[0:128, 2112:6208]
    x2oT = ab[0:128, 6208:10304]
    selT = af[0:16, 6656:7168]

    for tt in range(32):
        gi = tt % 4
        tok_off = gi * 128
        ld = S.dma("sync", xt, x_rows[tt * 128:(tt + 1) * 128, :])
        s1 = S.op("vector", lambda e: e.reduce_sum(st[:, 0:1], xt, axis=AX.X),
                  waits=[ld], mark=True)
        qq = S.op("scalar", lambda e: e.activation(sq, xt, AF.Square),
                  waits=[ld], mark=True)
        s2 = S.op("vector", lambda e: e.reduce_sum(st[:, 1:2], sq, axis=AX.X),
                  waits=[qq], mark=True)
        S.op("vector", lambda e: e.tensor_scalar(st[:, 2:3], st[:, 0:1], 1.0 / D, None, ALU.mult))
        S.op("vector", lambda e: e.tensor_mul(st[:, 7:8], st[:, 2:3], st[:, 2:3]))
        S.op("vector", lambda e: e.tensor_scalar(st[:, 6:7], st[:, 1:2], 1.0 / D, None, ALU.mult))
        vv = S.op("vector", lambda e: e.tensor_sub(st[:, 3:4], st[:, 6:7], st[:, 7:8]),
                  mark=True)
        sd = S.op("scalar", lambda e: e.activation(st[:, 4:5], st[:, 3:4], AF.Ln,
                                                   bias=B.cc[:, 0:1]), waits=[vv], mark=True)
        vi = S.op("scalar", lambda e: e.activation(st[:, 10:11], st[:, 4:5], AF.Exp,
                                                   scale=-0.5), mark=True)
        xn = S.op("vector", lambda e: e.tensor_scalar(
            xnf, xt, st[:, 2:3], st[:, 10:11], ALU.subtract, ALU.mult),
            waits=[vi], mark=True)
        xnc = S.op("vector", lambda e: e.tensor_copy(xnt, xnf), mark=True)
        q2 = S.op("scalar", lambda e: e.activation(sq, xnf, AF.Square),
                  waits=[xn], mark=True)
        ms = S.op("vector", lambda e: e.reduce_sum(st[:, 5:6], sq, axis=AX.X),
                  waits=[q2], mark=True)
        s2d = S.op("scalar", lambda e: e.activation(st[:, 6:7], st[:, 5:6], AF.Ln,
                                                    bias=B.cc[:, 1:2], scale=1.0 / D),
                   waits=[ms], mark=True)
        r2 = S.op("scalar", lambda e: e.activation(st[:, 11:12], st[:, 6:7], AF.Exp,
                                                   scale=-0.5), mark=True)
        x2 = S.op("vector", lambda e: e.tensor_scalar(
            x2t, xnf, st[:, 11:12], None, ALU.mult), waits=[r2], mark=True)

        prev_c1 = {0: None, 1: None}
        for kt in range(DT):
            pslot = B.pb[kt % 2][:, 0:128]
            t1 = S.op("tensor", lambda e, kt=kt, pslot=pslot: e.transpose(
                pslot, xt[:, kt * 128:(kt + 1) * 128], B.ident_f),
                waits=[ld, xn, prev_c1[kt % 2]], mark=True)
            c1 = S.op("vector", lambda e, kt=kt, pslot=pslot, tok_off=tok_off: e.tensor_copy(
                xoT[:, kt * 512 + tok_off: kt * 512 + tok_off + 128], pslot),
                waits=[t1], mark=True)
            prev_c1[kt % 2] = c1
            tslot = B.tb[kt % 2][:, 0:128]
            t2 = S.op("tensor", lambda e, kt=kt, tslot=tslot: e.transpose(
                tslot, xnt[:, kt * 128:(kt + 1) * 128], B.ident_bf),
                waits=[c1, xnc], mark=True)
            c2 = S.op("vector", lambda e, kt=kt, tslot=tslot, tok_off=tok_off: e.tensor_copy(
                xnoT[:, kt * 512 + tok_off: kt * 512 + tok_off + 128], tslot),
                waits=[t2], mark=True)
            t3 = S.op("tensor", lambda e, kt=kt, tslot=tslot: e.transpose(
                tslot, x2t[:, kt * 128:(kt + 1) * 128], B.ident_bf),
                waits=[c2, x2], mark=True)
            c3 = S.op("vector", lambda e, kt=kt, tslot=tslot, tok_off=tok_off: e.tensor_scalar(
                x2oT[:, kt * 512 + tok_off: kt * 512 + tok_off + 128], tslot,
                rwc[:, kt:kt + 1], None, ALU.mult), waits=[t3], mark=True)

        ps_lg = B.pb[2]
        for kt in range(DT):
            mm = S.op("tensor", lambda e, kt=kt, tok_off=tok_off: e.matmul(
                ps_lg[0:128, 0:8],
                xoT[:, kt * 512 + tok_off: kt * 512 + tok_off + 128],
                gwTf[:, kt * 8:(kt + 1) * 8],
                start=(kt == 0), stop=(kt == DT - 1)),
                waits=[c1, c3], mark=(kt == DT - 1))
        v1 = S.op("vector", lambda e: e.tensor_scalar(
            lgf[:, 8:16], gsum_bc, st[:, 2:3], None, ALU.mult), waits=[xn], mark=True)
        v2 = S.op("vector", lambda e: e.tensor_sub(
            lgf[:, 0:8], ps_lg[0:128, 0:8], lgf[:, 8:16]), waits=[mm], mark=True)
        v3 = S.op("vector", lambda e: e.tensor_mul(
            st[:, 12:13], st[:, 10:11], st[:, 11:12]), waits=[r2], mark=True)
        v4 = S.op("vector", lambda e: e.tensor_scalar(
            lgf[:, 0:8], lgf[:, 0:8], st[:, 12:13], None, ALU.mult),
            waits=[v3], mark=True)
        mx = S.op("vector", lambda e: e.reduce_max(
            st[:, 8:9], lgf[:, 0:8], axis=AX.X, negate=True),
            waits=[v4], mark=True)
        ex = S.op("scalar", lambda e: e.activation(
            st16[:, 8:16], lgf[:, 0:8], AF.Exp, bias=st[:, 8:9],
            accum_out=st[:, 9:10]), waits=[mx], mark=True)
        sc = S.op("vector", lambda e: e.reciprocal(st16[:, 8:9], st[:, 9:10]),
                  waits=[ex], mark=True)
        sel = S.op("vector", lambda e: e.tensor_scalar(
            lgf[:, 8:16], lgf[:, 0:8], st[:, 8:9], 0.0, ALU.add, ALU.is_ge),
            waits=[mx, ex], mark=True)
        ssc = S.op("vector", lambda e: e.tensor_scalar(
            st16[:, 0:8], lgf[:, 8:16], st16[:, 8:9], None, ALU.mult),
            waits=[sel, sc], mark=True)
        ts = S.op("tensor", lambda e: e.transpose(
            B.pb[3][0:16, 0:128], st16[:, 0:16], B.ident_f),
            waits=[ssc], mark=True)
        cs = S.op("vector", lambda e, tok_off=tok_off: e.tensor_copy(
            selT[:, tok_off:tok_off + 128], B.pb[3][0:16, 0:128]),
            waits=[ts], mark=True)

        if gi == 3:
            g = tt // 4
            col0 = g * 512
            w = S.barrier_toks()
            S.dma("sync", aps["x0T"].rearrange("(t p) c -> p t c", p=128)[:, :, col0:col0 + 512],
                  xoT.rearrange("p (t c) -> p t c", t=8), waits=w)
            S.dma("gpsimd", aps["xn0T"].rearrange("(t p) c -> p t c", p=128)[:, :, col0:col0 + 512],
                  xnoT.rearrange("p (t c) -> p t c", t=8), waits=w)
            S.dma("gpsimd", aps["xn2T"].rearrange("(t p) c -> p t c", p=128)[:, :, col0:col0 + 512],
                  x2oT.rearrange("p (t c) -> p t c", t=8), waits=w)
            S.dma("scalar", aps["selsc"][:, col0:col0 + 512], selT, waits=w)
        S.full_barrier()


def fm_view(ap_):
    return ap_.rearrange("(t p) c -> p t c", p=128)


DBG_GSA = {"nch": None, "heads": None, "skip_post": False, "skip_wo": False, "skip_fg": False, "skip_qkv": False, "plain_evac": False, "skip_memset": False, "skip_ft": False}


def emit_l0attn(nc, S, B, aps):
    af, ab = B.af, B.ab
    C = 64
    NCH = SEQ // C
    H = 8
    xn_b = ab[0:128, 0:16384]
    og = ab[0:128, 0:16384]
    qb = ab[0:128, 16384:32768]
    kb = ab[0:128, 32768:49152]
    vb = ab[0:128, 49152:65536]
    fT = ab[0:128, 65536:73728]
    t_f = ab[0:128, 73728:75776]
    t_g = ab[0:128, 75776:77824]
    wg2 = ab[0:128, 77824:78848]
    wslot = ab[0:128, 78848:82944]
    stil_fm = ab[0:128, 82944:83200]
    stil_tm = ab[0:64, 83200:83712]
    pm_all = ab[0:64, 83712:84224]
    qt_all = ab[0:64, 84224:84736]
    qtT_all = ab[0:64, 86272:86784]
    stilT_all = ab[0:64, 86784:87296]
    kv_tm = ab[0:64, 78848:80896]
    ogn = ab[0:128, 32768:49152]
    th = af[0:128, 0:512]
    sp = af[0:128, 512:1024]
    s_fm = af[0:128, 1024:1280]
    sig = af[0:128, 1280:1536]
    ep_fm = af[0:128, 1536:1792]
    en_fm = af[0:128, 1792:2048]
    ep_tm = af[0:64, 2048:2560]
    eend = af[0:1, 2560:3072]
    hk = ab[0:128, 84736:85248]
    hv = ab[0:64, 85248:86272]
    tmp1 = af[0:128, 3072:3584]
    tmp2 = af[0:64, 3584:4096]
    ok_all = af[0:64, 4288:4800]
    qe_all = af[0:64, 4800:5312]
    mx8 = af[0:64, 4160:4184]
    ecols = af[0:64, 4184:4192]
    thg_all = af[0:64, 6224:7248]
    thg2 = af[0:64, 7248:8272]
    ogf_all = ab[0:64, 80896:81920]
    sqb = af[0:128, 0:512]
    rrow = af[0:1, 5200:5712]
    rrow2 = af[0:1, 5712:6224]

    lwg2 = S.dma("gpsimd", wg2.rearrange("p (t n) -> p t n", t=1),
                 aps["L0_attn_Wg2"].rearrange("(t p) n -> p t n", p=128))
    last_wo_mm = None
    last_proj_mm = None
    for b in range(DBG_GSA.get("bpc") or BPC):
        lxn = S.dma("sync", xn_b.rearrange("p (t c) -> p t c", t=8),
                    fm_view(aps["xn0T"])[:, :, b * SEQ:(b + 1) * SEQ],
                    waits=[last_wo_mm])
        for wname, dst, silu in (() if DBG_GSA["skip_qkv"] else (("L0_attn_Wq", qb, True),
                                 ("L0_attn_Wk", kb, True),
                                 ("L0_attn_Wv", vb, False))):
            for halfW in range(2):
                lw = S.dma("gpsimd", wslot.rearrange("p (t n) -> p t n", t=8),
                           aps[wname][:, halfW * 512:(halfW + 1) * 512]
                           .rearrange("(t p) n -> p t n", p=128),
                           waits=[last_proj_mm, last_wo_mm,
                                  ("e", "vector", S.ecnt["vector"])])
                prev_ev = {0: None, 1: None}
                for mt4 in range(4):
                    for grp in range(4):
                        bank = B.pb[grp % 2]
                        for kt in range(DT):
                            mm = S.op("tensor", lambda e_, mt4=mt4, grp=grp, kt=kt, bank=bank: e_.matmul(
                                bank[:, 0:512],
                                wslot[:, kt * 512 + mt4 * 128: kt * 512 + (mt4 + 1) * 128],
                                xn_b[:, kt * 2048 + grp * 512: kt * 2048 + (grp + 1) * 512],
                                start=(kt == 0), stop=(kt == DT - 1)),
                                waits=[lw, lxn, prev_ev[grp % 2]],
                                mark=(kt == DT - 1))
                        last_proj_mm = mm
                        o = (halfW * 4 + mt4) * 2048 + grp * 512
                        if silu and not DBG_GSA["plain_evac"]:
                            t1 = S.op("scalar", lambda e_, bank=bank: e_.activation(
                                th, bank[:, 0:512], AF.Tanh, scale=0.5),
                                waits=[mm], mark=True)
                            t2 = S.op("vector", lambda e_: e_.tensor_scalar(
                                sp, th, 0.5, 0.5, ALU.mult, ALU.add),
                                waits=[t1], mark=True)
                            ev = S.op("vector", lambda e_, o=o, bank=bank, dst=dst: e_.tensor_mul(
                                dst[:, o:o + 512], bank[:, 0:512], sp),
                                waits=[t2], mark=True)
                        else:
                            ev = S.op("vector", lambda e_, o=o, bank=bank, dst=dst: e_.tensor_copy(
                                dst[:, o:o + 512], bank[:, 0:512]), waits=[mm], mark=True)
                        prev_ev[grp % 2] = ev
        fg_list = (("L0_attn_Wf1", 128, t_f), ("L0_attn_Wg1", 128, t_g))
        if DBG_GSA.get("fg_ngrp") is not None:
            pass
        if DBG_GSA.get("fg_first_only"):
            fg_list = fg_list[:1]
        for wname, wcols, dst in (() if DBG_GSA["skip_fg"] else fg_list):
            lw = S.dma("gpsimd", wslot[:, 0:1024].rearrange("p (t n) -> p t n", t=8),
                       aps[wname].rearrange("(t p) n -> p t n", p=128),
                       waits=[last_proj_mm])
            prev_ev = {0: None, 1: None}
            for grp in range(DBG_GSA.get("fg_ngrp", 4)):
                bank = B.pb[grp % 2]
                for kt in range(DT):
                    mm = S.op("tensor", lambda e_, grp=grp, kt=kt, bank=bank: e_.matmul(
                        bank[:, 0:512],
                        wslot[:, kt * 128:(kt + 1) * 128],
                        xn_b[:, kt * 2048 + grp * 512: kt * 2048 + (grp + 1) * 512],
                        start=(kt == 0), stop=(kt == DT - 1)),
                        waits=[lw, prev_ev[grp % 2]], mark=(kt == DT - 1))
                last_proj_mm = mm
                ev = S.op("vector", lambda e_, grp=grp, bank=bank, dst=dst: e_.tensor_copy(
                    dst[:, grp * 512:(grp + 1) * 512], bank[:, 0:512]),
                    waits=[mm], mark=True)
                prev_ev[grp % 2] = ev
        if DBG_GSA["skip_fg"]:
            continue
        lw2 = S.dma("gpsimd", wslot[:, 0:512].rearrange("p (t n) -> p t n", t=1),
                    aps["L0_attn_Wf2"].rearrange("(t p) n -> p t n", p=128),
                    waits=[last_proj_mm])
        prev_ev = {0: None, 1: None}
        for mt in range(4 if not DBG_GSA["skip_ft"] else 0):
            for grp in range(4):
                bank = B.pb[grp % 2]
                mm = S.op("tensor", lambda e_, mt=mt, grp=grp, bank=bank: e_.matmul(
                    bank[:, 0:512], wslot[:, mt * 128:(mt + 1) * 128],
                    t_f[:, grp * 512:(grp + 1) * 512], start=True, stop=True),
                    waits=[lw2, prev_ev[grp % 2]], mark=True)
                last_proj_mm = mm
                ev = S.op("vector", lambda e_, mt=mt, grp=grp, bank=bank: e_.tensor_copy(
                    fT[:, mt * 2048 + grp * 512: mt * 2048 + (grp + 1) * 512],
                    bank[:, 0:512]), waits=[mm], mark=True)
                prev_ev[grp % 2] = ev
        if DBG_GSA["skip_memset"]:
            z1 = z2 = None
        else:
            z1 = S.op("gpsimd", lambda e_: e_.memset(hk, 0.0), waits=[("e", "vector", S.ecnt["vector"])])
            z2 = S.op("gpsimd", lambda e_: e_.memset(hv, 0.0))
        bank_free = {}
        bank_free[("hk", "all")] = z2
        bank_free[("hv", "all")] = z2
        for ct in range(NCH if DBG_GSA["nch"] is None else DBG_GSA["nch"]):
            tnh = S.op("scalar", lambda e_, ct=ct: e_.activation(
                th[:, 0:256].rearrange("p (a c) -> p a c", a=4),
                fT.rearrange("p (a c) -> p a c", a=4)[:, :, ct * C:(ct + 1) * C],
                AF.Tanh, scale=0.5), waits=[("e", "vector", S.ecnt["vector"]),
                                            z1, z2], mark=True)
            vs = S.op("vector", lambda e_: e_.tensor_scalar(
                sig, th[:, 0:256], 0.5, 0.5, ALU.mult, ALU.add), waits=[tnh], mark=True)
            vsn = S.op("vector", lambda e_: e_.tensor_scalar(
                s_fm, th[:, 0:256], -0.5, 0.5, ALU.mult, ALU.add), mark=True)
            scans = []
            for mt in range(4):
                sc = S.op("vector", lambda e_, mt=mt: e_.tensor_tensor_scan(
                    ep_fm[:, mt * C:(mt + 1) * C], sig[:, mt * C:(mt + 1) * C],
                    sig[:, mt * C:(mt + 1) * C], 1.0, ALU.mult, ALU.bypass),
                    waits=[vs], mark=True)
                scans.append(sc)
            rec = S.op("vector", lambda e_: e_.reciprocal(en_fm, ep_fm),
                       waits=scans, mark=True)
            stf = S.op("vector", lambda e_: e_.tensor_mul(stil_fm, s_fm, en_fm),
                       waits=[rec, vsn], mark=True)
            for mt in range(4):
                tsl = B.tb[mt % 2][0:64, 0:128]
                tt = S.op("tensor", lambda e_, mt=mt, tsl=tsl: e_.transpose(
                    tsl, stil_fm[:, mt * C:(mt + 1) * C], B.ident_bf),
                    waits=[stf, bank_free.get(("tb", mt % 2))], mark=True)
                cv = S.op("vector", lambda e_, mt=mt, tsl=tsl: e_.tensor_copy(
                    stil_tm[:, mt * 128:(mt + 1) * 128], tsl),
                    waits=[tt], mark=True)
                bank_free[("tb", mt % 2)] = cv
                pbk = B.pb[4 + mt % 2]
                te = S.op("tensor", lambda e_, mt=mt, pbk=pbk: e_.transpose(
                    pbk[0:64, 0:128], ep_fm[:, mt * C:(mt + 1) * C],
                    B.ident_f), waits=scans + [bank_free.get(("pb", 4 + mt % 2))],
                    mark=True)
                ce = S.op("vector", lambda e_, mt=mt, pbk=pbk: e_.tensor_copy(
                    ep_tm[:, mt * 128:(mt + 1) * 128], pbk[0:64, 0:128]),
                    waits=[te], mark=True)
                bank_free[("pb", 4 + mt % 2)] = ce
            for mt in range(4):
                pbk = B.pb[4 + mt % 2]
                tz = S.op("tensor", lambda e_, mt=mt, pbk=pbk: e_.transpose(
                    pbk[0:1, 0:128], ep_fm[:, mt * C + C - 1: mt * C + C],
                    B.ident_f), waits=scans + [bank_free.get(("pb", 4 + mt % 2))],
                    mark=True)
                cz = S.op("vector", lambda e_, mt=mt, pbk=pbk: e_.tensor_copy(
                    eend[0:1, mt * 128:(mt + 1) * 128], pbk[0:1, 0:128]),
                    waits=[tz], mark=True)
                bank_free[("pb", 4 + mt % 2)] = cz
            dec_done = S.barrier_toks()
            NH = DBG_GSA["heads"] if DBG_GSA["heads"] is not None else H
            kv_toks = []
            for h in range(NH):
                for src_, dst_off, slot in ((kb, 0, 0), (vb, 1024, 1)):
                    tsl = B.tb[slot][0:64, 0:128]
                    tv = S.op("tensor", lambda e_, h=h, ct=ct, src_=src_, tsl=tsl: e_.transpose(
                        tsl, src_[:, h * 2048 + ct * C: h * 2048 + (ct + 1) * C],
                        B.ident_bf), waits=[bank_free.get(("tb", slot))] + dec_done,
                        mark=True)
                    cv = S.op("vector", lambda e_, h=h, dst_off=dst_off, tsl=tsl: e_.tensor_copy(
                        kv_tm[0:64, dst_off + h * 128: dst_off + (h + 1) * 128], tsl),
                        waits=[tv], mark=True)
                    bank_free[("tb", slot)] = cv
                    kv_toks.append(cv)
            for h in range(NH):
                mmP = S.op("tensor", lambda e_, h=h, ct=ct: e_.matmul(
                    B.pb[0][0:64, h * 64:(h + 1) * 64],
                    kb[:, h * 2048 + ct * C: h * 2048 + (ct + 1) * C],
                    qb[:, h * 2048 + ct * C: h * 2048 + (ct + 1) * C],
                    start=True, stop=True),
                    waits=[bank_free.get(("pb", 0))] + dec_done, mark=(h == NH - 1))
            vPm = S.op("vector", lambda e_, NH=NH: e_.tensor_tensor(
                pm_all[0:64, 0:NH * 64].rearrange("p (g j) -> p g j", g=NH),
                B.pb[0][0:64, 0:NH * 64].rearrange("p (g j) -> p g j", g=NH),
                B.triu64.rearrange("p (o j) -> p o j", o=1).broadcast_to((64, NH, 64)),
                ALU.mult), waits=[mmP], mark=True)
            bank_free[("pb", 0)] = vPm
            for h in range(NH):
                col = (h // 2) * 128 + (h % 2) * 64
                S.op("tensor", lambda e_, h=h, col=col: e_.matmul(
                    B.pb[1][0:64, h * 64:(h + 1) * 64],
                    pm_all[0:64, h * 64:(h + 1) * 64],
                    stil_tm[0:64, col:col + 64], start=True, stop=False),
                    waits=[vPm, bank_free.get(("pb", 1))])
                mmOk = S.op("tensor", lambda e_, h=h, ct=ct: e_.matmul(
                    B.pb[1][0:64, h * 64:(h + 1) * 64],
                    qb[:, h * 2048 + ct * C: h * 2048 + (ct + 1) * C],
                    hk[:, h * 64:(h + 1) * 64], start=False, stop=True),
                    waits=[bank_free.get(("hk", "all"))], mark=(h == NH - 1))
            ep_v = ep_tm[0:64, 0:NH * 64]
            v1 = S.op("vector", lambda e_, NH=NH: e_.tensor_mul(
                ok_all[0:64, 0:NH * 64], B.pb[1][0:64, 0:NH * 64], ep_v),
                waits=[mmOk], mark=True)
            bank_free[("pb", 1)] = v1
            v2 = S.op("vector", lambda e_, NH=NH: e_.reduce_max(
                mx8[0:64, 0:NH].rearrange("p (g o) -> p g o", o=1),
                ok_all[0:64, 0:NH * 64].rearrange("p (g j) -> p g j", g=NH),
                axis=AX.X, negate=True), mark=True)
            v3 = S.op("vector", lambda e_, NH=NH: e_.tensor_tensor(
                ok_all[0:64, 0:NH * 64].rearrange("p (g j) -> p g j", g=NH),
                ok_all[0:64, 0:NH * 64].rearrange("p (g j) -> p g j", g=NH),
                mx8[0:64, 0:NH].rearrange("p (g o) -> p g o", o=1).broadcast_to((64, NH, 64)),
                ALU.add), mark=True)
            s1 = S.op("scalar", lambda e_, NH=NH: e_.activation(
                qe_all[0:64, 0:NH * 64], ok_all[0:64, 0:NH * 64], AF.Exp),
                waits=[v3], mark=True)
            v4 = S.op("vector", lambda e_, NH=NH: e_.reduce_sum(
                mx8[0:64, 8:8 + NH].rearrange("p (g o) -> p g o", o=1),
                qe_all[0:64, 0:NH * 64].rearrange("p (g j) -> p g j", g=NH),
                axis=AX.X), waits=[s1], mark=True)
            v5 = S.op("vector", lambda e_, NH=NH: e_.reciprocal(
                mx8[0:64, 16:16 + NH], mx8[0:64, 8:8 + NH]), mark=True)
            v6 = S.op("vector", lambda e_, NH=NH: e_.tensor_tensor(
                ok_all[0:64, 0:NH * 64].rearrange("p (g j) -> p g j", g=NH),
                qe_all[0:64, 0:NH * 64].rearrange("p (g j) -> p g j", g=NH),
                mx8[0:64, 16:16 + NH].rearrange("p (g o) -> p g o", o=1).broadcast_to((64, NH, 64)),
                ALU.mult), mark=True)
            v7 = S.op("vector", lambda e_, NH=NH: e_.tensor_mul(
                qt_all[0:64, 0:NH * 64], ok_all[0:64, 0:NH * 64], ep_v), mark=True)
            aw = []
            for h in range(NH):
                col = (h // 2) * 128 + (h % 2) * 64
                par = h % 2
                tq = S.op("tensor", lambda e_, h=h, par=par: e_.transpose(
                    B.tb[0][0:64, par * 64:par * 64 + 64],
                    qt_all[0:64, h * 64:(h + 1) * 64], B.ident_bf[0:64, 0:64]),
                    waits=[v7, bank_free.get(("tb", 0))], mark=True)
                cq = S.op("vector", lambda e_, h=h, par=par: e_.tensor_copy(
                    qtT_all[0:64, h * 64:(h + 1) * 64],
                    B.tb[0][0:64, par * 64:par * 64 + 64]), waits=[tq], mark=True)
                bank_free[("tb", 0)] = cq
                ts_ = S.op("tensor", lambda e_, col=col, par=par: e_.transpose(
                    B.tb[1][0:64, par * 64:par * 64 + 64],
                    stil_tm[0:64, col:col + 64], B.ident_bf[0:64, 0:64]),
                    waits=[bank_free.get(("tb", 1))] + dec_done, mark=True)
                cs = S.op("vector", lambda e_, h=h, par=par: e_.tensor_copy(
                    stilT_all[0:64, h * 64:(h + 1) * 64],
                    B.tb[1][0:64, par * 64:par * 64 + 64]), waits=[ts_], mark=True)
                bank_free[("tb", 1)] = cs
                mmA = S.op("tensor", lambda e_, h=h: e_.matmul(
                    B.pb[0][0:64, h * 64:(h + 1) * 64],
                    stilT_all[0:64, h * 64:(h + 1) * 64],
                    qtT_all[0:64, h * 64:(h + 1) * 64], start=True, stop=True),
                    waits=[cq, cs, bank_free.get(("pb", 0))], mark=(h == NH - 1))
                aw.append(mmA)
            vAm = S.op("vector", lambda e_, NH=NH: e_.tensor_tensor(
                pm_all[0:64, 0:NH * 64].rearrange("p (g j) -> p g j", g=NH),
                B.pb[0][0:64, 0:NH * 64].rearrange("p (g j) -> p g j", g=NH),
                B.triu64.rearrange("p (o j) -> p o j", o=1).broadcast_to((64, NH, 64)),
                ALU.mult), waits=[aw[-1]], mark=True)
            bank_free[("pb", 0)] = vAm
            ovw = []
            for h in range(NH):
                obank = B.pb[4] if h < 4 else B.pb[1]
                ocol = (h % 4) * 128
                S.op("tensor", lambda e_, h=h, obank=obank, ocol=ocol: e_.matmul(
                    obank[0:64, ocol:ocol + 128],
                    pm_all[0:64, h * 64:(h + 1) * 64],
                    kv_tm[0:64, 1024 + h * 128: 1024 + (h + 1) * 128],
                    start=True, stop=False),
                    waits=[vAm, kv_toks[2 * h + 1],
                           bank_free.get(("pbov", 4 if h < 4 else 1))])
                mmVb = S.op("tensor", lambda e_, h=h, obank=obank, ocol=ocol: e_.matmul(
                    obank[0:64, ocol:ocol + 128],
                    qtT_all[0:64, h * 64:(h + 1) * 64],
                    hv[0:64, h * 128:(h + 1) * 128], start=False, stop=True),
                    waits=[bank_free.get(("hv", "all"))], mark=(h % 4 == 3))
                if h % 4 == 3:
                    ovw.append(mmVb)
            gw_ = []
            for gh in range((NH + 3) // 4):
                mmG = S.op("tensor", lambda e_, gh=gh, ct=ct: e_.matmul(
                    B.pb[5][0:64, 0:512], t_g[:, ct * C:(ct + 1) * C],
                    wg2[:, gh * 512:(gh + 1) * 512], start=True, stop=True),
                    waits=[lwg2, bank_free.get(("pb", 5))], mark=True)
                sth = S.op("scalar", lambda e_, gh=gh: e_.activation(
                    thg_all[0:64, gh * 512:(gh + 1) * 512], B.pb[5][0:64, 0:512],
                    AF.Tanh, scale=0.5), waits=[mmG], mark=True)
                bank_free[("pb", 5)] = sth
                gw_.append(sth)
            vsg = S.op("vector", lambda e_, NH=NH: e_.tensor_scalar(
                thg2[0:64, 0:NH * 128], thg_all[0:64, 0:NH * 128], 0.5, 0.5,
                ALU.mult, ALU.add), waits=gw_, mark=True)
            ogw = []
            for half in range((NH + 3) // 4):
                obank = B.pb[4] if half == 0 else B.pb[1]
                vog = S.op("vector", lambda e_, half=half, obank=obank: e_.tensor_mul(
                    ogf_all[0:64, half * 512:(half + 1) * 512],
                    obank[0:64, 0:512], thg2[0:64, half * 512:(half + 1) * 512]),
                    waits=[vsg] + ovw, mark=True)
                bank_free[("pbov", 4 if half == 0 else 1)] = vog
                ogw.append(vog)
            for h in range(NH):
                par = h % 2
                tto = S.op("tensor", lambda e_, h=h, par=par: e_.transpose(
                    B.tb[1][0:128, 256 + par * 64:256 + par * 64 + 64],
                    ogf_all[0:64, h * 128:(h + 1) * 128], B.ident_bf[0:64, 0:64]),
                    waits=ogw + [bank_free.get(("tb", 1))], mark=True)
                cog = S.op("vector", lambda e_, h=h, ct=ct, par=par: e_.tensor_copy(
                    og[:, h * 2048 + ct * C: h * 2048 + (ct + 1) * C],
                    B.tb[1][0:128, 256 + par * 64:256 + par * 64 + 64]),
                    waits=[tto], mark=True)
                bank_free[("tb", 1)] = cog
            for h in range(NH):
                col = (h // 2) * 128 + (h % 2) * 64
                mmHk = S.op("tensor", lambda e_, h=h, col=col: e_.matmul(
                    B.pb[2][0:128, h * 64:(h + 1) * 64],
                    kv_tm[0:64, h * 128:(h + 1) * 128],
                    stil_tm[0:64, col:col + 64], start=True, stop=True),
                    waits=[kv_toks[2 * h], bank_free.get(("pb", 2)), mmOk],
                    mark=(h == NH - 1))
            mmBc = S.op("tensor", lambda e_, NH=NH: e_.matmul(
                B.pb[3][0:128, 0:NH * 64], B.ones_row, eend[0:1, 0:NH * 64],
                start=True, stop=True),
                waits=[bank_free.get(("pb", 3))] + dec_done, mark=True)
            vt1 = S.op("vector", lambda e_, NH=NH: e_.tensor_add(
                tmp1[0:128, 0:NH * 64], B.pb[2][0:128, 0:NH * 64],
                hk[:, 0:NH * 64]), waits=[mmHk], mark=True)
            bank_free[("pb", 2)] = vt1
            vhk = S.op("vector", lambda e_, NH=NH: e_.tensor_mul(
                hk[:, 0:NH * 64], tmp1[0:128, 0:NH * 64], B.pb[3][0:128, 0:NH * 64]),
                waits=[vt1, mmBc], mark=True)
            bank_free[("pb", 3)] = vhk
            bank_free[("hk", "all")] = vhk
            for h in range(NH):
                mmEc = S.op("tensor", lambda e_, h=h: e_.matmul(
                    B.pb[0][0:64, h:h + 1], eend[0:1, h * 64:(h + 1) * 64],
                    B.ones_row[0:1, 0:1], start=True, stop=True),
                    waits=[bank_free.get(("pb", 0))] + dec_done, mark=(h == NH - 1))
            cec = S.op("vector", lambda e_, NH=NH: e_.tensor_copy(
                ecols[0:64, 0:NH], B.pb[0][0:64, 0:NH]), waits=[mmEc], mark=True)
            bank_free[("pb", 0)] = cec
            for h in range(NH):
                col = (h // 2) * 128 + (h % 2) * 64
                hbank = B.pb[2] if h < 4 else B.pb[3]
                hcol = (h % 4) * 128
                mmHv = S.op("tensor", lambda e_, col=col, hbank=hbank, hcol=hcol, h=h: e_.matmul(
                    hbank[0:64, hcol:hcol + 128], stil_tm[0:64, col:col + 64],
                    kv_tm[0:64, 1024 + h * 128: 1024 + (h + 1) * 128],
                    start=True, stop=True),
                    waits=[kv_toks[2 * h + 1],
                           bank_free.get(("pbhv", 2 if h < 4 else 3)),
                           vt1 if h < 4 else vhk, ovw[-1]],
                    mark=(h % 4 == 3))
                if h % 4 == 3:
                    nh4 = min(4, NH - (0 if h < 4 else 4))
                    half = 0 if h < 4 else 1
                    vt2 = S.op("vector", lambda e_, half=half, hbank=hbank, nh4=nh4: e_.tensor_add(
                        tmp2[0:64, 0:nh4 * 128], hbank[0:64, 0:nh4 * 128],
                        hv[0:64, half * 512: half * 512 + nh4 * 128]),
                        waits=[mmHv], mark=True)
                    bank_free[("pbhv", 2 if h < 4 else 3)] = vt2
                    vhv = S.op("vector", lambda e_, half=half, nh4=nh4: e_.tensor_tensor(
                        hv[0:64, half * 512: half * 512 + nh4 * 128]
                        .rearrange("p (g j) -> p g j", g=nh4),
                        tmp2[0:64, 0:nh4 * 128].rearrange("p (g j) -> p g j", g=nh4),
                        ecols[0:64, half * 4: half * 4 + nh4]
                        .rearrange("p (g o) -> p g o", o=1).broadcast_to((64, nh4, 128)),
                        ALU.mult), waits=[cec], mark=True)
                    bank_free[("hv", "all")] = vhv
        for h in range(H if not DBG_GSA["skip_post"] else 0):
            for grp in range(4):
                o = h * 2048 + grp * 512
                sq1 = S.op("scalar", lambda e_, o=o: e_.activation(
                    sqb, og[:, o:o + 512], AF.Square),
                    waits=[("e", "vector", S.ecnt["vector"])], mark=True)
                mmS = S.op("tensor", lambda e_: e_.matmul(
                    B.pb[0][0:1, 0:512], B.ones_col, sqb, start=True, stop=True),
                    waits=[sq1, bank_free.get(("pb", 0))], mark=True)
                ssd = S.op("scalar", lambda e_: e_.activation(
                    rrow, B.pb[0][0:1, 0:512], AF.Ln, bias=B.cc[0:1, 3:4]),
                    waits=[mmS], mark=True)
                bank_free[("pb", 0)] = ssd
                srr = S.op("scalar", lambda e_: e_.activation(
                    rrow2, rrow, AF.Exp, scale=-0.5), mark=True)
                mmB = S.op("tensor", lambda e_: e_.matmul(
                    B.pb[1][0:128, 0:512], B.ones_row, rrow2, start=True, stop=True),
                    waits=[srr, bank_free.get(("pb", 1))], mark=True)
                von = S.op("vector", lambda e_, o=o: e_.scalar_tensor_tensor(
                    ogn[:, o:o + 512], og[:, o:o + 512], float(np.sqrt(128.0)),
                    B.pb[1][0:128, 0:512], ALU.mult, ALU.mult),
                    waits=[mmB], mark=True)
                bank_free[("pb", 1)] = von
        for halfW in range(2 if not DBG_GSA["skip_wo"] else 0):
            lwo = S.dma("gpsimd", wslot.rearrange("p (t n) -> p t n", t=4),
                        aps["wo_sc_bf"][halfW * 512:(halfW + 1) * 512, :]
                        .rearrange("(t p) n -> p t n", p=128),
                        waits=[last_proj_mm, last_wo_mm,
                               ("e", "vector", S.ecnt["vector"]),
                               ("e", "tensor", S.ecnt["tensor"])])
            prev_ao = {0: None, 1: None}
            prev_aodma = {0: None, 1: None}
            for dout in range(DT):
                for grp in range(4):
                    par = (dout * 4 + grp) % 2
                    bank = B.pb[2 + par]
                    for kt in range(4):
                        mm = S.op("tensor", lambda e_, dout=dout, grp=grp, kt=kt, halfW=halfW, bank=bank: e_.matmul(
                            bank[:, 0:512],
                            wslot[:, kt * 1024 + dout * 128: kt * 1024 + (dout + 1) * 128],
                            ogn[:, (halfW * 4 + kt) * 2048 + grp * 512:
                                (halfW * 4 + kt) * 2048 + (grp + 1) * 512],
                            start=(kt == 0), stop=(kt == 3)),
                            waits=[lwo, prev_ao[par],
                                   ("e", "vector", S.ecnt["vector"])],
                            mark=(kt == 3))
                    last_wo_mm = mm
                    abuf = th if par == 0 else sp
                    aev = S.op("vector", lambda e_, par=par, bank=bank, abuf=abuf: e_.tensor_copy(
                        abuf, bank[:, 0:512]), waits=[mm, prev_aodma[par]], mark=True)
                    prev_ao[par] = aev
                    kw = {} if halfW == 0 else {"accum_op": ALU.add}
                    ad = S.dma("gpsimd", fm_view(aps["a0T"])[:, dout,
                               b * SEQ + grp * 512: b * SEQ + (grp + 1) * 512],
                               abuf, waits=[aev], **kw)
                    prev_aodma[par] = ad
    S.full_barrier()


def emit_l1attn(nc, S, B, aps):
    af, ab = B.af, B.ab
    xn_b = ab[0:128, 0:16384]
    qb = ab[0:128, 16384:24576]
    kb = ab[0:128, 24576:32768]
    vb = ab[0:128, 32768:40960]
    og = ab[0:128, 40960:49152]
    wslot = ab[0:128, 49152:53248]
    probs_e = [ab[0:128, 53248:55296], ab[0:128, 58624:60672]]
    probs_n = [ab[0:128, 55296:57344], ab[0:128, 60672:62720]]
    pT = [ab[0:128, 57344 + i * 128: 57344 + (i + 1) * 128] for i in range(4)]
    v_tm = ab[0:128, 62720:63744]
    scores = [af[0:128, 0:2048], af[0:128, 3072:5120]]
    st = af[0:128, 2048:2080]
    ao = [af[0:128, 5120 + i * 512: 5120 + (i + 1) * 512] for i in range(2)]

    last_proj_mm = None
    last_wo_mm = None
    last_wo_aev = None
    for b in range(BPC):
        lxn = S.dma("sync", xn_b.rearrange("p (t c) -> p t c", t=8),
                    fm_view(aps["xn1T"])[:, :, b * SEQ:(b + 1) * SEQ],
                    waits=[last_proj_mm])
        for half in range(2):
            for wname, dst, scale in (("wqT_bf", qb, 0.125),
                                      ("wkT_bf", kb, None),
                                      ("wvT_bf", vb, None)):
                lw = S.dma("gpsimd", wslot[:, 0:4096].rearrange("p (t n) -> p t n", t=8),
                           aps[wname][:, half * 512:(half + 1) * 512]
                           .rearrange("(t p) n -> p t n", p=128),
                           waits=[last_proj_mm, last_wo_mm])
                prev_ev = {0: None, 1: None}
                for mt4 in range(4):
                    for grp in range(4):
                        bank = B.pb[grp % 2]
                        for kt in range(DT):
                            mm = S.op("tensor", lambda e_, mt4=mt4, grp=grp, kt=kt, bank=bank: e_.matmul(
                                bank[:, 0:512],
                                wslot[:, kt * 512 + mt4 * 128: kt * 512 + (mt4 + 1) * 128],
                                xn_b[:, kt * 2048 + grp * 512: kt * 2048 + (grp + 1) * 512],
                                start=(kt == 0), stop=(kt == DT - 1)),
                                waits=[lw, lxn, prev_ev[grp % 2]],
                                mark=(kt == DT - 1))
                        last_proj_mm = mm
                        o = mt4 * 2048 + grp * 512
                        if scale is None:
                            ev = S.op("vector", lambda e_, o=o, bank=bank, dst=dst: e_.tensor_copy(
                                dst[:, o:o + 512], bank[:, 0:512]), waits=[mm], mark=True)
                        else:
                            ev = S.op("vector", lambda e_, o=o, bank=bank, dst=dst, scale=scale: e_.tensor_scalar(
                                dst[:, o:o + 512], bank[:, 0:512], scale, None, ALU.mult),
                                waits=[mm], mark=True)
                        prev_ev[grp % 2] = ev
            if b == 0 and half == 0 and "dbgq" in aps:
                S.full_barrier()
                S.dma("sync", aps["dbgq"], qb)
                S.dma("sync", aps["dbgk"], kb)
                S.dma("sync", aps["dbgv"], vb)
            for h8 in range(8):
                hp, row0 = h8 // 2, (h8 % 2) * 64
                qh = qb[row0:row0 + 64, hp * 2048:(hp + 1) * 2048]
                kh = kb[row0:row0 + 64, hp * 2048:(hp + 1) * 2048]
                vh = vb[row0:row0 + 64, hp * 2048:(hp + 1) * 2048]
                prev_vc = {0: None, 1: None}
                vts = []
                idv = B.ident_bf[row0:row0 + 64, row0:row0 + 64]
                for kt2 in range(16):
                    tsl = B.tb[kt2 % 2][:, 0:64]
                    tv = S.op("tensor", lambda e_, kt2=kt2, tsl=tsl, vh=vh, idv=idv: e_.transpose(
                        tsl, vh[:, kt2 * 128:(kt2 + 1) * 128], idv),
                        waits=[prev_vc[kt2 % 2], ("e", "vector", S.ecnt["vector"])],
                        mark=True)
                    cv = S.op("vector", lambda e_, kt2=kt2, tsl=tsl: e_.tensor_copy(
                        v_tm[:, kt2 * 64:(kt2 + 1) * 64], tsl), waits=[tv], mark=True)
                    prev_vc[kt2 % 2] = cv
                    vts.append(cv)
                prev_avev = {0: None, 1: None}
                prev_sc = {0: None, 1: None, 2: None, 3: None}
                prev_pc = {0: None, 1: None, 2: None, 3: None}
                av_slot = {0: None, 1: None, 2: None, 3: None}
                for qt in range(16):
                    dt_tile = qt if qt < 8 else qt - 8
                    klen = (dt_tile + 1) * 128
                    ngrp = (klen + 511) // 512
                    qp = qt % 2
                    sc_b = scores[qp]
                    pe_b = probs_e[qp]
                    pn_b = probs_n[qp]
                    st0 = st[:, qp * 16 + 0: qp * 16 + 1]
                    st1 = st[:, qp * 16 + 1: qp * 16 + 2]
                    st2 = st[:, qp * 16 + 2: qp * 16 + 3]
                    evs = []
                    for g2 in range(ngrp):
                        n = min(512, klen - g2 * 512)
                        bank = B.pb[qp * 2 + g2 % 2]
                        mm = S.op("tensor", lambda e_, qt=qt, g2=g2, n=n, bank=bank, qh=qh, kh=kh: e_.matmul(
                            bank[:, 0:n], qh[:, qt * 128:(qt + 1) * 128],
                            kh[:, g2 * 512: g2 * 512 + n], start=True, stop=True),
                            waits=[prev_sc[qp * 2 + g2 % 2], last_wo_aev],
                            mark=True)
                        if n > 128 or g2 < ngrp - 1:
                            ncopy = n if g2 < ngrp - 1 else n - 128
                            ev = S.op("vector", lambda e_, g2=g2, ncopy=ncopy, bank=bank, sc_b=sc_b: e_.tensor_copy(
                                sc_b[:, g2 * 512: g2 * 512 + ncopy], bank[:, 0:ncopy]),
                                waits=[mm], mark=True)
                            evs.append(ev)
                            prev_sc[qp * 2 + g2 % 2] = ev
                        if g2 == ngrp - 1:
                            ev = S.op("vector", lambda e_, dt_tile=dt_tile, g2=g2, n=n, bank=bank, sc_b=sc_b: e_.tensor_add(
                                sc_b[:, dt_tile * 128: dt_tile * 128 + 128],
                                bank[:, n - 128:n], B.causal),
                                waits=[mm], mark=True)
                            evs.append(ev)
                            prev_sc[qp * 2 + g2 % 2] = ev
                    mx = S.op("vector", lambda e_, klen=klen, sc_b=sc_b, st0=st0: e_.reduce_max(
                        st0, sc_b[:, 0:klen], axis=AX.X, negate=True),
                        waits=evs, mark=True)
                    ex = S.op("scalar", lambda e_, klen=klen, sc_b=sc_b, pe_b=pe_b, st0=st0, st1=st1: e_.activation(
                        pe_b[:, 0:klen], sc_b[:, 0:klen], AF.Exp,
                        bias=st0, accum_out=st1), waits=[mx], mark=True)
                    rd = S.op("vector", lambda e_, st1=st1, st2=st2: e_.reciprocal(st2, st1),
                              waits=[ex], mark=True)
                    nm = S.op("vector", lambda e_, klen=klen, pe_b=pe_b, pn_b=pn_b, st2=st2: e_.tensor_scalar(
                        pn_b[:, 0:klen], pe_b[:, 0:klen], st2, None,
                        ALU.mult), waits=[rd], mark=True)
                    for kt2 in range(dt_tile + 1):
                        psl = qp * 2 + kt2 % 2
                        tsl = B.tb[kt2 % 2][:, 64 + qp * 128: 192 + qp * 128]
                        tp = S.op("tensor", lambda e_, kt2=kt2, tsl=tsl, pn_b=pn_b: e_.transpose(
                            tsl, pn_b[:, kt2 * 128:(kt2 + 1) * 128], B.ident_bf),
                            waits=[nm, prev_pc[psl]], mark=True)
                        cp = S.op("vector", lambda e_, psl=psl, tsl=tsl: e_.tensor_copy(
                            pT[psl], tsl), waits=[tp, av_slot[psl]], mark=True)
                        prev_pc[psl] = cp
                        mm = S.op("tensor", lambda e_, kt2=kt2, dt_tile=dt_tile, psl=psl, qp=qp: e_.matmul(
                            B.pb[4 + qp][0:64, 0:128], v_tm[:, kt2 * 64:(kt2 + 1) * 64],
                            pT[psl], start=(kt2 == 0), stop=(kt2 == dt_tile)),
                            waits=[cp, vts[kt2], prev_avev[qp]], mark=True)
                        av_slot[psl] = mm
                    oev = S.op("vector", lambda e_, qt=qt, hp=hp, row0=row0, qp=qp: e_.tensor_copy(
                        og[row0:row0 + 64, hp * 2048 + qt * 128: hp * 2048 + (qt + 1) * 128],
                        B.pb[4 + qp][0:64, 0:128]), waits=[mm], mark=True)
                    prev_avev[qp] = oev
            if b == 0 and "dbgog" in aps:
                S.full_barrier()
                S.dma("sync", aps["dbgog" if half == 0 else "dbgog2"], og)
            lwo = S.dma("gpsimd", wslot[:, 0:4096].rearrange("p (t n) -> p t n", t=4),
                        aps["woT_bf"][half * 512:(half + 1) * 512, :]
                        .rearrange("(t p) n -> p t n", p=128),
                        waits=[last_proj_mm, last_wo_mm])
            prev_ao = {0: None, 1: None}
            prev_aodma = {0: None, 1: None}
            for dout in range(DT):
                for grp in range(4):
                    bank = B.pb[2 + (dout * 4 + grp) % 2]
                    for kt in range(4):
                        mm = S.op("tensor", lambda e_, dout=dout, grp=grp, kt=kt, bank=bank: e_.matmul(
                            bank[:, 0:512],
                            wslot[:, kt * 1024 + dout * 128: kt * 1024 + (dout + 1) * 128],
                            og[:, kt * 2048 + grp * 512: kt * 2048 + (grp + 1) * 512],
                            start=(kt == 0), stop=(kt == 3)),
                            waits=[lwo, prev_ao[(dout * 4 + grp) % 2],
                                   ("e", "vector", S.ecnt["vector"])],
                            mark=(kt == 3))
                    last_wo_mm = mm
                    par = (dout * 4 + grp) % 2
                    aev = S.op("vector", lambda e_, par=par, bank=bank: e_.tensor_copy(
                        ao[par], bank[:, 0:512]), waits=[mm, prev_aodma[par]], mark=True)
                    prev_ao[par] = aev
                    last_wo_aev = aev
                    kw = {} if half == 0 else {"accum_op": ALU.add}
                    ad = S.dma("gpsimd", fm_view(aps["a1T"])[:, dout,
                               b * SEQ + grp * 512: b * SEQ + (grp + 1) * 512],
                               ao[par], waits=[aev], **kw)
                    prev_aodma[par] = ad
    S.full_barrier()


def emit_l0moe(nc, S, B, aps):
    af, ab = B.af, B.ab
    w1sb = ab[0:128, 0:32768]
    w2sb = ab[0:128, 32768:65536]
    hg = ab[0:128, 65536:81920]
    xng = ab[0:128, 81920:86016]
    fo = [af[0:128, 0:512], af[0:128, 512:1024]]
    ssc = af[0:128, 1024:1536]
    srow = af[0:1, 1536:5632]
    b1c = af[0:128, 5632:5664]
    b2c = af[0:128, 5664:5672]

    last_mm1 = None
    last_mm2 = None
    last_silu = None
    last_fev = None
    last_bcast = None
    for e in range(8):
        lw1 = S.dma("gpsimd", w1sb.rearrange("p (t n) -> p t n", t=8),
                    aps["L0_ffn_W1"][e].rearrange("(t p) n -> p t n", p=128),
                    waits=[last_mm1])
        lw2 = S.dma("gpsimd", w2sb.rearrange("p (t n) -> p t n", t=32),
                    aps["L0_ffn_W2"][e].rearrange("(t p) n -> p t n", p=128),
                    waits=[last_mm2])
        lb1 = S.dma("sync", b1c, aps["b1c"][e], waits=[last_silu])
        lb2 = S.dma("sync", b2c, aps["b2c"][e], waits=[last_fev])
        lsr = S.dma("sync", srow, aps["selsc"][e:e + 1, :], waits=[last_bcast])
        prev_silu = {0: None, 1: None}
        prev_fev = {0: None, 1: None}
        prev_fdma = {0: None, 1: None}
        prev_cssc = None
        for blk in range(8):
            c0 = blk * 512
            lx = S.dma("sync" if blk % 2 == 0 else "scalar",
                       xng.rearrange("p (t c) -> p t c", t=8),
                       fm_view(aps["xn2T"])[:, :, c0:c0 + 512],
                       waits=[last_mm1])
            mbc = S.op("tensor", lambda e_, c0=c0: e_.matmul(
                B.pb[4][0:128, 0:512], B.ones_row, srow[0:1, c0:c0 + 512],
                start=True, stop=True), waits=[lsr, prev_cssc], mark=True)
            last_bcast = mbc
            cssc = S.op("vector", lambda e_: e_.tensor_copy(ssc, B.pb[4][0:128, 0:512]),
                        waits=[mbc], mark=True)
            prev_cssc = cssc
            silu_toks = []
            for ft in range(32):
                bank = B.pb[ft % 2]
                for kt in range(DT):
                    mm = S.op("tensor", lambda e_, ft=ft, kt=kt, bank=bank: e_.matmul(
                        bank[:, 0:512],
                        w1sb[:, kt * 4096 + ft * 128: kt * 4096 + (ft + 1) * 128],
                        xng[:, kt * 512:(kt + 1) * 512],
                        start=(kt == 0), stop=(kt == DT - 1)),
                        waits=[lw1, lx, prev_silu[ft % 2]],
                        mark=(kt == DT - 1))
                last_mm1 = mm
                sil = S.op("scalar", lambda e_, ft=ft, bank=bank: e_.activation(
                    hg[:, ft * 512:(ft + 1) * 512], bank[:, 0:512], AF.Silu,
                    bias=b1c[:, ft:ft + 1]), waits=[mm, lb1], mark=True)
                prev_silu[ft % 2] = sil
                silu_toks.append(sil)
                last_silu = sil
            for ot in range(DT):
                bank = B.pb[2 + ot % 2]
                for kt in range(32):
                    mm = S.op("tensor", lambda e_, ot=ot, kt=kt, bank=bank: e_.matmul(
                        bank[:, 0:512],
                        w2sb[:, kt * 1024 + ot * 128: kt * 1024 + (ot + 1) * 128],
                        hg[:, kt * 512:(kt + 1) * 512],
                        start=(kt == 0), stop=(kt == 31)),
                        waits=[lw2, silu_toks[31], prev_fev[ot % 2]],
                        mark=(kt == 31))
                last_mm2 = mm
                fev = S.op("vector", lambda e_, ot=ot, bank=bank: e_.scalar_tensor_tensor(
                    fo[ot % 2], bank[:, 0:512], b2c[:, ot:ot + 1], ssc,
                    ALU.add, ALU.mult),
                    waits=[mm, lb2, cssc, prev_fdma[ot % 2]], mark=True)
                last_fev = fev
                prev_fev[ot % 2] = fev
                kw = {"accum_op": ALU.add} if e > 0 else {}
                fd = S.dma("gpsimd", fm_view(aps["f0T"])[:, ot, c0:c0 + 512],
                           fo[ot % 2], waits=[fev], **kw)
                prev_fdma[ot % 2] = fd
    S.full_barrier()


def emit_resln(nc, S, B, aps, xin, ain, fin, xout, xnout):
    af, ab = B.af, B.ab
    GT = 256
    xa = af[0:128, 0:2048]
    xb = af[0:128, 2048:4096]
    xd = af[0:128, 4096:6144]
    xn1 = ab[0:128, 0:2048]
    r = lambda i: af[0:1, 6144 + i * 256: 6144 + (i + 1) * 256]
    for g in range(T // GT):
        c0 = g * GT
        la = S.dma("sync", xa.rearrange("p (t c) -> p t c", t=8),
                   fm_view(aps[xin])[:, :, c0:c0 + GT])
        lb = S.dma("scalar", xb.rearrange("p (t c) -> p t c", t=8),
                   fm_view(aps[ain])[:, :, c0:c0 + GT])
        adds1 = []
        for kt in range(DT):
            o = kt * GT
            a1 = S.op("vector", lambda e, o=o: e.tensor_add(
                xd[:, o:o + GT], xa[:, o:o + GT], xb[:, o:o + GT]),
                waits=[la, lb], mark=True)
            adds1.append(a1)
        lc = S.dma("gpsimd", xb.rearrange("p (t c) -> p t c", t=8),
                   fm_view(aps[fin])[:, :, c0:c0 + GT], waits=adds1)
        adds = []
        for kt in range(DT):
            o = kt * GT
            a2 = S.op("vector", lambda e, o=o: e.tensor_add(
                xa[:, o:o + GT], xd[:, o:o + GT], xb[:, o:o + GT]),
                waits=[lc], mark=True)
            adds.append(a2)
        for kt in range(DT):
            o = kt * GT
            sqk = S.op("scalar", lambda e, o=o: e.activation(
                xb[:, o:o + GT], xa[:, o:o + GT], AF.Square),
                waits=[adds[kt]], mark=True)
            S.op("tensor", lambda e, o=o, kt=kt: e.matmul(
                B.pb[2][0:1, 0:GT], B.ones_col, xa[:, o:o + GT],
                start=(kt == 0), stop=(kt == DT - 1)), waits=[adds[kt]])
            mS2 = S.op("tensor", lambda e, o=o, kt=kt: e.matmul(
                B.pb[3][0:1, 0:GT], B.ones_col, xb[:, o:o + GT],
                start=(kt == 0), stop=(kt == DT - 1)), waits=[sqk],
                mark=(kt == DT - 1))
        S.op("vector", lambda e: e.tensor_scalar(r(0), B.pb[2][0:1, 0:GT],
                                                 1.0 / D, None, ALU.mult), waits=[mS2])
        S.op("vector", lambda e: e.tensor_mul(r(1), r(0), r(0)))
        S.op("vector", lambda e: e.tensor_scalar(r(2), B.pb[3][0:1, 0:GT],
                                                 1.0 / D, None, ALU.mult))
        vv = S.op("vector", lambda e: e.tensor_sub(r(3), r(2), r(1)), mark=True)
        ln = S.op("scalar", lambda e: e.activation(r(2), r(3), AF.Ln,
                                                   bias=B.cc[0:1, 0:1]),
                  waits=[vv], mark=True)
        ist = S.op("scalar", lambda e: e.activation(r(4), r(2), AF.Exp,
                                                    scale=-0.5), mark=True)
        mb = S.op("tensor", lambda e: e.matmul(
            B.pb[4][0:128, 0:GT], B.ones_row, r(0), start=True, stop=True),
            waits=[vv], mark=True)
        ib = S.op("tensor", lambda e: e.matmul(
            B.pb[5][0:128, 0:GT], B.ones_row, r(4), start=True, stop=True),
            waits=[ist], mark=True)
        outs = []
        for kt in range(DT):
            o = kt * GT
            S.op("vector", lambda e, o=o: e.tensor_sub(
                xd[:, o:o + GT], xa[:, o:o + GT], B.pb[4][0:128, 0:GT]),
                waits=[mb])
            nx = S.op("vector", lambda e, o=o: e.tensor_mul(
                xn1[:, o:o + GT], xd[:, o:o + GT], B.pb[5][0:128, 0:GT]),
                waits=[ib], mark=True)
            outs.append(nx)
        S.dma("sync", fm_view(aps[xout])[:, :, c0:c0 + GT],
              xa.rearrange("p (t c) -> p t c", t=8), waits=adds)
        S.dma("gpsimd", fm_view(aps[xnout])[:, :, c0:c0 + GT],
              xn1.rearrange("p (t c) -> p t c", t=8), waits=outs)
        S.full_barrier()


def emit_egress(nc, S, B, aps):
    af = B.af
    GT = 256
    o_rows = aps["out"].rearrange("b n d -> (b n) d")
    xa = af[0:128, 0:2048]
    xb = af[0:128, 2048:4096]
    xd = af[0:128, 4096:6144]
    otile = af[0:128, 6144:7168]
    for g in range(T // GT):
        c0 = g * GT
        la = S.dma("sync", xa.rearrange("p (t c) -> p t c", t=8),
                   fm_view(aps["x1T"])[:, :, c0:c0 + GT])
        lb = S.dma("scalar", xb.rearrange("p (t c) -> p t c", t=8),
                   fm_view(aps["a1T"])[:, :, c0:c0 + GT])
        adds1 = []
        for kt in range(DT):
            o = kt * GT
            a1 = S.op("vector", lambda e, o=o: e.tensor_add(
                xd[:, o:o + GT], xa[:, o:o + GT], xb[:, o:o + GT]),
                waits=[la, lb], mark=True)
            adds1.append(a1)
        lc = S.dma("gpsimd", xb.rearrange("p (t c) -> p t c", t=8),
                   fm_view(aps["f1T"])[:, :, c0:c0 + GT], waits=adds1)
        adds = []
        for kt in range(DT):
            o = kt * GT
            a2 = S.op("vector", lambda e, o=o: e.tensor_add(
                xa[:, o:o + GT], xd[:, o:o + GT], xb[:, o:o + GT]),
                waits=[lc], mark=True)
            adds.append(a2)
        prev_cp = {0: None, 1: None}
        for s in range(GT // 128):
            cps = []
            for kt in range(DT):
                pslot = B.pb[(kt % 2) + 2][:, 0:128]
                t1 = S.op("tensor", lambda e, kt=kt, s=s, pslot=pslot: e.transpose(
                    pslot, xa[:, kt * GT + s * 128: kt * GT + s * 128 + 128],
                    B.ident_f), waits=[adds[kt], prev_cp[kt % 2]], mark=True)
                c1 = S.op("vector", lambda e, kt=kt, pslot=pslot: e.tensor_copy(
                    otile[:, kt * 128:(kt + 1) * 128], pslot),
                    waits=[t1], mark=True)
                prev_cp[kt % 2] = c1
                cps.append(c1)
            dto = S.dma("sync", o_rows[(g * 2 + s) * 128:(g * 2 + s) * 128 + 128, :],
                        otile, waits=cps)
            prev_cp = {0: dto, 1: dto}
        S.full_barrier()


def emit_l1ffn(nc, S, B, aps):
    af, ab = B.af, B.ab
    GT = 512
    u1 = ab[0:128, 0:2048]
    v1 = ab[0:128, 2048:10240]
    u2 = ab[0:128, 10240:18432]
    v2 = ab[0:128, 18432:20480]
    w0 = S.dma("gpsimd", u1.rearrange("p (t n) -> p t n", t=8),
               aps["L1_ffn_U1"].rearrange("(t p) n -> p t n", p=128))
    w1 = S.dma("gpsimd", v1.rearrange("p (t n) -> p t n", t=2),
               aps["L1_ffn_V1"].rearrange("(t p) n -> p t n", p=128))
    w2 = S.dma("gpsimd", u2.rearrange("p (t n) -> p t n", t=32),
               aps["L1_ffn_U2"].rearrange("(t p) n -> p t n", p=128))
    w3 = S.dma("gpsimd", v2.rearrange("p (t n) -> p t n", t=2),
               aps["L1_ffn_V2"].rearrange("(t p) n -> p t n", p=128))
    b1c = af[0:128, 0:32]
    b2c = af[0:128, 32:40]
    wb1 = S.dma("sync", b1c, aps["l1b1c"])
    wb2 = S.dma("sync", b2c, aps["l1b2c"])
    xng = ab[0:128, 20480:24576]
    y1 = ab[0:128, 24576:25600]
    hg = ab[0:128, 25600:41984]
    qg = ab[0:128, 41984:43008]
    t0 = af[0:128, 64:576]
    sqc = af[0:128, 576:1088]
    cub = af[0:128, 1088:1600]
    inn = af[0:128, 1600:2112]
    th = af[0:128, 2112:2624]
    tt_ = af[0:128, 2624:3136]
    fo = af[0:128, 3136:3648]
    S.full_barrier()
    C1 = 0.044715
    C2 = 0.7978845608028654
    for g in range(T // GT):
        c0 = g * GT
        lx = S.dma("sync", xng.rearrange("p (t c) -> p t c", t=8),
                   fm_view(aps["xn1T"])[:, :, c0:c0 + GT])
        for mt in range(2):
            for kt in range(DT):
                mm = S.op("tensor", lambda e, mt=mt, kt=kt: e.matmul(
                    B.pb[0][:, 0:GT], u1[:, kt * 256 + mt * 128: kt * 256 + mt * 128 + 128],
                    xng[:, kt * GT:(kt + 1) * GT],
                    start=(kt == 0), stop=(kt == DT - 1)),
                    waits=[lx, w0, ("e", "vector", S.ecnt["vector"])],
                    mark=(kt == DT - 1))
            S.op("vector", lambda e, mt=mt: e.tensor_copy(
                y1[:, mt * GT:(mt + 1) * GT], B.pb[0][:, 0:GT]),
                waits=[mm], mark=True)
        hts = []
        for ft in range(32):
            for kt in range(2):
                mm = S.op("tensor", lambda e, ft=ft, kt=kt: e.matmul(
                    B.pb[1][:, 0:GT], v1[:, kt * 4096 + ft * 128: kt * 4096 + ft * 128 + 128],
                    y1[:, kt * GT:(kt + 1) * GT],
                    start=(kt == 0), stop=(kt == 1)),
                    waits=[w1, ("e", "vector", S.ecnt["vector"])],
                    mark=(kt == 1))
            bc = S.op("vector", lambda e, ft=ft: e.tensor_scalar(
                t0, B.pb[1][:, 0:GT], b1c[:, ft:ft + 1], None, ALU.add),
                waits=[mm, wb1], mark=True)
            S.op("scalar", lambda e: e.activation(sqc, t0, AF.Square))
            S.op("vector", lambda e: e.tensor_mul(cub, sqc, t0),
                 waits=[("e", "scalar", S.ecnt["scalar"])])
            S.op("vector", lambda e: e.scalar_tensor_tensor(
                inn, cub, C1, t0, ALU.mult, ALU.add))
            tha = S.op("scalar", lambda e: e.activation(th, inn, AF.Tanh, scale=C2),
                       waits=[("e", "vector", S.ecnt["vector"])], mark=True)
            S.op("vector", lambda e: e.tensor_scalar(tt_, th, 0.5, 0.5,
                                                     ALU.mult, ALU.add), waits=[tha])
            hv = S.op("vector", lambda e, ft=ft: e.tensor_mul(
                hg[:, ft * GT:(ft + 1) * GT], t0, tt_), mark=True)
            hts.append(hv)
        for mt in range(2):
            for kt in range(32):
                mm = S.op("tensor", lambda e, mt=mt, kt=kt: e.matmul(
                    B.pb[2][:, 0:GT], u2[:, kt * 256 + mt * 128: kt * 256 + mt * 128 + 128],
                    hg[:, kt * GT:(kt + 1) * GT],
                    start=(kt == 0), stop=(kt == 31)),
                    waits=[w2, hts[kt], ("e", "vector", S.ecnt["vector"])],
                    mark=(kt == 31))
            S.op("vector", lambda e, mt=mt: e.tensor_copy(
                qg[:, mt * GT:(mt + 1) * GT], B.pb[2][:, 0:GT]),
                waits=[mm], mark=True)
        prev_fc = None
        prev_fd = None
        for ot in range(DT):
            for kt in range(2):
                mm = S.op("tensor", lambda e, ot=ot, kt=kt: e.matmul(
                    B.pb[3][:, 0:GT], v2[:, kt * 1024 + ot * 128: kt * 1024 + ot * 128 + 128],
                    qg[:, kt * GT:(kt + 1) * GT],
                    start=(kt == 0), stop=(kt == 1)),
                    waits=[w3, ("e", "vector", S.ecnt["vector"]), prev_fc],
                    mark=(kt == 1))
            fc = S.op("vector", lambda e, ot=ot: e.tensor_scalar(
                fo, B.pb[3][:, 0:GT], b2c[:, ot:ot + 1], None, ALU.add),
                waits=[mm, wb2, prev_fd], mark=True)
            prev_fc = fc
            prev_fd = S.dma("sync", fm_view(aps["f1T"])[:, ot, c0:c0 + GT], fo, waits=[fc])
        S.full_barrier()


def catalog():
    return {
        "x_in": ([BPC, SEQ, D], F32),
        "out": ([BPC, SEQ, D], F32),
        "wo_sc_bf": ([D, D], BF16),
        "wqT_bf": ([D, D], BF16),
        "wkT_bf": ([D, D], BF16),
        "wvT_bf": ([D, D], BF16),
        "woT_bf": ([D, D], BF16),
        "gate_wTf": ([D, 8], F32),
        "gsum": ([1, 8], F32),
        "b1c": ([8, 128, 32], F32),
        "b2c": ([8, 128, 8], F32),
        "l1b1c": ([128, 32], F32),
        "l1b2c": ([128, 8], F32),
        "rmsw_c": ([128, 8], F32),
        "dbg_nwc": ([128, 8], F32),
        "dbgq": ([128, 8192], BF16),
        "dbgk": ([128, 8192], BF16),
        "dbgv": ([128, 8192], BF16),
        "dbgog": ([128, 8192], BF16),
        "dbgog2": ([128, 8192], BF16),
        "x0T": ([D, T], F32),
        "xn0T": ([D, T], BF16),
        "xn2T": ([D, T], BF16),
        "selsc": ([16, T], F32),
        "a0T": ([D, T], F32),
        "f0T": ([D, T], F32),
        "x1T": ([D, T], F32),
        "xn1T": ([D, T], BF16),
        "a1T": ([D, T], F32),
        "f1T": ([D, T], F32),
    }


WEIGHT_SHAPES = {
    "L0_attn_Wq": [D, D], "L0_attn_Wk": [D, D], "L0_attn_Wv": [D, D],
    "L0_attn_Wf1": [D, 128], "L0_attn_Wf2": [128, 512],
    "L0_attn_Wg1": [D, 128], "L0_attn_Wg2": [128, D],
    "L0_attn_Wo": [D, D], "L0_attn_norm_w": [D],
    "L0_ffn_rms_w": [D], "L0_ffn_gate_w": [8, D],
    "L0_ffn_W1": [8, D, 4096], "L0_ffn_b1": [8, 4096],
    "L0_ffn_W2": [8, 4096, D], "L0_ffn_b2": [8, D],
    "L1_attn_Wq": [D, D], "L1_attn_Wk": [D, D], "L1_attn_Wv": [D, D],
    "L1_attn_Wo": [D, D],
    "L1_ffn_U1": [D, 256], "L1_ffn_V1": [256, 4096], "L1_ffn_b1": [4096],
    "L1_ffn_U2": [4096, 256], "L1_ffn_V2": [256, D], "L1_ffn_b2": [D],
}


def build_launch(phase_fns, ext_in, ext_out, weights_needed):
    nc = bass.Bass("TRN2", target_bir_lowering=False)
    cat = catalog()
    aps = {}
    for name in weights_needed:
        aps[name] = nc.dram_tensor(name, WEIGHT_SHAPES[name], F32,
                                   kind="ExternalInput").ap()
    for name in ext_in:
        sh, dt = cat[name]
        aps[name] = nc.dram_tensor(name, sh, dt, kind="ExternalInput").ap()
    for name in ext_out:
        sh, dt = cat[name]
        aps[name] = nc.dram_tensor(name, sh, dt, kind="ExternalOutput").ap()
    S = Sched(nc)
    B = Bufs(nc)
    emit_const(nc, S, B)
    S.full_barrier()
    for fn in phase_fns:
        fn(nc, S, B, aps)
        S.full_barrier()
    S.finalize()
    return nc


LAUNCHES = [
    ("prep", [emit_prep], [],
     ["wo_sc_bf", "gate_wTf", "gsum", "b1c", "b2c", "l1b1c", "l1b2c", "rmsw_c",
      "wqT_bf", "wkT_bf", "wvT_bf", "woT_bf"],
     ["L0_attn_norm_w", "L0_ffn_rms_w", "L0_ffn_b1", "L0_ffn_b2",
      "L1_ffn_b1", "L1_ffn_b2", "L0_ffn_gate_w", "L0_attn_Wo",
      "L1_attn_Wq", "L1_attn_Wk", "L1_attn_Wv", "L1_attn_Wo"]),
    ("ingest", [emit_ingest], ["x_in", "rmsw_c", "gate_wTf", "gsum"],
     ["x0T", "xn0T", "xn2T", "selsc"], []),
    ("l0attn", [emit_l0attn], ["xn0T", "wo_sc_bf"], ["a0T"],
     ["L0_attn_Wq", "L0_attn_Wk", "L0_attn_Wv", "L0_attn_Wf1", "L0_attn_Wf2",
      "L0_attn_Wg1", "L0_attn_Wg2"]),
    ("l1attn", [emit_l1attn], ["xn1T", "wqT_bf", "wkT_bf", "wvT_bf", "woT_bf"],
     ["a1T"], []),
    ("l0moe", [emit_l0moe], ["xn2T", "selsc", "b1c", "b2c"], ["f0T"],
     ["L0_ffn_W1", "L0_ffn_W2"]),
    ("resln", [lambda nc, S, B, aps: emit_resln(nc, S, B, aps, "x0T", "a0T",
                                                "f0T", "x1T", "xn1T")],
     ["x0T", "a0T", "f0T"], ["x1T", "xn1T"], []),
    ("l1ffn", [emit_l1ffn], ["xn1T", "l1b1c", "l1b2c"], ["f1T"],
     ["L1_ffn_U1", "L1_ffn_V1", "L1_ffn_U2", "L1_ffn_V2"]),
    ("egress", [emit_egress], ["x1T", "a1T", "f1T"], ["out"], []),
]


FUSED_ORDER = ["prep", "ingest", "l0attn", "l0moe", "resln", "l1attn",
               "l1ffn", "egress"]


def build_fused():
    nc = bass.Bass("TRN2", target_bir_lowering=False)
    cat = catalog()
    aps = {}
    for name, sh in WEIGHT_SHAPES.items():
        aps[name] = nc.dram_tensor(name, sh, F32, kind="ExternalInput").ap()
    sh, dt = cat["x_in"]
    aps["x_in"] = nc.dram_tensor("x_in", sh, dt, kind="ExternalInput").ap()
    sh, dt = cat["out"]
    aps["out"] = nc.dram_tensor("out", sh, dt, kind="ExternalOutput").ap()
    for name, (sh, dt) in cat.items():
        if name in ("x_in", "out") or name.startswith("dbg"):
            continue
        aps[name] = nc.dram_tensor(name, sh, dt, kind="Internal").ap()
    S = Sched(nc)
    B = Bufs(nc)
    emit_const(nc, S, B)
    S.full_barrier()
    by_name = {l[0]: l for l in LAUNCHES}
    for name in FUSED_ORDER:
        for fn in by_name[name][1]:
            fn(nc, S, B, aps)
            S.full_barrier()
    S.finalize()
    return nc


_FUSED_NC = None
_NCS = {}


def _get_nc(name):
    if name not in _NCS:
        for n, phases, ext_in, ext_out, weights in LAUNCHES:
            if n == name:
                _NCS[name] = (build_launch(phases, ext_in, ext_out, weights),
                              ext_in, ext_out, weights)
                break
    return _NCS[name]


def kernel(**inputs):
    global _FUSED_NC
    x = np.ascontiguousarray(np.asarray(inputs["L0_attn_x"]))
    weights_np = {k: np.ascontiguousarray(np.asarray(v))
                  for k, v in inputs.items() if k != "L0_attn_x"}
    if _FUSED_NC is None:
        _FUSED_NC = build_fused()
    in_maps = []
    for c in range(NCORES):
        im = dict(weights_np)
        im["x_in"] = np.ascontiguousarray(x[c * BPC:(c + 1) * BPC])
        in_maps.append(im)
    res = run_bass_kernel_spmd(_FUSED_NC, in_maps, core_ids=list(range(NCORES)))
    return np.concatenate([np.asarray(res.results[c]["out"])
                           for c in range(NCORES)], axis=0)


# version 06
# speedup vs baseline: 1.1945x; 1.0147x over previous
import numpy as np
import ml_dtypes
import concourse.bass as bass
import concourse.mybir as mybir
from concourse.bass_utils import run_bass_kernel_spmd

F32 = mybir.dt.float32
BF16 = mybir.dt.bfloat16
AF = mybir.ActivationFunctionType
ALU = mybir.AluOpType
AX = mybir.AxisListType

NCORES = 8
BPC = 2
SEQ = 2048
T = BPC * SEQ
D = 1024
DT = D // 128
NEG = -1.0e30

ENGINES = ("tensor", "vector", "scalar", "gpsimd", "sync")
DMA_QUEUES = ("sync", "scalar", "gpsimd")


class Sched:

    def __init__(self, nc):
        self.nc = nc
        self.plan = {e: [] for e in ENGINES}
        self.esem = {e: nc.alloc_semaphore(f"es_{e}") for e in ENGINES}
        self.dsem = {q: nc.alloc_semaphore(f"ds_{q}") for q in DMA_QUEUES}
        self.ecnt = {e: 0 for e in ENGINES}
        self.dcnt = {q: 0 for q in DMA_QUEUES}

    SELF_SERIAL = ("vector", "scalar", "gpsimd")

    def op(self, eng, fn, waits=(), mark=False):
        waits = [w for w in waits if w is not None]
        if eng in self.SELF_SERIAL:
            if self.ecnt[eng]:
                waits.append(("e", eng, self.ecnt[eng]))
            mark = True
        tok = None
        if mark:
            self.ecnt[eng] += 1
            tok = ("e", eng, self.ecnt[eng])
        self.plan[eng].append((fn, tuple(waits),
                               ("e", eng) if mark else None))
        return tok

    def dma(self, q, out, in_, waits=(), **kw):
        self.dcnt[q] += 16
        tok = ("d", q, self.dcnt[q])
        def fn(e, out=out, in_=in_, kw=kw):
            return e.dma_start(out=out, in_=in_, **kw)
        self.plan[q].append((fn, tuple(w for w in waits if w is not None),
                             ("d", q)))
        return tok

    def barrier_toks(self):
        toks = []
        for e in ENGINES:
            if self.ecnt[e]:
                toks.append(("e", e, self.ecnt[e]))
        for q in DMA_QUEUES:
            if self.dcnt[q]:
                toks.append(("d", q, self.dcnt[q]))
        return toks

    def full_barrier(self):
        toks = self.barrier_toks()
        for e in ENGINES:
            self.plan[e].append((None, tuple(toks), None))

    def finalize(self):
        nc = self.nc
        with nc.Block() as block:
            for eng_name in ENGINES:
                def run(eng, eng_name=eng_name):
                    hwm = {}
                    for fn, waits, markkind in self.plan[eng_name]:
                        for kind, name, val in waits:
                            if (kind == "e" and name == eng_name
                                    and eng_name not in self.SELF_SERIAL):
                                continue
                            if val <= 0:
                                continue
                            sem = self.esem[name] if kind == "e" else self.dsem[name]
                            key = (kind, name)
                            if hwm.get(key, 0) < val:
                                hwm[key] = val
                                eng.wait_ge(sem, val)
                        if fn is None:
                            continue
                        ins = fn(eng)
                        if markkind is not None:
                            kind, name = markkind
                            if kind == "e":
                                ins.then_inc(self.esem[name], 1)
                            else:
                                ins.then_inc(self.dsem[name], 16)
                    if eng_name == "sync":
                        for q in DMA_QUEUES:
                            if self.dcnt[q]:
                                eng.wait_ge(self.dsem[q], self.dcnt[q])
                        for e2 in ENGINES:
                            if e2 != "sync" and self.ecnt[e2]:
                                eng.wait_ge(self.esem[e2], self.ecnt[e2])
                getattr(block, eng_name)(run)


class Bufs:

    def __init__(self, nc):
        ap = lambda h: h.ap()
        self.ab = ap(nc.alloc_sbuf_tensor("ab", [128, 88000], BF16))
        self.af = ap(nc.alloc_sbuf_tensor("af", [128, 8600], F32))
        self.ident_bf = ap(nc.alloc_sbuf_tensor("ident_bf", [128, 128], BF16))
        self.ident_f = ap(nc.alloc_sbuf_tensor("ident_f", [128, 128], F32))
        self.triu64 = ap(nc.alloc_sbuf_tensor("triu64", [64, 64], F32))
        self.causal = ap(nc.alloc_sbuf_tensor("causal", [128, 128], F32))
        self.ones_row = ap(nc.alloc_sbuf_tensor("ones_row", [1, 128], F32))
        self.ones_col = ap(nc.alloc_sbuf_tensor("ones_col", [128, 1], F32))
        self.cc = ap(nc.alloc_sbuf_tensor("cc", [128, 8], F32))
        self.pb = [ap(nc.alloc_psum_tensor(f"pb{i}", [128, 512], F32)) for i in range(6)]
        self.tb = [ap(nc.alloc_psum_tensor(f"tb{i}", [128, 1024], BF16)) for i in range(2)]


def emit_const(nc, S, B):
    S.op("gpsimd", lambda e: e.memset(B.ident_bf, 0.0))
    S.op("gpsimd", lambda e: e.affine_select(
        out=B.ident_bf, in_=B.ident_bf, compare_op=ALU.not_equal, fill=1.0,
        base=0, pattern=[[-1, 128]], channel_multiplier=1))
    S.op("gpsimd", lambda e: e.memset(B.ident_f, 0.0))
    S.op("gpsimd", lambda e: e.affine_select(
        out=B.ident_f, in_=B.ident_f, compare_op=ALU.not_equal, fill=1.0,
        base=0, pattern=[[-1, 128]], channel_multiplier=1))
    S.op("gpsimd", lambda e: e.memset(B.triu64, 0.0))
    S.op("gpsimd", lambda e: e.affine_select(
        out=B.triu64, in_=B.triu64, compare_op=ALU.is_gt, fill=1.0,
        base=0, pattern=[[-1, 64]], channel_multiplier=1))
    S.op("gpsimd", lambda e: e.memset(B.causal, 0.0))
    S.op("gpsimd", lambda e: e.affine_select(
        out=B.causal, in_=B.causal, compare_op=ALU.is_ge, fill=NEG,
        base=0, pattern=[[-1, 128]], channel_multiplier=1))
    S.op("gpsimd", lambda e: e.memset(B.ones_row, 1.0))
    S.op("gpsimd", lambda e: e.memset(B.cc[:, 0:1], 1e-5))
    S.op("gpsimd", lambda e: e.memset(B.cc[:, 1:2], 1e-6))
    S.op("gpsimd", lambda e: e.memset(B.cc[:, 2:3], 1.0))
    S.op("gpsimd", lambda e: e.memset(B.cc[:, 3:4], 128 * 1e-5))
    S.op("gpsimd", lambda e: e.memset(B.ones_col, 1.0), mark=True)


def emit_prep(nc, S, B, aps):
    af, ab = B.af, B.ab

    def vec_to_cols(src_ap, n, dst_sb):
        stg = af[0:n, 0:128]
        ld = S.dma("sync", stg, src_ap.rearrange("(a b) -> a b", b=128))
        tt = S.op("tensor", lambda e: e.transpose(
            B.pb[0][:, 0:n], stg, B.ident_f[0:n, 0:n]), waits=[ld], mark=True)
        cv = S.op("vector", lambda e: e.tensor_copy(dst_sb, B.pb[0][:, 0:n]),
                  waits=[tt], mark=True)
        S.full_barrier()
        return cv

    nwc = af[0:128, 256:264]
    vec_to_cols(aps["L0_attn_norm_w"], 8, nwc)
    if "dbg_nwc" in aps:
        S.dma("sync", aps["dbg_nwc"], nwc)
        S.full_barrier()
    rwc = af[0:128, 264:272]
    vec_to_cols(aps["L0_ffn_rms_w"], 8, rwc)
    S.dma("sync", aps["rmsw_c"], rwc)
    for ei in range(8):
        dst = af[0:128, 280:312]
        vec_to_cols(aps["L0_ffn_b1"][ei], 32, dst)
        S.dma("sync", aps["b1c"][ei], dst)
        S.full_barrier()
        dst2 = af[0:128, 312:320]
        vec_to_cols(aps["L0_ffn_b2"][ei], 8, dst2)
        S.dma("sync", aps["b2c"][ei], dst2)
        S.full_barrier()
    dst = af[0:128, 280:312]
    vec_to_cols(aps["L1_ffn_b1"], 32, dst)
    S.dma("sync", aps["l1b1c"], dst)
    S.full_barrier()
    dst2 = af[0:128, 312:320]
    vec_to_cols(aps["L1_ffn_b2"], 8, dst2)
    S.dma("sync", aps["l1b2c"], dst2)
    S.full_barrier()

    gsb = af[0:8, 4096:5120]
    g0 = S.dma("sync", gsb, aps["L0_ffn_gate_w"])
    for kt in range(DT):
        tt = S.op("tensor", lambda e, kt=kt: e.transpose(
            B.pb[1][:, kt * 8:(kt + 1) * 8], gsb[:, kt * 128:(kt + 1) * 128],
            B.ident_f[0:8, 0:8]), waits=[g0], mark=(kt == DT - 1))
    gft = af[0:128, 5120:5184]
    for kt in range(DT):
        cg = S.op("vector", lambda e, kt=kt: e.tensor_scalar(
            gft[:, kt * 8:(kt + 1) * 8], B.pb[1][:, kt * 8:(kt + 1) * 8],
            rwc[:, kt:kt + 1], None, ALU.mult), waits=[tt], mark=True)
    for kt in range(DT):
        mg = S.op("tensor", lambda e, kt=kt: e.matmul(
            B.pb[2][0:1, 0:8], B.ones_col, gft[:, kt * 8:(kt + 1) * 8],
            start=(kt == 0), stop=(kt == DT - 1)), waits=[cg], mark=(kt == DT - 1))
    gsum_sb = af[0:1, 5184:5192]
    cgs = S.op("vector", lambda e: e.tensor_copy(gsum_sb, B.pb[2][0:1, 0:8]),
               waits=[mg], mark=True)
    S.dma("sync", aps["gate_wTf"].rearrange("(t p) n -> p t n", p=128),
          gft.rearrange("p (t n) -> p t n", t=8), waits=[cg])
    S.dma("sync", aps["gsum"], gsum_sb, waits=[cgs])
    S.full_barrier()

    for wname, oname in (("L1_attn_Wq", "wqT_bf"), ("L1_attn_Wk", "wkT_bf"),
                         ("L1_attn_Wv", "wvT_bf"), ("L1_attn_Wo", "woT_bf")):
        for rb in range(DT):
            wrow = af[0:128, 1024:2048]
            ld = S.dma("sync", wrow, aps[wname][rb * 128:(rb + 1) * 128, :])
            prev_tc = {0: None, 1: None}
            for ct in range(DT):
                bank = B.pb[ct % 2]
                tt = S.op("tensor", lambda e, ct=ct, bank=bank: e.transpose(
                    bank[:, 0:128], wrow[:, ct * 128:(ct + 1) * 128], B.ident_f),
                    waits=[ld, prev_tc[ct % 2]], mark=True)
                wtile = ab[0:128, ct * 128:(ct + 1) * 128]
                cv = S.op("vector", lambda e, ct=ct, bank=bank, wtile=wtile: e.tensor_copy(
                    wtile, bank[:, 0:128]), waits=[tt], mark=True)
                prev_tc[ct % 2] = cv
                S.dma("scalar", aps[oname][ct * 128:(ct + 1) * 128,
                                           rb * 128:(rb + 1) * 128], wtile,
                      waits=[cv])
            S.full_barrier()

    for kt in range(DT):
        wrow = af[0:128, 1024:2048]
        wbf = ab[0:128, 0:1024]
        ld = S.dma("sync", wrow, aps["L0_attn_Wo"][kt * 128:(kt + 1) * 128, :])
        sc = S.op("vector", lambda e, kt=kt: e.tensor_scalar(
            wbf, wrow, nwc[:, kt:kt + 1], None, ALU.mult), waits=[ld], mark=True)
        S.dma("sync", aps["wo_sc_bf"][kt * 128:(kt + 1) * 128, :], wbf, waits=[sc])
        S.full_barrier()


def emit_ingest(nc, S, B, aps):
    af, ab = B.af, B.ab
    x_rows = aps["x_in"].rearrange("b n d -> (b n) d")

    rwc = af[0:128, 0:8]
    S.dma("sync", rwc, aps["rmsw_c"])
    gwTf = af[0:128, 8192:8256]
    S.dma("sync", gwTf.rearrange("p (t n) -> p t n", t=8),
          aps["gate_wTf"].rearrange("(t p) n -> p t n", p=128))
    gsum_sb = af[0:1, 8256:8264]
    g1 = S.dma("sync", gsum_sb, aps["gsum"])
    mgb = S.op("tensor", lambda e: e.matmul(
        B.pb[4][0:128, 0:8], B.ones_row, gsum_sb, start=True, stop=True),
        waits=[g1], mark=True)
    gsum_bc = af[0:128, 8264:8272]
    S.op("vector", lambda e: e.tensor_copy(gsum_bc, B.pb[4][0:128, 0:8]),
         waits=[mgb], mark=True)
    S.full_barrier()

    xt = af[0:128, 16:1040]
    sq = af[0:128, 1040:2064]
    xnf = af[0:128, 7168:8192]
    st = af[0:128, 2064:2080]
    st16 = af[0:128, 2080:2096]
    lgf = af[0:128, 2096:2112]
    xnt = ab[0:128, 64:1088]
    x2t = ab[0:128, 1088:2112]
    xoT = af[0:128, 2560:6656]
    xnoT = ab[0:128, 2112:6208]
    x2oT = ab[0:128, 6208:10304]
    selT = af[0:16, 6656:7168]

    for tt in range(32):
        gi = tt % 4
        tok_off = gi * 128
        ld = S.dma("sync", xt, x_rows[tt * 128:(tt + 1) * 128, :])
        s1 = S.op("vector", lambda e: e.reduce_sum(st[:, 0:1], xt, axis=AX.X),
                  waits=[ld], mark=True)
        qq = S.op("scalar", lambda e: e.activation(sq, xt, AF.Square),
                  waits=[ld], mark=True)
        s2 = S.op("vector", lambda e: e.reduce_sum(st[:, 1:2], sq, axis=AX.X),
                  waits=[qq], mark=True)
        S.op("vector", lambda e: e.tensor_scalar(st[:, 2:3], st[:, 0:1], 1.0 / D, None, ALU.mult))
        S.op("vector", lambda e: e.tensor_mul(st[:, 7:8], st[:, 2:3], st[:, 2:3]))
        S.op("vector", lambda e: e.tensor_scalar(st[:, 6:7], st[:, 1:2], 1.0 / D, None, ALU.mult))
        vv = S.op("vector", lambda e: e.tensor_sub(st[:, 3:4], st[:, 6:7], st[:, 7:8]),
                  mark=True)
        sd = S.op("scalar", lambda e: e.activation(st[:, 4:5], st[:, 3:4], AF.Ln,
                                                   bias=B.cc[:, 0:1]), waits=[vv], mark=True)
        vi = S.op("scalar", lambda e: e.activation(st[:, 10:11], st[:, 4:5], AF.Exp,
                                                   scale=-0.5), mark=True)
        xn = S.op("vector", lambda e: e.tensor_scalar(
            xnf, xt, st[:, 2:3], st[:, 10:11], ALU.subtract, ALU.mult),
            waits=[vi], mark=True)
        xnc = S.op("vector", lambda e: e.tensor_copy(xnt, xnf), mark=True)
        q2 = S.op("scalar", lambda e: e.activation(sq, xnf, AF.Square),
                  waits=[xn], mark=True)
        ms = S.op("vector", lambda e: e.reduce_sum(st[:, 5:6], sq, axis=AX.X),
                  waits=[q2], mark=True)
        s2d = S.op("scalar", lambda e: e.activation(st[:, 6:7], st[:, 5:6], AF.Ln,
                                                    bias=B.cc[:, 1:2], scale=1.0 / D),
                   waits=[ms], mark=True)
        r2 = S.op("scalar", lambda e: e.activation(st[:, 11:12], st[:, 6:7], AF.Exp,
                                                   scale=-0.5), mark=True)
        x2 = S.op("vector", lambda e: e.tensor_scalar(
            x2t, xnf, st[:, 11:12], None, ALU.mult), waits=[r2], mark=True)

        prev_c1 = {0: None, 1: None}
        for kt in range(DT):
            pslot = B.pb[kt % 2][:, 0:128]
            t1 = S.op("tensor", lambda e, kt=kt, pslot=pslot: e.transpose(
                pslot, xt[:, kt * 128:(kt + 1) * 128], B.ident_f),
                waits=[ld, xn, prev_c1[kt % 2]], mark=True)
            c1 = S.op("vector", lambda e, kt=kt, pslot=pslot, tok_off=tok_off: e.tensor_copy(
                xoT[:, kt * 512 + tok_off: kt * 512 + tok_off + 128], pslot),
                waits=[t1], mark=True)
            prev_c1[kt % 2] = c1
            tslot = B.tb[kt % 2][:, 0:128]
            t2 = S.op("tensor", lambda e, kt=kt, tslot=tslot: e.transpose(
                tslot, xnt[:, kt * 128:(kt + 1) * 128], B.ident_bf),
                waits=[c1, xnc], mark=True)
            c2 = S.op("vector", lambda e, kt=kt, tslot=tslot, tok_off=tok_off: e.tensor_copy(
                xnoT[:, kt * 512 + tok_off: kt * 512 + tok_off + 128], tslot),
                waits=[t2], mark=True)
            t3 = S.op("tensor", lambda e, kt=kt, tslot=tslot: e.transpose(
                tslot, x2t[:, kt * 128:(kt + 1) * 128], B.ident_bf),
                waits=[c2, x2], mark=True)
            c3 = S.op("vector", lambda e, kt=kt, tslot=tslot, tok_off=tok_off: e.tensor_scalar(
                x2oT[:, kt * 512 + tok_off: kt * 512 + tok_off + 128], tslot,
                rwc[:, kt:kt + 1], None, ALU.mult), waits=[t3], mark=True)

        ps_lg = B.pb[2]
        for kt in range(DT):
            mm = S.op("tensor", lambda e, kt=kt, tok_off=tok_off: e.matmul(
                ps_lg[0:128, 0:8],
                xoT[:, kt * 512 + tok_off: kt * 512 + tok_off + 128],
                gwTf[:, kt * 8:(kt + 1) * 8],
                start=(kt == 0), stop=(kt == DT - 1)),
                waits=[c1, c3], mark=(kt == DT - 1))
        v1 = S.op("vector", lambda e: e.tensor_scalar(
            lgf[:, 8:16], gsum_bc, st[:, 2:3], None, ALU.mult), waits=[xn], mark=True)
        v2 = S.op("vector", lambda e: e.tensor_sub(
            lgf[:, 0:8], ps_lg[0:128, 0:8], lgf[:, 8:16]), waits=[mm], mark=True)
        v3 = S.op("vector", lambda e: e.tensor_mul(
            st[:, 12:13], st[:, 10:11], st[:, 11:12]), waits=[r2], mark=True)
        v4 = S.op("vector", lambda e: e.tensor_scalar(
            lgf[:, 0:8], lgf[:, 0:8], st[:, 12:13], None, ALU.mult),
            waits=[v3], mark=True)
        mx = S.op("vector", lambda e: e.reduce_max(
            st[:, 8:9], lgf[:, 0:8], axis=AX.X, negate=True),
            waits=[v4], mark=True)
        ex = S.op("scalar", lambda e: e.activation(
            st16[:, 8:16], lgf[:, 0:8], AF.Exp, bias=st[:, 8:9],
            accum_out=st[:, 9:10]), waits=[mx], mark=True)
        sc = S.op("vector", lambda e: e.reciprocal(st16[:, 8:9], st[:, 9:10]),
                  waits=[ex], mark=True)
        sel = S.op("vector", lambda e: e.tensor_scalar(
            lgf[:, 8:16], lgf[:, 0:8], st[:, 8:9], 0.0, ALU.add, ALU.is_ge),
            waits=[mx, ex], mark=True)
        ssc = S.op("vector", lambda e: e.tensor_scalar(
            st16[:, 0:8], lgf[:, 8:16], st16[:, 8:9], None, ALU.mult),
            waits=[sel, sc], mark=True)
        ts = S.op("tensor", lambda e: e.transpose(
            B.pb[3][0:16, 0:128], st16[:, 0:16], B.ident_f),
            waits=[ssc], mark=True)
        cs = S.op("vector", lambda e, tok_off=tok_off: e.tensor_copy(
            selT[:, tok_off:tok_off + 128], B.pb[3][0:16, 0:128]),
            waits=[ts], mark=True)

        if gi == 3:
            g = tt // 4
            col0 = g * 512
            w = S.barrier_toks()
            S.dma("sync", aps["x0T"].rearrange("(t p) c -> p t c", p=128)[:, :, col0:col0 + 512],
                  xoT.rearrange("p (t c) -> p t c", t=8), waits=w)
            S.dma("gpsimd", aps["xn0T"].rearrange("(t p) c -> p t c", p=128)[:, :, col0:col0 + 512],
                  xnoT.rearrange("p (t c) -> p t c", t=8), waits=w)
            S.dma("gpsimd", aps["xn2T"].rearrange("(t p) c -> p t c", p=128)[:, :, col0:col0 + 512],
                  x2oT.rearrange("p (t c) -> p t c", t=8), waits=w)
            S.dma("scalar", aps["selsc"][:, col0:col0 + 512], selT, waits=w)
        S.full_barrier()


def fm_view(ap_):
    return ap_.rearrange("(t p) c -> p t c", p=128)


DBG_GSA = {"nch": None, "heads": None, "skip_post": False, "skip_wo": False, "skip_fg": False, "skip_qkv": False, "plain_evac": False, "skip_memset": False, "skip_ft": False}


def emit_l0attn(nc, S, B, aps):
    af, ab = B.af, B.ab
    C = 64
    NCH = SEQ // C
    H = 8
    xn_b = ab[0:128, 0:16384]
    og = ab[0:128, 0:16384]
    qb = ab[0:128, 16384:32768]
    kb = ab[0:128, 32768:49152]
    vb = ab[0:128, 49152:65536]
    fT = ab[0:128, 65536:73728]
    t_f = ab[0:128, 73728:75776]
    t_g = ab[0:128, 75776:77824]
    wg2 = ab[0:128, 77824:78848]
    wslot = ab[0:128, 78848:82944]
    stil_fm = ab[0:128, 82944:83200]
    stil_tm = ab[0:64, 83200:83712]
    pm_all = ab[0:64, 83712:84224]
    qt_all = ab[0:64, 84224:84736]
    qtT_all = ab[0:64, 86272:86784]
    stilT_all = ab[0:64, 86784:87296]
    kv_tm = ab[0:64, 78848:80896]
    ogn = ab[0:128, 32768:49152]
    th = af[0:128, 0:512]
    sp = af[0:128, 512:1024]
    s_fm = af[0:128, 1024:1280]
    sig = af[0:128, 1280:1536]
    ep_fm = af[0:128, 1536:1792]
    en_fm = af[0:128, 1792:2048]
    ep_tm = af[0:64, 2048:2560]
    eend = af[0:1, 2560:3072]
    hk = ab[0:128, 84736:85248]
    hv = ab[0:64, 85248:86272]
    tmp1 = af[0:128, 3072:3584]
    tmp2 = af[0:64, 3584:4096]
    ok_all = af[0:64, 4288:4800]
    qe_all = af[0:64, 4800:5312]
    mx8 = af[0:64, 4160:4184]
    ecols = af[0:64, 4184:4192]
    thg_all = af[0:64, 6224:7248]
    thg2 = af[0:64, 7248:8272]
    ogf_all = ab[0:64, 80896:81920]
    sqb = af[0:128, 0:512]
    rrow = af[0:1, 5200:5712]
    rrow2 = af[0:1, 5712:6224]

    lwg2 = S.dma("gpsimd", wg2.rearrange("p (t n) -> p t n", t=1),
                 aps["L0_attn_Wg2"].rearrange("(t p) n -> p t n", p=128))
    last_wo_mm = None
    last_proj_mm = None
    for b in range(DBG_GSA.get("bpc") or BPC):
        lxn = S.dma("sync", xn_b.rearrange("p (t c) -> p t c", t=8),
                    fm_view(aps["xn0T"])[:, :, b * SEQ:(b + 1) * SEQ],
                    waits=[last_wo_mm])
        for wname, dst, silu in (() if DBG_GSA["skip_qkv"] else (("L0_attn_Wq", qb, True),
                                 ("L0_attn_Wk", kb, True),
                                 ("L0_attn_Wv", vb, False))):
            for halfW in range(2):
                lw = S.dma("gpsimd", wslot.rearrange("p (t n) -> p t n", t=8),
                           aps[wname][:, halfW * 512:(halfW + 1) * 512]
                           .rearrange("(t p) n -> p t n", p=128),
                           waits=[last_proj_mm, last_wo_mm,
                                  ("e", "vector", S.ecnt["vector"])])
                prev_ev = {0: None, 1: None}
                for mt4 in range(4):
                    for grp in range(4):
                        bank = B.pb[grp % 2]
                        for kt in range(DT):
                            mm = S.op("tensor", lambda e_, mt4=mt4, grp=grp, kt=kt, bank=bank: e_.matmul(
                                bank[:, 0:512],
                                wslot[:, kt * 512 + mt4 * 128: kt * 512 + (mt4 + 1) * 128],
                                xn_b[:, kt * 2048 + grp * 512: kt * 2048 + (grp + 1) * 512],
                                start=(kt == 0), stop=(kt == DT - 1)),
                                waits=[lw, lxn, prev_ev[grp % 2]],
                                mark=(kt == DT - 1))
                        last_proj_mm = mm
                        o = (halfW * 4 + mt4) * 2048 + grp * 512
                        if silu and not DBG_GSA["plain_evac"]:
                            t1 = S.op("scalar", lambda e_, bank=bank: e_.activation(
                                th, bank[:, 0:512], AF.Tanh, scale=0.5),
                                waits=[mm], mark=True)
                            t2 = S.op("vector", lambda e_: e_.tensor_scalar(
                                sp, th, 0.5, 0.5, ALU.mult, ALU.add),
                                waits=[t1], mark=True)
                            ev = S.op("vector", lambda e_, o=o, bank=bank, dst=dst: e_.tensor_mul(
                                dst[:, o:o + 512], bank[:, 0:512], sp),
                                waits=[t2], mark=True)
                        else:
                            ev = S.op("vector", lambda e_, o=o, bank=bank, dst=dst: e_.tensor_copy(
                                dst[:, o:o + 512], bank[:, 0:512]), waits=[mm], mark=True)
                        prev_ev[grp % 2] = ev
        fg_list = (("L0_attn_Wf1", 128, t_f), ("L0_attn_Wg1", 128, t_g))
        if DBG_GSA.get("fg_ngrp") is not None:
            pass
        if DBG_GSA.get("fg_first_only"):
            fg_list = fg_list[:1]
        for wname, wcols, dst in (() if DBG_GSA["skip_fg"] else fg_list):
            lw = S.dma("gpsimd", wslot[:, 0:1024].rearrange("p (t n) -> p t n", t=8),
                       aps[wname].rearrange("(t p) n -> p t n", p=128),
                       waits=[last_proj_mm])
            prev_ev = {0: None, 1: None}
            for grp in range(DBG_GSA.get("fg_ngrp", 4)):
                bank = B.pb[grp % 2]
                for kt in range(DT):
                    mm = S.op("tensor", lambda e_, grp=grp, kt=kt, bank=bank: e_.matmul(
                        bank[:, 0:512],
                        wslot[:, kt * 128:(kt + 1) * 128],
                        xn_b[:, kt * 2048 + grp * 512: kt * 2048 + (grp + 1) * 512],
                        start=(kt == 0), stop=(kt == DT - 1)),
                        waits=[lw, prev_ev[grp % 2]], mark=(kt == DT - 1))
                last_proj_mm = mm
                ev = S.op("vector", lambda e_, grp=grp, bank=bank, dst=dst: e_.tensor_copy(
                    dst[:, grp * 512:(grp + 1) * 512], bank[:, 0:512]),
                    waits=[mm], mark=True)
                prev_ev[grp % 2] = ev
        if DBG_GSA["skip_fg"]:
            continue
        lw2 = S.dma("gpsimd", wslot[:, 0:512].rearrange("p (t n) -> p t n", t=1),
                    aps["L0_attn_Wf2"].rearrange("(t p) n -> p t n", p=128),
                    waits=[last_proj_mm])
        prev_ev = {0: None, 1: None}
        for mt in range(4 if not DBG_GSA["skip_ft"] else 0):
            for grp in range(4):
                bank = B.pb[grp % 2]
                mm = S.op("tensor", lambda e_, mt=mt, grp=grp, bank=bank: e_.matmul(
                    bank[:, 0:512], wslot[:, mt * 128:(mt + 1) * 128],
                    t_f[:, grp * 512:(grp + 1) * 512], start=True, stop=True),
                    waits=[lw2, prev_ev[grp % 2]], mark=True)
                last_proj_mm = mm
                ev = S.op("vector", lambda e_, mt=mt, grp=grp, bank=bank: e_.tensor_copy(
                    fT[:, mt * 2048 + grp * 512: mt * 2048 + (grp + 1) * 512],
                    bank[:, 0:512]), waits=[mm], mark=True)
                prev_ev[grp % 2] = ev
        if DBG_GSA["skip_memset"]:
            z1 = z2 = None
        else:
            z1 = S.op("gpsimd", lambda e_: e_.memset(hk, 0.0), waits=[("e", "vector", S.ecnt["vector"])])
            z2 = S.op("gpsimd", lambda e_: e_.memset(hv, 0.0))
        bank_free = {}
        bank_free[("hk", "all")] = z2
        bank_free[("hv", "all")] = z2
        for ct in range(NCH if DBG_GSA["nch"] is None else DBG_GSA["nch"]):
            tnh = S.op("scalar", lambda e_, ct=ct: e_.activation(
                th[:, 0:256].rearrange("p (a c) -> p a c", a=4),
                fT.rearrange("p (a c) -> p a c", a=4)[:, :, ct * C:(ct + 1) * C],
                AF.Tanh, scale=0.5), waits=[("e", "vector", S.ecnt["vector"]),
                                            z1, z2], mark=True)
            vs = S.op("vector", lambda e_: e_.tensor_scalar(
                sig, th[:, 0:256], 0.5, 0.5, ALU.mult, ALU.add), waits=[tnh], mark=True)
            vsn = S.op("vector", lambda e_: e_.tensor_scalar(
                s_fm, th[:, 0:256], -0.5, 0.5, ALU.mult, ALU.add), mark=True)
            scans = []
            for mt in range(4):
                sc = S.op("vector", lambda e_, mt=mt: e_.tensor_tensor_scan(
                    ep_fm[:, mt * C:(mt + 1) * C], sig[:, mt * C:(mt + 1) * C],
                    sig[:, mt * C:(mt + 1) * C], 1.0, ALU.mult, ALU.bypass),
                    waits=[vs], mark=True)
                scans.append(sc)
            rec = S.op("vector", lambda e_: e_.reciprocal(en_fm, ep_fm),
                       waits=scans, mark=True)
            stf = S.op("vector", lambda e_: e_.tensor_mul(stil_fm, s_fm, en_fm),
                       waits=[rec, vsn], mark=True)
            for mt in range(4):
                tsl = B.tb[mt % 2][0:64, 0:128]
                tt = S.op("tensor", lambda e_, mt=mt, tsl=tsl: e_.transpose(
                    tsl, stil_fm[:, mt * C:(mt + 1) * C], B.ident_bf),
                    waits=[stf, bank_free.get(("tb", mt % 2))], mark=True)
                cv = S.op("vector", lambda e_, mt=mt, tsl=tsl: e_.tensor_copy(
                    stil_tm[:, mt * 128:(mt + 1) * 128], tsl),
                    waits=[tt], mark=True)
                bank_free[("tb", mt % 2)] = cv
                pbk = B.pb[4 + mt % 2]
                te = S.op("tensor", lambda e_, mt=mt, pbk=pbk: e_.transpose(
                    pbk[0:64, 0:128], ep_fm[:, mt * C:(mt + 1) * C],
                    B.ident_f), waits=scans + [bank_free.get(("pb", 4 + mt % 2))],
                    mark=True)
                ce = S.op("vector", lambda e_, mt=mt, pbk=pbk: e_.tensor_copy(
                    ep_tm[:, mt * 128:(mt + 1) * 128], pbk[0:64, 0:128]),
                    waits=[te], mark=True)
                bank_free[("pb", 4 + mt % 2)] = ce
            for mt in range(4):
                pbk = B.pb[4 + mt % 2]
                tz = S.op("tensor", lambda e_, mt=mt, pbk=pbk: e_.transpose(
                    pbk[0:1, 0:128], ep_fm[:, mt * C + C - 1: mt * C + C],
                    B.ident_f), waits=scans + [bank_free.get(("pb", 4 + mt % 2))],
                    mark=True)
                cz = S.op("vector", lambda e_, mt=mt, pbk=pbk: e_.tensor_copy(
                    eend[0:1, mt * 128:(mt + 1) * 128], pbk[0:1, 0:128]),
                    waits=[tz], mark=True)
                bank_free[("pb", 4 + mt % 2)] = cz
            dec_done = S.barrier_toks()
            NH = DBG_GSA["heads"] if DBG_GSA["heads"] is not None else H
            kv_toks = []
            for h in range(NH):
                for src_, dst_off, slot in ((kb, 0, 0), (vb, 1024, 1)):
                    tsl = B.tb[slot][0:64, 0:128]
                    tv = S.op("tensor", lambda e_, h=h, ct=ct, src_=src_, tsl=tsl: e_.transpose(
                        tsl, src_[:, h * 2048 + ct * C: h * 2048 + (ct + 1) * C],
                        B.ident_bf), waits=[bank_free.get(("tb", slot))] + dec_done,
                        mark=True)
                    cv = S.op("vector", lambda e_, h=h, dst_off=dst_off, tsl=tsl: e_.tensor_copy(
                        kv_tm[0:64, dst_off + h * 128: dst_off + (h + 1) * 128], tsl),
                        waits=[tv], mark=True)
                    bank_free[("tb", slot)] = cv
                    kv_toks.append(cv)
            for h in range(NH):
                mmP = S.op("tensor", lambda e_, h=h, ct=ct: e_.matmul(
                    B.pb[0][0:64, h * 64:(h + 1) * 64],
                    kb[:, h * 2048 + ct * C: h * 2048 + (ct + 1) * C],
                    qb[:, h * 2048 + ct * C: h * 2048 + (ct + 1) * C],
                    start=True, stop=True),
                    waits=[bank_free.get(("pb", 0))] + dec_done, mark=(h == NH - 1))
            vPm = S.op("vector", lambda e_, NH=NH: e_.tensor_tensor(
                pm_all[0:64, 0:NH * 64].rearrange("p (g j) -> p g j", g=NH),
                B.pb[0][0:64, 0:NH * 64].rearrange("p (g j) -> p g j", g=NH),
                B.triu64.rearrange("p (o j) -> p o j", o=1).broadcast_to((64, NH, 64)),
                ALU.mult), waits=[mmP], mark=True)
            bank_free[("pb", 0)] = vPm
            for h in range(NH):
                col = (h // 2) * 128 + (h % 2) * 64
                S.op("tensor", lambda e_, h=h, col=col: e_.matmul(
                    B.pb[1][0:64, h * 64:(h + 1) * 64],
                    pm_all[0:64, h * 64:(h + 1) * 64],
                    stil_tm[0:64, col:col + 64], start=True, stop=False),
                    waits=[vPm, bank_free.get(("pb", 1))])
                mmOk = S.op("tensor", lambda e_, h=h, ct=ct: e_.matmul(
                    B.pb[1][0:64, h * 64:(h + 1) * 64],
                    qb[:, h * 2048 + ct * C: h * 2048 + (ct + 1) * C],
                    hk[:, h * 64:(h + 1) * 64], start=False, stop=True),
                    waits=[bank_free.get(("hk", "all"))], mark=(h == NH - 1))
            ep_v = ep_tm[0:64, 0:NH * 64]
            v1 = S.op("vector", lambda e_, NH=NH: e_.tensor_mul(
                ok_all[0:64, 0:NH * 64], B.pb[1][0:64, 0:NH * 64], ep_v),
                waits=[mmOk], mark=True)
            bank_free[("pb", 1)] = v1
            v2 = S.op("vector", lambda e_, NH=NH: e_.reduce_max(
                mx8[0:64, 0:NH].rearrange("p (g o) -> p g o", o=1),
                ok_all[0:64, 0:NH * 64].rearrange("p (g j) -> p g j", g=NH),
                axis=AX.X, negate=True), mark=True)
            v3 = S.op("vector", lambda e_, NH=NH: e_.tensor_tensor(
                ok_all[0:64, 0:NH * 64].rearrange("p (g j) -> p g j", g=NH),
                ok_all[0:64, 0:NH * 64].rearrange("p (g j) -> p g j", g=NH),
                mx8[0:64, 0:NH].rearrange("p (g o) -> p g o", o=1).broadcast_to((64, NH, 64)),
                ALU.add), mark=True)
            s1 = S.op("scalar", lambda e_, NH=NH: e_.activation(
                qe_all[0:64, 0:NH * 64], ok_all[0:64, 0:NH * 64], AF.Exp),
                waits=[v3], mark=True)
            v4 = S.op("vector", lambda e_, NH=NH: e_.reduce_sum(
                mx8[0:64, 8:8 + NH].rearrange("p (g o) -> p g o", o=1),
                qe_all[0:64, 0:NH * 64].rearrange("p (g j) -> p g j", g=NH),
                axis=AX.X), waits=[s1], mark=True)
            v5 = S.op("vector", lambda e_, NH=NH: e_.reciprocal(
                mx8[0:64, 16:16 + NH], mx8[0:64, 8:8 + NH]), mark=True)
            v6 = S.op("vector", lambda e_, NH=NH: e_.tensor_tensor(
                ok_all[0:64, 0:NH * 64].rearrange("p (g j) -> p g j", g=NH),
                qe_all[0:64, 0:NH * 64].rearrange("p (g j) -> p g j", g=NH),
                mx8[0:64, 16:16 + NH].rearrange("p (g o) -> p g o", o=1).broadcast_to((64, NH, 64)),
                ALU.mult), mark=True)
            v7 = S.op("vector", lambda e_, NH=NH: e_.tensor_mul(
                qt_all[0:64, 0:NH * 64], ok_all[0:64, 0:NH * 64], ep_v), mark=True)
            aw = []
            for h in range(NH):
                col = (h // 2) * 128 + (h % 2) * 64
                par = h % 2
                tq = S.op("tensor", lambda e_, h=h, par=par: e_.transpose(
                    B.tb[0][0:64, par * 64:par * 64 + 64],
                    qt_all[0:64, h * 64:(h + 1) * 64], B.ident_bf[0:64, 0:64]),
                    waits=[v7, bank_free.get(("tb", 0))], mark=True)
                cq = S.op("vector", lambda e_, h=h, par=par: e_.tensor_copy(
                    qtT_all[0:64, h * 64:(h + 1) * 64],
                    B.tb[0][0:64, par * 64:par * 64 + 64]), waits=[tq], mark=True)
                bank_free[("tb", 0)] = cq
                ts_ = S.op("tensor", lambda e_, col=col, par=par: e_.transpose(
                    B.tb[1][0:64, par * 64:par * 64 + 64],
                    stil_tm[0:64, col:col + 64], B.ident_bf[0:64, 0:64]),
                    waits=[bank_free.get(("tb", 1))] + dec_done, mark=True)
                cs = S.op("vector", lambda e_, h=h, par=par: e_.tensor_copy(
                    stilT_all[0:64, h * 64:(h + 1) * 64],
                    B.tb[1][0:64, par * 64:par * 64 + 64]), waits=[ts_], mark=True)
                bank_free[("tb", 1)] = cs
                mmA = S.op("tensor", lambda e_, h=h: e_.matmul(
                    B.pb[0][0:64, h * 64:(h + 1) * 64],
                    stilT_all[0:64, h * 64:(h + 1) * 64],
                    qtT_all[0:64, h * 64:(h + 1) * 64], start=True, stop=True),
                    waits=[cq, cs, bank_free.get(("pb", 0))], mark=(h == NH - 1))
                aw.append(mmA)
            vAm = S.op("vector", lambda e_, NH=NH: e_.tensor_tensor(
                pm_all[0:64, 0:NH * 64].rearrange("p (g j) -> p g j", g=NH),
                B.pb[0][0:64, 0:NH * 64].rearrange("p (g j) -> p g j", g=NH),
                B.triu64.rearrange("p (o j) -> p o j", o=1).broadcast_to((64, NH, 64)),
                ALU.mult), waits=[aw[-1]], mark=True)
            bank_free[("pb", 0)] = vAm
            ovw = []
            for h in range(NH):
                obank = B.pb[4] if h < 4 else B.pb[1]
                ocol = (h % 4) * 128
                S.op("tensor", lambda e_, h=h, obank=obank, ocol=ocol: e_.matmul(
                    obank[0:64, ocol:ocol + 128],
                    pm_all[0:64, h * 64:(h + 1) * 64],
                    kv_tm[0:64, 1024 + h * 128: 1024 + (h + 1) * 128],
                    start=True, stop=False),
                    waits=[vAm, kv_toks[2 * h + 1],
                           bank_free.get(("pbov", 4 if h < 4 else 1))])
                mmVb = S.op("tensor", lambda e_, h=h, obank=obank, ocol=ocol: e_.matmul(
                    obank[0:64, ocol:ocol + 128],
                    qtT_all[0:64, h * 64:(h + 1) * 64],
                    hv[0:64, h * 128:(h + 1) * 128], start=False, stop=True),
                    waits=[bank_free.get(("hv", "all"))], mark=(h % 4 == 3))
                if h % 4 == 3:
                    ovw.append(mmVb)
            gw_ = []
            for gh in range((NH + 3) // 4):
                mmG = S.op("tensor", lambda e_, gh=gh, ct=ct: e_.matmul(
                    B.pb[5][0:64, 0:512], t_g[:, ct * C:(ct + 1) * C],
                    wg2[:, gh * 512:(gh + 1) * 512], start=True, stop=True),
                    waits=[lwg2, bank_free.get(("pb", 5))], mark=True)
                sth = S.op("scalar", lambda e_, gh=gh: e_.activation(
                    thg_all[0:64, gh * 512:(gh + 1) * 512], B.pb[5][0:64, 0:512],
                    AF.Tanh, scale=0.5), waits=[mmG], mark=True)
                bank_free[("pb", 5)] = sth
                gw_.append(sth)
            vsg = S.op("vector", lambda e_, NH=NH: e_.tensor_scalar(
                thg2[0:64, 0:NH * 128], thg_all[0:64, 0:NH * 128], 0.5, 0.5,
                ALU.mult, ALU.add), waits=gw_, mark=True)
            ogw = []
            for half in range((NH + 3) // 4):
                obank = B.pb[4] if half == 0 else B.pb[1]
                vog = S.op("vector", lambda e_, half=half, obank=obank: e_.tensor_mul(
                    ogf_all[0:64, half * 512:(half + 1) * 512],
                    obank[0:64, 0:512], thg2[0:64, half * 512:(half + 1) * 512]),
                    waits=[vsg] + ovw, mark=True)
                bank_free[("pbov", 4 if half == 0 else 1)] = vog
                ogw.append(vog)
            for h in range(NH):
                par = h % 2
                tto = S.op("tensor", lambda e_, h=h, par=par: e_.transpose(
                    B.tb[1][0:128, 256 + par * 64:256 + par * 64 + 64],
                    ogf_all[0:64, h * 128:(h + 1) * 128], B.ident_bf[0:64, 0:64]),
                    waits=ogw + [bank_free.get(("tb", 1))], mark=True)
                cog = S.op("vector", lambda e_, h=h, ct=ct, par=par: e_.tensor_copy(
                    og[:, h * 2048 + ct * C: h * 2048 + (ct + 1) * C],
                    B.tb[1][0:128, 256 + par * 64:256 + par * 64 + 64]),
                    waits=[tto], mark=True)
                bank_free[("tb", 1)] = cog
            for h in range(NH):
                col = (h // 2) * 128 + (h % 2) * 64
                mmHk = S.op("tensor", lambda e_, h=h, col=col: e_.matmul(
                    B.pb[2][0:128, h * 64:(h + 1) * 64],
                    kv_tm[0:64, h * 128:(h + 1) * 128],
                    stil_tm[0:64, col:col + 64], start=True, stop=True),
                    waits=[kv_toks[2 * h], bank_free.get(("pb", 2)), mmOk],
                    mark=(h == NH - 1))
            mmBc = S.op("tensor", lambda e_, NH=NH: e_.matmul(
                B.pb[3][0:128, 0:NH * 64], B.ones_row, eend[0:1, 0:NH * 64],
                start=True, stop=True),
                waits=[bank_free.get(("pb", 3))] + dec_done, mark=True)
            vt1 = S.op("vector", lambda e_, NH=NH: e_.tensor_add(
                tmp1[0:128, 0:NH * 64], B.pb[2][0:128, 0:NH * 64],
                hk[:, 0:NH * 64]), waits=[mmHk], mark=True)
            bank_free[("pb", 2)] = vt1
            vhk = S.op("vector", lambda e_, NH=NH: e_.tensor_mul(
                hk[:, 0:NH * 64], tmp1[0:128, 0:NH * 64], B.pb[3][0:128, 0:NH * 64]),
                waits=[vt1, mmBc], mark=True)
            bank_free[("pb", 3)] = vhk
            bank_free[("hk", "all")] = vhk
            for h in range(NH):
                mmEc = S.op("tensor", lambda e_, h=h: e_.matmul(
                    B.pb[0][0:64, h:h + 1], eend[0:1, h * 64:(h + 1) * 64],
                    B.ones_row[0:1, 0:1], start=True, stop=True),
                    waits=[bank_free.get(("pb", 0))] + dec_done, mark=(h == NH - 1))
            cec = S.op("vector", lambda e_, NH=NH: e_.tensor_copy(
                ecols[0:64, 0:NH], B.pb[0][0:64, 0:NH]), waits=[mmEc], mark=True)
            bank_free[("pb", 0)] = cec
            for h in range(NH):
                col = (h // 2) * 128 + (h % 2) * 64
                hbank = B.pb[2] if h < 4 else B.pb[3]
                hcol = (h % 4) * 128
                mmHv = S.op("tensor", lambda e_, col=col, hbank=hbank, hcol=hcol, h=h: e_.matmul(
                    hbank[0:64, hcol:hcol + 128], stil_tm[0:64, col:col + 64],
                    kv_tm[0:64, 1024 + h * 128: 1024 + (h + 1) * 128],
                    start=True, stop=True),
                    waits=[kv_toks[2 * h + 1],
                           bank_free.get(("pbhv", 2 if h < 4 else 3)),
                           vt1 if h < 4 else vhk, ovw[-1]],
                    mark=(h % 4 == 3))
                if h % 4 == 3:
                    nh4 = min(4, NH - (0 if h < 4 else 4))
                    half = 0 if h < 4 else 1
                    vt2 = S.op("vector", lambda e_, half=half, hbank=hbank, nh4=nh4: e_.tensor_add(
                        tmp2[0:64, 0:nh4 * 128], hbank[0:64, 0:nh4 * 128],
                        hv[0:64, half * 512: half * 512 + nh4 * 128]),
                        waits=[mmHv], mark=True)
                    bank_free[("pbhv", 2 if h < 4 else 3)] = vt2
                    vhv = S.op("vector", lambda e_, half=half, nh4=nh4: e_.tensor_tensor(
                        hv[0:64, half * 512: half * 512 + nh4 * 128]
                        .rearrange("p (g j) -> p g j", g=nh4),
                        tmp2[0:64, 0:nh4 * 128].rearrange("p (g j) -> p g j", g=nh4),
                        ecols[0:64, half * 4: half * 4 + nh4]
                        .rearrange("p (g o) -> p g o", o=1).broadcast_to((64, nh4, 128)),
                        ALU.mult), waits=[cec], mark=True)
                    bank_free[("hv", "all")] = vhv
        for h in range(H if not DBG_GSA["skip_post"] else 0):
            for grp in range(4):
                o = h * 2048 + grp * 512
                sq1 = S.op("scalar", lambda e_, o=o: e_.activation(
                    sqb, og[:, o:o + 512], AF.Square),
                    waits=[("e", "vector", S.ecnt["vector"])], mark=True)
                mmS = S.op("tensor", lambda e_: e_.matmul(
                    B.pb[0][0:1, 0:512], B.ones_col, sqb, start=True, stop=True),
                    waits=[sq1, bank_free.get(("pb", 0))], mark=True)
                ssd = S.op("scalar", lambda e_: e_.activation(
                    rrow, B.pb[0][0:1, 0:512], AF.Ln, bias=B.cc[0:1, 3:4]),
                    waits=[mmS], mark=True)
                bank_free[("pb", 0)] = ssd
                srr = S.op("scalar", lambda e_: e_.activation(
                    rrow2, rrow, AF.Exp, scale=-0.5), mark=True)
                mmB = S.op("tensor", lambda e_: e_.matmul(
                    B.pb[1][0:128, 0:512], B.ones_row, rrow2, start=True, stop=True),
                    waits=[srr, bank_free.get(("pb", 1))], mark=True)
                von = S.op("vector", lambda e_, o=o: e_.scalar_tensor_tensor(
                    ogn[:, o:o + 512], og[:, o:o + 512], float(np.sqrt(128.0)),
                    B.pb[1][0:128, 0:512], ALU.mult, ALU.mult),
                    waits=[mmB], mark=True)
                bank_free[("pb", 1)] = von
        for halfW in range(2 if not DBG_GSA["skip_wo"] else 0):
            lwo = S.dma("gpsimd", wslot.rearrange("p (t n) -> p t n", t=4),
                        aps["wo_sc_bf"][halfW * 512:(halfW + 1) * 512, :]
                        .rearrange("(t p) n -> p t n", p=128),
                        waits=[last_proj_mm, last_wo_mm,
                               ("e", "vector", S.ecnt["vector"]),
                               ("e", "tensor", S.ecnt["tensor"])])
            prev_ao = {0: None, 1: None}
            prev_aodma = {0: None, 1: None}
            for dout in range(DT):
                for grp in range(4):
                    par = (dout * 4 + grp) % 2
                    bank = B.pb[2 + par]
                    for kt in range(4):
                        mm = S.op("tensor", lambda e_, dout=dout, grp=grp, kt=kt, halfW=halfW, bank=bank: e_.matmul(
                            bank[:, 0:512],
                            wslot[:, kt * 1024 + dout * 128: kt * 1024 + (dout + 1) * 128],
                            ogn[:, (halfW * 4 + kt) * 2048 + grp * 512:
                                (halfW * 4 + kt) * 2048 + (grp + 1) * 512],
                            start=(kt == 0), stop=(kt == 3)),
                            waits=[lwo, prev_ao[par],
                                   ("e", "vector", S.ecnt["vector"])],
                            mark=(kt == 3))
                    last_wo_mm = mm
                    abuf = th if par == 0 else sp
                    aev = S.op("vector", lambda e_, par=par, bank=bank, abuf=abuf: e_.tensor_copy(
                        abuf, bank[:, 0:512]), waits=[mm, prev_aodma[par]], mark=True)
                    prev_ao[par] = aev
                    kw = {} if halfW == 0 else {"accum_op": ALU.add}
                    ad = S.dma("gpsimd", fm_view(aps["a0T"])[:, dout,
                               b * SEQ + grp * 512: b * SEQ + (grp + 1) * 512],
                               abuf, waits=[aev], **kw)
                    prev_aodma[par] = ad
    S.full_barrier()


def emit_l1attn(nc, S, B, aps):
    af, ab = B.af, B.ab
    xn_b = ab[0:128, 0:16384]
    qb = ab[0:128, 16384:24576]
    kb = ab[0:128, 24576:32768]
    vb = ab[0:128, 32768:40960]
    og = ab[0:128, 40960:49152]
    wslot = ab[0:128, 49152:53248]
    probs_e = [ab[0:128, 53248:55296], ab[0:128, 58624:60672]]
    probs_n = [ab[0:128, 55296:57344], ab[0:128, 60672:62720]]
    pT = [ab[0:128, 57344 + i * 128: 57344 + (i + 1) * 128] for i in range(8)]
    v_tm = ab[0:128, 62720:63744]
    scores = [af[0:128, 0:2048], af[0:128, 3072:5120]]
    st = af[0:128, 2048:2080]
    ao = [af[0:128, 5120 + i * 512: 5120 + (i + 1) * 512] for i in range(2)]

    last_proj_mm = None
    last_wo_mm = None
    last_wo_aev = None
    for b in range(BPC):
        lxn = S.dma("sync", xn_b.rearrange("p (t c) -> p t c", t=8),
                    fm_view(aps["xn1T"])[:, :, b * SEQ:(b + 1) * SEQ],
                    waits=[last_proj_mm])
        for half in range(2):
            for wname, dst, scale in (("wqT_bf", qb, 0.125),
                                      ("wkT_bf", kb, None),
                                      ("wvT_bf", vb, None)):
                lw = S.dma("gpsimd", wslot[:, 0:4096].rearrange("p (t n) -> p t n", t=8),
                           aps[wname][:, half * 512:(half + 1) * 512]
                           .rearrange("(t p) n -> p t n", p=128),
                           waits=[last_proj_mm, last_wo_mm])
                prev_ev = {0: None, 1: None}
                for mt4 in range(4):
                    for grp in range(4):
                        bank = B.pb[grp % 2]
                        for kt in range(DT):
                            mm = S.op("tensor", lambda e_, mt4=mt4, grp=grp, kt=kt, bank=bank: e_.matmul(
                                bank[:, 0:512],
                                wslot[:, kt * 512 + mt4 * 128: kt * 512 + (mt4 + 1) * 128],
                                xn_b[:, kt * 2048 + grp * 512: kt * 2048 + (grp + 1) * 512],
                                start=(kt == 0), stop=(kt == DT - 1)),
                                waits=[lw, lxn, prev_ev[grp % 2]],
                                mark=(kt == DT - 1))
                        last_proj_mm = mm
                        o = mt4 * 2048 + grp * 512
                        if scale is None:
                            ev = S.op("vector", lambda e_, o=o, bank=bank, dst=dst: e_.tensor_copy(
                                dst[:, o:o + 512], bank[:, 0:512]), waits=[mm], mark=True)
                        else:
                            ev = S.op("vector", lambda e_, o=o, bank=bank, dst=dst, scale=scale: e_.tensor_scalar(
                                dst[:, o:o + 512], bank[:, 0:512], scale, None, ALU.mult),
                                waits=[mm], mark=True)
                        prev_ev[grp % 2] = ev
            if b == 0 and half == 0 and "dbgq" in aps:
                S.full_barrier()
                S.dma("sync", aps["dbgq"], qb)
                S.dma("sync", aps["dbgk"], kb)
                S.dma("sync", aps["dbgv"], vb)
            for h8 in range(8):
                hp, row0 = h8 // 2, (h8 % 2) * 64
                qh = qb[row0:row0 + 64, hp * 2048:(hp + 1) * 2048]
                kh = kb[row0:row0 + 64, hp * 2048:(hp + 1) * 2048]
                vh = vb[row0:row0 + 64, hp * 2048:(hp + 1) * 2048]
                prev_vc = {0: None, 1: None}
                vts = []
                idv = B.ident_bf[row0:row0 + 64, row0:row0 + 64]
                for kt2 in range(16):
                    tsl = B.tb[kt2 % 2][:, 0:64]
                    tv = S.op("tensor", lambda e_, kt2=kt2, tsl=tsl, vh=vh, idv=idv: e_.transpose(
                        tsl, vh[:, kt2 * 128:(kt2 + 1) * 128], idv),
                        waits=[prev_vc[kt2 % 2], ("e", "vector", S.ecnt["vector"])],
                        mark=True)
                    cv = S.op("vector", lambda e_, kt2=kt2, tsl=tsl: e_.tensor_copy(
                        v_tm[:, kt2 * 64:(kt2 + 1) * 64], tsl), waits=[tv], mark=True)
                    prev_vc[kt2 % 2] = cv
                    vts.append(cv)
                prev_avev = {0: None, 1: None}
                prev_sc = {0: None, 1: None, 2: None, 3: None}
                prev_pc = {0: None, 1: None}
                av_slot = {i: None for i in range(8)}
                for qt in range(16):
                    dt_tile = qt if qt < 8 else qt - 8
                    klen = (dt_tile + 1) * 128
                    ngrp = (klen + 511) // 512
                    qp = qt % 2
                    sc_b = scores[qp]
                    pe_b = probs_e[qp]
                    pn_b = probs_n[qp]
                    st0 = st[:, qp * 16 + 0: qp * 16 + 1]
                    st1 = st[:, qp * 16 + 1: qp * 16 + 2]
                    st2 = st[:, qp * 16 + 2: qp * 16 + 3]
                    evs = []
                    for g2 in range(ngrp):
                        n = min(512, klen - g2 * 512)
                        bank = B.pb[qp * 2 + g2 % 2]
                        mm = S.op("tensor", lambda e_, qt=qt, g2=g2, n=n, bank=bank, qh=qh, kh=kh: e_.matmul(
                            bank[:, 0:n], qh[:, qt * 128:(qt + 1) * 128],
                            kh[:, g2 * 512: g2 * 512 + n], start=True, stop=True),
                            waits=[prev_sc[qp * 2 + g2 % 2], last_wo_aev],
                            mark=True)
                        if n > 128 or g2 < ngrp - 1:
                            ncopy = n if g2 < ngrp - 1 else n - 128
                            ev = S.op("vector", lambda e_, g2=g2, ncopy=ncopy, bank=bank, sc_b=sc_b: e_.tensor_copy(
                                sc_b[:, g2 * 512: g2 * 512 + ncopy], bank[:, 0:ncopy]),
                                waits=[mm], mark=True)
                            evs.append(ev)
                            prev_sc[qp * 2 + g2 % 2] = ev
                        if g2 == ngrp - 1:
                            ev = S.op("vector", lambda e_, dt_tile=dt_tile, g2=g2, n=n, bank=bank, sc_b=sc_b: e_.tensor_add(
                                sc_b[:, dt_tile * 128: dt_tile * 128 + 128],
                                bank[:, n - 128:n], B.causal),
                                waits=[mm], mark=True)
                            evs.append(ev)
                            prev_sc[qp * 2 + g2 % 2] = ev
                    mx = S.op("vector", lambda e_, klen=klen, sc_b=sc_b, st0=st0: e_.reduce_max(
                        st0, sc_b[:, 0:klen], axis=AX.X, negate=True),
                        waits=evs, mark=True)
                    ex = S.op("scalar", lambda e_, klen=klen, sc_b=sc_b, pe_b=pe_b, st0=st0, st1=st1: e_.activation(
                        pe_b[:, 0:klen], sc_b[:, 0:klen], AF.Exp,
                        bias=st0, accum_out=st1), waits=[mx], mark=True)
                    rd = S.op("vector", lambda e_, st1=st1, st2=st2: e_.reciprocal(st2, st1),
                              waits=[ex], mark=True)
                    nm = S.op("vector", lambda e_, klen=klen, pe_b=pe_b, pn_b=pn_b, st2=st2: e_.tensor_scalar(
                        pn_b[:, 0:klen], pe_b[:, 0:klen], st2, None,
                        ALU.mult), waits=[rd], mark=True)
                    for kt2 in range(dt_tile + 1):
                        psl = qp * 4 + kt2 % 4
                        tsl = B.tb[psl % 2][:, 64 + (psl // 2) * 128: 192 + (psl // 2) * 128]
                        tp = S.op("tensor", lambda e_, kt2=kt2, tsl=tsl, pn_b=pn_b: e_.transpose(
                            tsl, pn_b[:, kt2 * 128:(kt2 + 1) * 128], B.ident_bf),
                            waits=[nm, prev_pc[psl % 2]], mark=True)
                        cp = S.op("vector", lambda e_, psl=psl, tsl=tsl: e_.tensor_copy(
                            pT[psl], tsl), waits=[tp, av_slot[psl]], mark=True)
                        prev_pc[psl % 2] = cp
                        mm = S.op("tensor", lambda e_, kt2=kt2, dt_tile=dt_tile, psl=psl, qp=qp: e_.matmul(
                            B.pb[4 + qp][0:64, 0:128], v_tm[:, kt2 * 64:(kt2 + 1) * 64],
                            pT[psl], start=(kt2 == 0), stop=(kt2 == dt_tile)),
                            waits=[cp, vts[kt2], prev_avev[qp]], mark=True)
                        av_slot[psl] = mm
                    oev = S.op("vector", lambda e_, qt=qt, hp=hp, row0=row0, qp=qp: e_.tensor_copy(
                        og[row0:row0 + 64, hp * 2048 + qt * 128: hp * 2048 + (qt + 1) * 128],
                        B.pb[4 + qp][0:64, 0:128]), waits=[mm], mark=True)
                    prev_avev[qp] = oev
            if b == 0 and "dbgog" in aps:
                S.full_barrier()
                S.dma("sync", aps["dbgog" if half == 0 else "dbgog2"], og)
            lwo = S.dma("gpsimd", wslot[:, 0:4096].rearrange("p (t n) -> p t n", t=4),
                        aps["woT_bf"][half * 512:(half + 1) * 512, :]
                        .rearrange("(t p) n -> p t n", p=128),
                        waits=[last_proj_mm, last_wo_mm])
            prev_ao = {0: None, 1: None}
            prev_aodma = {0: None, 1: None}
            for dout in range(DT):
                for grp in range(4):
                    bank = B.pb[2 + (dout * 4 + grp) % 2]
                    for kt in range(4):
                        mm = S.op("tensor", lambda e_, dout=dout, grp=grp, kt=kt, bank=bank: e_.matmul(
                            bank[:, 0:512],
                            wslot[:, kt * 1024 + dout * 128: kt * 1024 + (dout + 1) * 128],
                            og[:, kt * 2048 + grp * 512: kt * 2048 + (grp + 1) * 512],
                            start=(kt == 0), stop=(kt == 3)),
                            waits=[lwo, prev_ao[(dout * 4 + grp) % 2],
                                   ("e", "vector", S.ecnt["vector"])],
                            mark=(kt == 3))
                    last_wo_mm = mm
                    par = (dout * 4 + grp) % 2
                    aev = S.op("vector", lambda e_, par=par, bank=bank: e_.tensor_copy(
                        ao[par], bank[:, 0:512]), waits=[mm, prev_aodma[par]], mark=True)
                    prev_ao[par] = aev
                    last_wo_aev = aev
                    kw = {} if half == 0 else {"accum_op": ALU.add}
                    ad = S.dma("gpsimd", fm_view(aps["a1T"])[:, dout,
                               b * SEQ + grp * 512: b * SEQ + (grp + 1) * 512],
                               ao[par], waits=[aev], **kw)
                    prev_aodma[par] = ad
    S.full_barrier()


def emit_l0moe(nc, S, B, aps):
    af, ab = B.af, B.ab
    w1sb = ab[0:128, 0:32768]
    w2sb = ab[0:128, 32768:65536]
    hg = ab[0:128, 65536:81920]
    xng = ab[0:128, 81920:86016]
    fo = [af[0:128, 0:512], af[0:128, 512:1024]]
    ssc = af[0:128, 1024:1536]
    srow = af[0:1, 1536:5632]
    b1c = af[0:128, 5632:5664]
    b2c = af[0:128, 5664:5672]

    last_mm1 = None
    last_mm2 = None
    last_silu = None
    last_fev = None
    last_bcast = None
    for e in range(8):
        lw1 = S.dma("gpsimd", w1sb.rearrange("p (t n) -> p t n", t=8),
                    aps["L0_ffn_W1"][e].rearrange("(t p) n -> p t n", p=128),
                    waits=[last_mm1])
        lw2 = S.dma("gpsimd", w2sb.rearrange("p (t n) -> p t n", t=32),
                    aps["L0_ffn_W2"][e].rearrange("(t p) n -> p t n", p=128),
                    waits=[last_mm2])
        lb1 = S.dma("sync", b1c, aps["b1c"][e], waits=[last_silu])
        lb2 = S.dma("sync", b2c, aps["b2c"][e], waits=[last_fev])
        lsr = S.dma("sync", srow, aps["selsc"][e:e + 1, :], waits=[last_bcast])
        prev_silu = {0: None, 1: None}
        prev_fev = {0: None, 1: None}
        prev_fdma = {0: None, 1: None}
        prev_cssc = None
        for blk in range(8):
            c0 = blk * 512
            lx = S.dma("sync" if blk % 2 == 0 else "scalar",
                       xng.rearrange("p (t c) -> p t c", t=8),
                       fm_view(aps["xn2T"])[:, :, c0:c0 + 512],
                       waits=[last_mm1])
            mbc = S.op("tensor", lambda e_, c0=c0: e_.matmul(
                B.pb[4][0:128, 0:512], B.ones_row, srow[0:1, c0:c0 + 512],
                start=True, stop=True), waits=[lsr, prev_cssc], mark=True)
            last_bcast = mbc
            cssc = S.op("vector", lambda e_: e_.tensor_copy(ssc, B.pb[4][0:128, 0:512]),
                        waits=[mbc], mark=True)
            prev_cssc = cssc
            silu_toks = []
            for ft in range(32):
                bank = B.pb[ft % 2]
                for kt in range(DT):
                    mm = S.op("tensor", lambda e_, ft=ft, kt=kt, bank=bank: e_.matmul(
                        bank[:, 0:512],
                        w1sb[:, kt * 4096 + ft * 128: kt * 4096 + (ft + 1) * 128],
                        xng[:, kt * 512:(kt + 1) * 512],
                        start=(kt == 0), stop=(kt == DT - 1)),
                        waits=[lw1, lx, prev_silu[ft % 2]],
                        mark=(kt == DT - 1))
                last_mm1 = mm
                sil = S.op("scalar", lambda e_, ft=ft, bank=bank: e_.activation(
                    hg[:, ft * 512:(ft + 1) * 512], bank[:, 0:512], AF.Silu,
                    bias=b1c[:, ft:ft + 1]), waits=[mm, lb1], mark=True)
                prev_silu[ft % 2] = sil
                silu_toks.append(sil)
                last_silu = sil
            for ot in range(DT):
                bank = B.pb[2 + ot % 2]
                for kt in range(32):
                    mm = S.op("tensor", lambda e_, ot=ot, kt=kt, bank=bank: e_.matmul(
                        bank[:, 0:512],
                        w2sb[:, kt * 1024 + ot * 128: kt * 1024 + (ot + 1) * 128],
                        hg[:, kt * 512:(kt + 1) * 512],
                        start=(kt == 0), stop=(kt == 31)),
                        waits=[lw2, silu_toks[31], prev_fev[ot % 2]],
                        mark=(kt == 31))
                last_mm2 = mm
                fev = S.op("vector", lambda e_, ot=ot, bank=bank: e_.scalar_tensor_tensor(
                    fo[ot % 2], bank[:, 0:512], b2c[:, ot:ot + 1], ssc,
                    ALU.add, ALU.mult),
                    waits=[mm, lb2, cssc, prev_fdma[ot % 2]], mark=True)
                last_fev = fev
                prev_fev[ot % 2] = fev
                kw = {"accum_op": ALU.add} if e > 0 else {}
                fd = S.dma("gpsimd", fm_view(aps["f0T"])[:, ot, c0:c0 + 512],
                           fo[ot % 2], waits=[fev], **kw)
                prev_fdma[ot % 2] = fd
    S.full_barrier()


def emit_resln(nc, S, B, aps, xin, ain, fin, xout, xnout):
    af, ab = B.af, B.ab
    GT = 256
    xa = af[0:128, 0:2048]
    xb = af[0:128, 2048:4096]
    xd = af[0:128, 4096:6144]
    xn1 = ab[0:128, 0:2048]
    r = lambda i: af[0:1, 6144 + i * 256: 6144 + (i + 1) * 256]
    for g in range(T // GT):
        c0 = g * GT
        la = S.dma("sync", xa.rearrange("p (t c) -> p t c", t=8),
                   fm_view(aps[xin])[:, :, c0:c0 + GT])
        lb = S.dma("scalar", xb.rearrange("p (t c) -> p t c", t=8),
                   fm_view(aps[ain])[:, :, c0:c0 + GT])
        adds1 = []
        for kt in range(DT):
            o = kt * GT
            a1 = S.op("vector", lambda e, o=o: e.tensor_add(
                xd[:, o:o + GT], xa[:, o:o + GT], xb[:, o:o + GT]),
                waits=[la, lb], mark=True)
            adds1.append(a1)
        lc = S.dma("gpsimd", xb.rearrange("p (t c) -> p t c", t=8),
                   fm_view(aps[fin])[:, :, c0:c0 + GT], waits=adds1)
        adds = []
        for kt in range(DT):
            o = kt * GT
            a2 = S.op("vector", lambda e, o=o: e.tensor_add(
                xa[:, o:o + GT], xd[:, o:o + GT], xb[:, o:o + GT]),
                waits=[lc], mark=True)
            adds.append(a2)
        for kt in range(DT):
            o = kt * GT
            sqk = S.op("scalar", lambda e, o=o: e.activation(
                xb[:, o:o + GT], xa[:, o:o + GT], AF.Square),
                waits=[adds[kt]], mark=True)
            S.op("tensor", lambda e, o=o, kt=kt: e.matmul(
                B.pb[2][0:1, 0:GT], B.ones_col, xa[:, o:o + GT],
                start=(kt == 0), stop=(kt == DT - 1)), waits=[adds[kt]])
            mS2 = S.op("tensor", lambda e, o=o, kt=kt: e.matmul(
                B.pb[3][0:1, 0:GT], B.ones_col, xb[:, o:o + GT],
                start=(kt == 0), stop=(kt == DT - 1)), waits=[sqk],
                mark=(kt == DT - 1))
        S.op("vector", lambda e: e.tensor_scalar(r(0), B.pb[2][0:1, 0:GT],
                                                 1.0 / D, None, ALU.mult), waits=[mS2])
        S.op("vector", lambda e: e.tensor_mul(r(1), r(0), r(0)))
        S.op("vector", lambda e: e.tensor_scalar(r(2), B.pb[3][0:1, 0:GT],
                                                 1.0 / D, None, ALU.mult))
        vv = S.op("vector", lambda e: e.tensor_sub(r(3), r(2), r(1)), mark=True)
        ln = S.op("scalar", lambda e: e.activation(r(2), r(3), AF.Ln,
                                                   bias=B.cc[0:1, 0:1]),
                  waits=[vv], mark=True)
        ist = S.op("scalar", lambda e: e.activation(r(4), r(2), AF.Exp,
                                                    scale=-0.5), mark=True)
        mb = S.op("tensor", lambda e: e.matmul(
            B.pb[4][0:128, 0:GT], B.ones_row, r(0), start=True, stop=True),
            waits=[vv], mark=True)
        ib = S.op("tensor", lambda e: e.matmul(
            B.pb[5][0:128, 0:GT], B.ones_row, r(4), start=True, stop=True),
            waits=[ist], mark=True)
        outs = []
        for kt in range(DT):
            o = kt * GT
            S.op("vector", lambda e, o=o: e.tensor_sub(
                xd[:, o:o + GT], xa[:, o:o + GT], B.pb[4][0:128, 0:GT]),
                waits=[mb])
            nx = S.op("vector", lambda e, o=o: e.tensor_mul(
                xn1[:, o:o + GT], xd[:, o:o + GT], B.pb[5][0:128, 0:GT]),
                waits=[ib], mark=True)
            outs.append(nx)
        S.dma("sync", fm_view(aps[xout])[:, :, c0:c0 + GT],
              xa.rearrange("p (t c) -> p t c", t=8), waits=adds)
        S.dma("gpsimd", fm_view(aps[xnout])[:, :, c0:c0 + GT],
              xn1.rearrange("p (t c) -> p t c", t=8), waits=outs)
        S.full_barrier()


def emit_egress(nc, S, B, aps):
    af = B.af
    GT = 256
    o_rows = aps["out"].rearrange("b n d -> (b n) d")
    xa = af[0:128, 0:2048]
    xb = af[0:128, 2048:4096]
    xd = af[0:128, 4096:6144]
    otile = af[0:128, 6144:7168]
    for g in range(T // GT):
        c0 = g * GT
        la = S.dma("sync", xa.rearrange("p (t c) -> p t c", t=8),
                   fm_view(aps["x1T"])[:, :, c0:c0 + GT])
        lb = S.dma("scalar", xb.rearrange("p (t c) -> p t c", t=8),
                   fm_view(aps["a1T"])[:, :, c0:c0 + GT])
        adds1 = []
        for kt in range(DT):
            o = kt * GT
            a1 = S.op("vector", lambda e, o=o: e.tensor_add(
                xd[:, o:o + GT], xa[:, o:o + GT], xb[:, o:o + GT]),
                waits=[la, lb], mark=True)
            adds1.append(a1)
        lc = S.dma("gpsimd", xb.rearrange("p (t c) -> p t c", t=8),
                   fm_view(aps["f1T"])[:, :, c0:c0 + GT], waits=adds1)
        adds = []
        for kt in range(DT):
            o = kt * GT
            a2 = S.op("vector", lambda e, o=o: e.tensor_add(
                xa[:, o:o + GT], xd[:, o:o + GT], xb[:, o:o + GT]),
                waits=[lc], mark=True)
            adds.append(a2)
        prev_cp = {0: None, 1: None}
        for s in range(GT // 128):
            cps = []
            for kt in range(DT):
                pslot = B.pb[(kt % 2) + 2][:, 0:128]
                t1 = S.op("tensor", lambda e, kt=kt, s=s, pslot=pslot: e.transpose(
                    pslot, xa[:, kt * GT + s * 128: kt * GT + s * 128 + 128],
                    B.ident_f), waits=[adds[kt], prev_cp[kt % 2]], mark=True)
                c1 = S.op("vector", lambda e, kt=kt, pslot=pslot: e.tensor_copy(
                    otile[:, kt * 128:(kt + 1) * 128], pslot),
                    waits=[t1], mark=True)
                prev_cp[kt % 2] = c1
                cps.append(c1)
            dto = S.dma("sync", o_rows[(g * 2 + s) * 128:(g * 2 + s) * 128 + 128, :],
                        otile, waits=cps)
            prev_cp = {0: dto, 1: dto}
        S.full_barrier()


def emit_l1ffn(nc, S, B, aps):
    af, ab = B.af, B.ab
    GT = 512
    u1 = ab[0:128, 0:2048]
    v1 = ab[0:128, 2048:10240]
    u2 = ab[0:128, 10240:18432]
    v2 = ab[0:128, 18432:20480]
    w0 = S.dma("gpsimd", u1.rearrange("p (t n) -> p t n", t=8),
               aps["L1_ffn_U1"].rearrange("(t p) n -> p t n", p=128))
    w1 = S.dma("gpsimd", v1.rearrange("p (t n) -> p t n", t=2),
               aps["L1_ffn_V1"].rearrange("(t p) n -> p t n", p=128))
    w2 = S.dma("gpsimd", u2.rearrange("p (t n) -> p t n", t=32),
               aps["L1_ffn_U2"].rearrange("(t p) n -> p t n", p=128))
    w3 = S.dma("gpsimd", v2.rearrange("p (t n) -> p t n", t=2),
               aps["L1_ffn_V2"].rearrange("(t p) n -> p t n", p=128))
    b1c = af[0:128, 0:32]
    b2c = af[0:128, 32:40]
    wb1 = S.dma("sync", b1c, aps["l1b1c"])
    wb2 = S.dma("sync", b2c, aps["l1b2c"])
    xng = ab[0:128, 20480:24576]
    y1 = ab[0:128, 24576:25600]
    hg = ab[0:128, 25600:41984]
    qg = ab[0:128, 41984:43008]
    gx = ab[0:128, 43008:51200]
    gs1 = ab[0:128, 51200:59392]
    gs2 = ab[0:128, 59392:67584]
    t0 = af[0:128, 64:576]
    sqc = af[0:128, 576:1088]
    cub = af[0:128, 1088:1600]
    inn = af[0:128, 1600:2112]
    th = af[0:128, 2112:2624]
    tt_ = af[0:128, 2624:3136]
    fo = af[0:128, 3136:3648]
    S.full_barrier()
    C1 = 0.044715
    C2 = 0.7978845608028654
    for g in range(T // GT):
        c0 = g * GT
        lx = S.dma("sync", xng.rearrange("p (t c) -> p t c", t=8),
                   fm_view(aps["xn1T"])[:, :, c0:c0 + GT])
        for mt in range(2):
            for kt in range(DT):
                mm = S.op("tensor", lambda e, mt=mt, kt=kt: e.matmul(
                    B.pb[0][:, 0:GT], u1[:, kt * 256 + mt * 128: kt * 256 + mt * 128 + 128],
                    xng[:, kt * GT:(kt + 1) * GT],
                    start=(kt == 0), stop=(kt == DT - 1)),
                    waits=[lx, w0, ("e", "vector", S.ecnt["vector"])],
                    mark=(kt == DT - 1))
            S.op("vector", lambda e, mt=mt: e.tensor_copy(
                y1[:, mt * GT:(mt + 1) * GT], B.pb[0][:, 0:GT]),
                waits=[mm], mark=True)
        hts = []
        for fh in range(2):
            bcs = []
            for f2 in range(16):
                ft = fh * 16 + f2
                bank = B.pb[1 + f2 % 2]
                for kt in range(2):
                    mm = S.op("tensor", lambda e, ft=ft, kt=kt, bank=bank: e.matmul(
                        bank[:, 0:GT], v1[:, kt * 4096 + ft * 128: kt * 4096 + ft * 128 + 128],
                        y1[:, kt * GT:(kt + 1) * GT],
                        start=(kt == 0), stop=(kt == 1)),
                        waits=[w1, ("e", "vector", S.ecnt["vector"])],
                        mark=(kt == 1))
                bc = S.op("vector", lambda e, ft=ft, f2=f2, bank=bank: e.tensor_scalar(
                    gx[:, f2 * GT:(f2 + 1) * GT], bank[:, 0:GT],
                    b1c[:, ft:ft + 1], None, ALU.add),
                    waits=[mm, wb1], mark=True)
                bcs.append(bc)
            W16 = 16 * GT
            S.op("scalar", lambda e: e.activation(gs1[:, 0:W16], gx[:, 0:W16],
                                                  AF.Square), waits=bcs)
            S.op("vector", lambda e: e.tensor_mul(gs2[:, 0:W16], gs1[:, 0:W16],
                                                  gx[:, 0:W16]),
                 waits=[("e", "scalar", S.ecnt["scalar"])])
            S.op("vector", lambda e: e.scalar_tensor_tensor(
                gs1[:, 0:W16], gs2[:, 0:W16], C1, gx[:, 0:W16], ALU.mult, ALU.add))
            tha = S.op("scalar", lambda e: e.activation(gs2[:, 0:W16], gs1[:, 0:W16],
                                                        AF.Tanh, scale=C2),
                       waits=[("e", "vector", S.ecnt["vector"])], mark=True)
            S.op("vector", lambda e: e.tensor_scalar(gs1[:, 0:W16], gs2[:, 0:W16],
                                                     0.5, 0.5, ALU.mult, ALU.add),
                 waits=[tha])
            hv = S.op("vector", lambda e, fh=fh, W16=W16: e.tensor_mul(
                hg[:, fh * W16:(fh + 1) * W16], gx[:, 0:W16], gs1[:, 0:W16]),
                mark=True)
            hts.append(hv)
        for mt in range(2):
            for kt in range(32):
                mm = S.op("tensor", lambda e, mt=mt, kt=kt: e.matmul(
                    B.pb[2][:, 0:GT], u2[:, kt * 256 + mt * 128: kt * 256 + mt * 128 + 128],
                    hg[:, kt * GT:(kt + 1) * GT],
                    start=(kt == 0), stop=(kt == 31)),
                    waits=[w2, hts[kt // 16], ("e", "vector", S.ecnt["vector"])],
                    mark=(kt == 31))
            S.op("vector", lambda e, mt=mt: e.tensor_copy(
                qg[:, mt * GT:(mt + 1) * GT], B.pb[2][:, 0:GT]),
                waits=[mm], mark=True)
        prev_fc = None
        prev_fd = None
        for ot in range(DT):
            for kt in range(2):
                mm = S.op("tensor", lambda e, ot=ot, kt=kt: e.matmul(
                    B.pb[3][:, 0:GT], v2[:, kt * 1024 + ot * 128: kt * 1024 + ot * 128 + 128],
                    qg[:, kt * GT:(kt + 1) * GT],
                    start=(kt == 0), stop=(kt == 1)),
                    waits=[w3, ("e", "vector", S.ecnt["vector"]), prev_fc],
                    mark=(kt == 1))
            fc = S.op("vector", lambda e, ot=ot: e.tensor_scalar(
                fo, B.pb[3][:, 0:GT], b2c[:, ot:ot + 1], None, ALU.add),
                waits=[mm, wb2, prev_fd], mark=True)
            prev_fc = fc
            prev_fd = S.dma("sync", fm_view(aps["f1T"])[:, ot, c0:c0 + GT], fo, waits=[fc])
        S.full_barrier()


def catalog():
    return {
        "x_in": ([BPC, SEQ, D], F32),
        "out": ([BPC, SEQ, D], F32),
        "wo_sc_bf": ([D, D], BF16),
        "wqT_bf": ([D, D], BF16),
        "wkT_bf": ([D, D], BF16),
        "wvT_bf": ([D, D], BF16),
        "woT_bf": ([D, D], BF16),
        "gate_wTf": ([D, 8], F32),
        "gsum": ([1, 8], F32),
        "b1c": ([8, 128, 32], F32),
        "b2c": ([8, 128, 8], F32),
        "l1b1c": ([128, 32], F32),
        "l1b2c": ([128, 8], F32),
        "rmsw_c": ([128, 8], F32),
        "dbg_nwc": ([128, 8], F32),
        "dbgq": ([128, 8192], BF16),
        "dbgk": ([128, 8192], BF16),
        "dbgv": ([128, 8192], BF16),
        "dbgog": ([128, 8192], BF16),
        "dbgog2": ([128, 8192], BF16),
        "x0T": ([D, T], F32),
        "xn0T": ([D, T], BF16),
        "xn2T": ([D, T], BF16),
        "selsc": ([16, T], F32),
        "a0T": ([D, T], F32),
        "f0T": ([D, T], F32),
        "x1T": ([D, T], F32),
        "xn1T": ([D, T], BF16),
        "a1T": ([D, T], F32),
        "f1T": ([D, T], F32),
    }


WEIGHT_SHAPES = {
    "L0_attn_Wq": [D, D], "L0_attn_Wk": [D, D], "L0_attn_Wv": [D, D],
    "L0_attn_Wf1": [D, 128], "L0_attn_Wf2": [128, 512],
    "L0_attn_Wg1": [D, 128], "L0_attn_Wg2": [128, D],
    "L0_attn_Wo": [D, D], "L0_attn_norm_w": [D],
    "L0_ffn_rms_w": [D], "L0_ffn_gate_w": [8, D],
    "L0_ffn_W1": [8, D, 4096], "L0_ffn_b1": [8, 4096],
    "L0_ffn_W2": [8, 4096, D], "L0_ffn_b2": [8, D],
    "L1_attn_Wq": [D, D], "L1_attn_Wk": [D, D], "L1_attn_Wv": [D, D],
    "L1_attn_Wo": [D, D],
    "L1_ffn_U1": [D, 256], "L1_ffn_V1": [256, 4096], "L1_ffn_b1": [4096],
    "L1_ffn_U2": [4096, 256], "L1_ffn_V2": [256, D], "L1_ffn_b2": [D],
}


def build_launch(phase_fns, ext_in, ext_out, weights_needed):
    nc = bass.Bass("TRN2", target_bir_lowering=False)
    cat = catalog()
    aps = {}
    for name in weights_needed:
        aps[name] = nc.dram_tensor(name, WEIGHT_SHAPES[name], F32,
                                   kind="ExternalInput").ap()
    for name in ext_in:
        sh, dt = cat[name]
        aps[name] = nc.dram_tensor(name, sh, dt, kind="ExternalInput").ap()
    for name in ext_out:
        sh, dt = cat[name]
        aps[name] = nc.dram_tensor(name, sh, dt, kind="ExternalOutput").ap()
    S = Sched(nc)
    B = Bufs(nc)
    emit_const(nc, S, B)
    S.full_barrier()
    for fn in phase_fns:
        fn(nc, S, B, aps)
        S.full_barrier()
    S.finalize()
    return nc


LAUNCHES = [
    ("prep", [emit_prep], [],
     ["wo_sc_bf", "gate_wTf", "gsum", "b1c", "b2c", "l1b1c", "l1b2c", "rmsw_c",
      "wqT_bf", "wkT_bf", "wvT_bf", "woT_bf"],
     ["L0_attn_norm_w", "L0_ffn_rms_w", "L0_ffn_b1", "L0_ffn_b2",
      "L1_ffn_b1", "L1_ffn_b2", "L0_ffn_gate_w", "L0_attn_Wo",
      "L1_attn_Wq", "L1_attn_Wk", "L1_attn_Wv", "L1_attn_Wo"]),
    ("ingest", [emit_ingest], ["x_in", "rmsw_c", "gate_wTf", "gsum"],
     ["x0T", "xn0T", "xn2T", "selsc"], []),
    ("l0attn", [emit_l0attn], ["xn0T", "wo_sc_bf"], ["a0T"],
     ["L0_attn_Wq", "L0_attn_Wk", "L0_attn_Wv", "L0_attn_Wf1", "L0_attn_Wf2",
      "L0_attn_Wg1", "L0_attn_Wg2"]),
    ("l1attn", [emit_l1attn], ["xn1T", "wqT_bf", "wkT_bf", "wvT_bf", "woT_bf"],
     ["a1T"], []),
    ("l0moe", [emit_l0moe], ["xn2T", "selsc", "b1c", "b2c"], ["f0T"],
     ["L0_ffn_W1", "L0_ffn_W2"]),
    ("resln", [lambda nc, S, B, aps: emit_resln(nc, S, B, aps, "x0T", "a0T",
                                                "f0T", "x1T", "xn1T")],
     ["x0T", "a0T", "f0T"], ["x1T", "xn1T"], []),
    ("l1ffn", [emit_l1ffn], ["xn1T", "l1b1c", "l1b2c"], ["f1T"],
     ["L1_ffn_U1", "L1_ffn_V1", "L1_ffn_U2", "L1_ffn_V2"]),
    ("egress", [emit_egress], ["x1T", "a1T", "f1T"], ["out"], []),
]


FUSED_ORDER = ["prep", "ingest", "l0attn", "l0moe", "resln", "l1attn",
               "l1ffn", "egress"]


def build_fused():
    nc = bass.Bass("TRN2", target_bir_lowering=False)
    cat = catalog()
    aps = {}
    for name, sh in WEIGHT_SHAPES.items():
        aps[name] = nc.dram_tensor(name, sh, F32, kind="ExternalInput").ap()
    sh, dt = cat["x_in"]
    aps["x_in"] = nc.dram_tensor("x_in", sh, dt, kind="ExternalInput").ap()
    sh, dt = cat["out"]
    aps["out"] = nc.dram_tensor("out", sh, dt, kind="ExternalOutput").ap()
    for name, (sh, dt) in cat.items():
        if name in ("x_in", "out") or name.startswith("dbg"):
            continue
        aps[name] = nc.dram_tensor(name, sh, dt, kind="Internal").ap()
    S = Sched(nc)
    B = Bufs(nc)
    emit_const(nc, S, B)
    S.full_barrier()
    by_name = {l[0]: l for l in LAUNCHES}
    for name in FUSED_ORDER:
        for fn in by_name[name][1]:
            fn(nc, S, B, aps)
            S.full_barrier()
    S.finalize()
    return nc


_FUSED_NC = None
_NCS = {}


def _get_nc(name):
    if name not in _NCS:
        for n, phases, ext_in, ext_out, weights in LAUNCHES:
            if n == name:
                _NCS[name] = (build_launch(phases, ext_in, ext_out, weights),
                              ext_in, ext_out, weights)
                break
    return _NCS[name]


def kernel(**inputs):
    global _FUSED_NC
    x = np.ascontiguousarray(np.asarray(inputs["L0_attn_x"]))
    weights_np = {k: np.ascontiguousarray(np.asarray(v))
                  for k, v in inputs.items() if k != "L0_attn_x"}
    if _FUSED_NC is None:
        _FUSED_NC = build_fused()
    in_maps = []
    for c in range(NCORES):
        im = dict(weights_np)
        im["x_in"] = np.ascontiguousarray(x[c * BPC:(c + 1) * BPC])
        in_maps.append(im)
    res = run_bass_kernel_spmd(_FUSED_NC, in_maps, core_ids=list(range(NCORES)))
    return np.concatenate([np.asarray(res.results[c]["out"])
                           for c in range(NCORES)], axis=0)
